# version 17
# baseline (speedup 1.0000x reference)
import contextlib
from collections import defaultdict

import numpy as np

import concourse.bass as bass
import concourse.mybir as mybir
from concourse.bass_utils import run_bass_kernel_spmd

F32 = mybir.dt.float32
BF16 = mybir.dt.bfloat16
AF = mybir.ActivationFunctionType
ALU = mybir.AluOpType
AX = mybir.AxisListType

T = 2048
D = 1024
C = 8
TC = 4
TW = 512
NEXP = 34
NI = 4
NH = 16
QT = 16

STRICT = ("dve", "act", "pool")
N_DMA_SLOTS = 6

ENG_SECT = {"sp": "sync", "pe": "tensor", "act": "scalar", "dve": "vector", "pool": "gpsimd"}


class _Op:
    __slots__ = ("oid", "engine", "fn", "deps", "needs_inc", "inc_val",
                 "is_dma", "slot_key", "slot_use", "tag", "chain_ok")


class Sched:
    def __init__(self):
        self.ops = []
        self.eng = defaultdict(list)
        self.res = {}
        self.dma_q_count = defaultdict(int)
        self.dma_slot_uses = defaultdict(int)

    def op(self, engine, fn, reads=(), writes=(), extra_deps=(), tag="", chain_ok=False):
        deps = set()
        for r in reads:
            w, _ = self.res.get(r, (None, []))
            if w is not None:
                deps.add(w)
        for wk in writes:
            w, rd = self.res.get(wk, (None, []))
            if w is not None:
                deps.add(w)
            deps.update(rd)
        deps.update(e for e in extra_deps if e is not None)
        o = _Op()
        o.oid = len(self.ops)
        o.engine = engine
        o.fn = fn
        o.deps = deps
        o.needs_inc = False
        o.inc_val = 0
        o.is_dma = False
        o.slot_key = None
        o.slot_use = 0
        o.tag = tag
        o.chain_ok = chain_ok
        self.ops.append(o)
        self.eng[engine].append(o)
        for r in reads:
            w, rd = self.res.get(r, (None, []))
            self.res[r] = (w, rd + [o.oid])
        for wk in writes:
            self.res[wk] = (o.oid, [])
        return o.oid

    def dma(self, queue, fn, reads=(), writes=(), tag=""):
        oid = self.op(queue, fn, reads, writes, tag=tag)
        o = self.ops[oid]
        o.is_dma = True
        slot = self.dma_q_count[queue] % N_DMA_SLOTS
        self.dma_q_count[queue] += 1
        o.slot_key = (queue, slot)
        self.dma_slot_uses[o.slot_key] += 1
        o.slot_use = self.dma_slot_uses[o.slot_key]
        return oid

    def barrier(self):
        lasts = set()
        for lst in self.eng.values():
            if lst:
                lasts.add(lst[-1].oid)
        dma_last = {}
        for o in self.ops:
            if o.is_dma:
                dma_last[o.slot_key] = o.oid
        deps = lasts | set(dma_last.values())
        for e in ("sp", "pe", "act", "dve", "pool"):
            self.op(e, lambda eng: eng.nop(), extra_deps=sorted(deps), tag="barrier")

    def materialize(self, nc, block, eng_sems, dma_sems):
        for o in self.ops:
            for d in o.deps:
                p = self.ops[d]
                if p.is_dma:
                    continue
                if p.engine == o.engine and (
                        p.engine not in STRICT or (p.chain_ok and o.chain_ok)):
                    continue
                p.needs_inc = True
        for name, lst in self.eng.items():
            c = 0
            for o in lst:
                if o.is_dma:
                    o.inc_val = 16 * o.slot_use
                elif o.needs_inc:
                    c += 1
                    o.inc_val = c

        sched = self

        for name in ("sp", "pe", "act", "dve", "pool"):
            def emitter(eng, name=name):
                maxw = {}

                def wait(sem, val):
                    if maxw.get(sem.name, -1) >= val:
                        return
                    eng.wait_ge(sem, val)
                    maxw[sem.name] = val

                for o in sched.eng[name]:
                    for d in sorted(o.deps):
                        p = sched.ops[d]
                        if p.is_dma:
                            wait(dma_sems[p.slot_key], p.inc_val)
                        else:
                            if p.engine == o.engine and (
                                    p.engine not in STRICT
                                    or (p.chain_ok and o.chain_ok)):
                                continue
                            if not p.needs_inc:
                                continue
                            wait(eng_sems[p.engine], p.inc_val)
                    if o.is_dma and o.slot_use > 1:
                        wait(dma_sems[o.slot_key], 16 * (o.slot_use - 1))
                    inst = o.fn(eng)
                    if o.is_dma:
                        inst.then_inc(dma_sems[o.slot_key], 16)
                    elif o.needs_inc:
                        inst.then_inc(eng_sems[o.engine], 1)

            getattr(block, ENG_SECT[name])(emitter)


def build_program(phases=("f1", "at", "f2"), dump=None):
    nc = bass.Bass("TRN2", target_bir_lowering=False)

    x_in = nc.dram_tensor("x", [D, T], F32, kind="ExternalInput").ap()
    qkvw = nc.dram_tensor("qkvw", [24, D, 128], F32, kind="ExternalInput").ap()
    outw = nc.dram_tensor("outw", [8, D, 128], F32, kind="ExternalInput").ap()
    gatew = nc.dram_tensor("gatew", [NEXP, D, 512], F32, kind="ExternalInput").ap()
    upw = nc.dram_tensor("upw", [NEXP, D, 512], F32, kind="ExternalInput").ap()
    downw = nc.dram_tensor("downw", [NEXP, 512, D], F32, kind="ExternalInput").ap()
    gw = nc.dram_tensor("gw", [D, 32], F32, kind="ExternalInput").ap()
    gbias = nc.dram_tensor("gbias", [1, 32], F32, kind="ExternalInput").ap()
    out_d = nc.dram_tensor("out", [D, T], F32, kind="ExternalOutput").ap()
    combT_d = {t: nc.dram_tensor(f"combT_d{t}", [32, T], BF16, kind="Internal").ap()
               for t in ("f1", "f2")}
    wbf_d = {m: nc.dram_tensor(f"wbf_{m}", [NEXP, 128, 4096], BF16, kind="Internal").ap()
             for m in ("wg", "wu", "wd")}
    dbg_d = None
    dbg_bf = None
    dump = _BUILD_DUMP[0]
    if dump is not None:
        dbg_d = nc.dram_tensor("dbg", [128, 4 * 16384], F32, kind="ExternalOutput").ap()
        dbg_bf = nc.dram_tensor("dbgb", [128, 16384], BF16, kind="ExternalOutput").ap()

    s = Sched()
    est = contextlib.ExitStack()

    with est:
        x8 = est.enter_context(nc.sbuf_tensor("x8", [128, C * T], F32))
        hln = est.enter_context(nc.sbuf_tensor("hln", [128, C * T], BF16))
        id_bf = est.enter_context(nc.sbuf_tensor("id_bf", [128, 128], BF16))
        id_f32 = est.enter_context(nc.sbuf_tensor("id_f32", [128, 128], F32))
        cmask = est.enter_context(nc.sbuf_tensor("cmask", [128, 128], F32))
        ones_col_bf = est.enter_context(nc.sbuf_tensor("ones_col_bf", [128, 1], BF16))
        ones_row_bf = est.enter_context(nc.sbuf_tensor("ones_row_bf", [1, 128], BF16))
        ones_row_f = est.enter_context(nc.sbuf_tensor("ones_row_f", [1, 128], F32))
        gwT = est.enter_context(nc.sbuf_tensor("gwT", [128, C * 32], F32))
        bias32 = est.enter_context(nc.sbuf_tensor("bias32", [128, 32], F32))
        eps1 = est.enter_context(nc.sbuf_tensor("eps1", [1, 1], F32))

        XR = [[f"x.{c}.{t}" for t in range(TC)] for c in range(C)]
        HR = [[f"h.{c}.{t}" for t in range(TC)] for c in range(C)]
        allx = [k for row in XR for k in row]

        def x_view(c, tc, wdt=TW):
            return x8[:, c * T + tc * TW: c * T + tc * TW + wdt]

        def h_view(c, tc, wdt=TW):
            return hln[:, c * T + tc * TW: c * T + tc * TW + wdt]

        s.dma("act", lambda e: e.dma_start(
            out=x8[:, :].rearrange("p (c t) -> p c t", c=C),
            in_=x_in.rearrange("(c p) t -> p c t", p=128)),
            writes=allx, tag="xload")
        s.dma("act", lambda e: e.dma_start(
            out=gwT[:, :].rearrange("p (c e) -> p c e", c=C),
            in_=gw.rearrange("(c p) e -> p c e", p=128)),
            writes=["gwT"], tag="gwload")
        s.dma("act", lambda e: e.dma_start(
            out=bias32[:, :],
            in_=bass.AP(gbias.tensor, 0, [[0, 128], [1, 32]])),
            writes=["bias32"], tag="gbias")

        s.op("pool", lambda e: e.memset(id_bf[:, :], 0.0), writes=["id_bf"])
        s.op("pool", lambda e: e.affine_select(
            out=id_bf[:, :], in_=id_bf[:, :], compare_op=ALU.not_equal,
            fill=1.0, base=0, pattern=[[-1, 128]], channel_multiplier=1),
            reads=["id_bf"], writes=["id_bf"])
        s.op("pool", lambda e: e.memset(id_f32[:, :], 0.0), writes=["id_f32"])
        s.op("pool", lambda e: e.affine_select(
            out=id_f32[:, :], in_=id_f32[:, :], compare_op=ALU.not_equal,
            fill=1.0, base=0, pattern=[[-1, 128]], channel_multiplier=1),
            reads=["id_f32"], writes=["id_f32"])
        s.op("pool", lambda e: e.memset(cmask[:, :], 0.0), writes=["cmask"])
        s.op("pool", lambda e: e.affine_select(
            out=cmask[:, :], in_=cmask[:, :], compare_op=ALU.is_ge,
            fill=-1e9, base=0, pattern=[[-1, 128]], channel_multiplier=1),
            reads=["cmask"], writes=["cmask"])
        s.op("pool", lambda e: e.memset(ones_col_bf[:, :], 1.0), writes=["ones_col"])
        s.op("pool", lambda e: e.memset(ones_row_bf[:, :], 1.0), writes=["ones_row"])
        s.op("pool", lambda e: e.memset(ones_row_f[:, :], 1.0), writes=["ones_row_f"])
        s.op("pool", lambda e: e.memset(eps1[:, :], 1e-5), writes=["eps1"])

        def layernorm(tag, gate, combine_sb=None, combT_sb=None):
            with contextlib.ExitStack() as sc:
                xb = sc.enter_context(nc.sbuf_tensor(f"xb{tag}", [128, 2 * T], BF16))
                sq = sc.enter_context(nc.sbuf_tensor(f"sq{tag}", [128, 2 * T], BF16))
                mrow = sc.enter_context(nc.sbuf_tensor(f"mrow{tag}", [1, T], F32))
                trow = sc.enter_context(nc.sbuf_tensor(f"trow{tag}", [1, T], F32))
                rrow = sc.enter_context(nc.sbuf_tensor(f"rrow{tag}", [1, T], F32))
                xl = sc.enter_context(nc.sbuf_tensor(f"xl{tag}", [128, 2 * TW], F32))
                if gate:
                    sem_ = sc.enter_context(nc.sbuf_tensor(f"sem{tag}", [32, T], F32))

                with contextlib.ExitStack() as pc:
                    px = pc.enter_context(nc.psum_tensor(f"px{tag}", [1, T], F32))
                    pq = pc.enter_context(nc.psum_tensor(f"pq{tag}", [1, T], F32))
                    for c in range(C):
                        sl = c % 2
                        s.op("pool", lambda e, c=c, sl=sl: e.tensor_copy(
                            out=xb[:, sl * T:(sl + 1) * T], in_=x8[:, c * T:(c + 1) * T]),
                            reads=XR[c], writes=[f"lnxb.{sl}"], tag=f"lncast{c}")
                        s.op("act", lambda e, sl=sl: e.activation(
                            out=sq[:, sl * T:(sl + 1) * T], in_=xb[:, sl * T:(sl + 1) * T],
                            func=AF.Square),
                            reads=[f"lnxb.{sl}"], writes=[f"lnsq.{sl}"])
                        for tc in range(TC):
                            s.op("pe", lambda e, c=c, sl=sl, tc=tc: e.matmul(
                                px[0:1, tc * TW:(tc + 1) * TW], ones_col_bf[:, :],
                                xb[:, sl * T + tc * TW: sl * T + (tc + 1) * TW],
                                start=(c == 0), stop=(c == C - 1)),
                                reads=[f"lnxb.{sl}", "ones_col"], writes=[f"lnpx.{tc}"])
                            s.op("pe", lambda e, c=c, sl=sl, tc=tc: e.matmul(
                                pq[0:1, tc * TW:(tc + 1) * TW], ones_col_bf[:, :],
                                sq[:, sl * T + tc * TW: sl * T + (tc + 1) * TW],
                                start=(c == 0), stop=(c == C - 1)),
                                reads=[f"lnsq.{sl}", "ones_col"], writes=[f"lnpq.{tc}"])
                    pxr = [f"lnpx.{t}" for t in range(TC)]
                    pqr = [f"lnpq.{t}" for t in range(TC)]
                    s.op("dve", lambda e: e.tensor_scalar_mul(
                        mrow[:, :], px[0:1, :], 1.0 / D), reads=pxr, writes=["mrow"])
                    s.op("dve", lambda e: e.tensor_mul(
                        trow[:, :], mrow[:, :], mrow[:, :]), reads=["mrow"], writes=["trow"])
                    s.op("dve", lambda e: e.scalar_tensor_tensor(
                        out=trow[:, :], in0=pq[0:1, :], scalar=1.0 / D, in1=trow[:, :],
                        op0=ALU.mult, op1=ALU.subtract),
                        reads=pqr + ["trow"], writes=["trow"])
                    s.op("act", lambda e: e.activation(
                        out=trow[:, :], in_=trow[:, :], func=AF.Ln, bias=eps1[:, :]),
                        reads=["trow", "eps1"], writes=["trow"])
                    s.op("act", lambda e: e.activation(
                        out=rrow[:, :], in_=trow[:, :], func=AF.Exp, scale=-0.5),
                        reads=["trow"], writes=["rrow"])
                s.barrier()

                with contextlib.ExitStack() as pc:
                    pmb = pc.enter_context(nc.psum_tensor(f"pmb{tag}", [128, TW], F32))
                    prb = pc.enter_context(nc.psum_tensor(f"prb{tag}", [128, TW], F32))
                    if gate:
                        pg = pc.enter_context(nc.psum_tensor(f"pg{tag}", [32, T], F32))
                    for tc in range(TC):
                        s.op("pe", lambda e, tc=tc: e.matmul(
                            pmb[:, :], ones_row_f[:, :], mrow[0:1, tc * TW:(tc + 1) * TW],
                            start=True, stop=True),
                            reads=["mrow", "ones_row_f"], writes=["pmb"])
                        s.op("pe", lambda e, tc=tc: e.matmul(
                            prb[:, :], ones_row_f[:, :], rrow[0:1, tc * TW:(tc + 1) * TW],
                            start=True, stop=True),
                            reads=["rrow", "ones_row_f"], writes=["prb"])
                        for c in range(C):
                            sl = c % 2
                            s.op("dve", lambda e, c=c, tc=tc, sl=sl: e.tensor_sub(
                                xl[:, sl * TW:(sl + 1) * TW], x_view(c, tc), pmb[:, :]),
                                reads=XR[c][tc:tc + 1] + ["pmb"], writes=[f"lnxl.{sl}"])
                            s.op("dve", lambda e, sl=sl: e.tensor_mul(
                                xl[:, sl * TW:(sl + 1) * TW],
                                xl[:, sl * TW:(sl + 1) * TW], prb[:, :]),
                                reads=[f"lnxl.{sl}", "prb"], writes=[f"lnxl.{sl}"])
                            s.op("act", lambda e, c=c, tc=tc, sl=sl: e.activation(
                                out=h_view(c, tc), in_=xl[:, sl * TW:(sl + 1) * TW],
                                func=AF.Copy),
                                reads=[f"lnxl.{sl}"], writes=[HR[c][tc]])
                            if gate:
                                s.op("pe", lambda e, c=c, tc=tc, sl=sl: e.matmul(
                                    pg[0:32, tc * TW:(tc + 1) * TW],
                                    gwT[:, c * 32:(c + 1) * 32],
                                    xl[:, sl * TW:(sl + 1) * TW],
                                    start=(c == 0), stop=(c == C - 1)),
                                    reads=[f"lnxl.{sl}", "gwT"], writes=[f"lnpg.{tc}"])
                    if gate:
                        for tc in range(TC):
                            s.op("act", lambda e, tc=tc: e.activation(
                                out=sem_[0:32, tc * TW:(tc + 1) * TW],
                                in_=pg[0:32, tc * TW:(tc + 1) * TW], func=AF.Sigmoid),
                                reads=[f"lnpg.{tc}"], writes=[f"sem.{tc}"])
                s.barrier()

                if not gate:
                    return

                with contextlib.ExitStack() as gc:
                    sc_all = gc.enter_context(nc.sbuf_tensor(f"scall{tag}", [128, 512], F32))
                    scb = gc.enter_context(nc.sbuf_tensor(f"scb{tag}", [128, 512], F32))
                    u1 = gc.enter_context(nc.sbuf_tensor(f"u1{tag}", [128, 128], F32))
                    u2 = gc.enter_context(nc.sbuf_tensor(f"u2{tag}", [128, 128], F32))
                    gsc = gc.enter_context(nc.sbuf_tensor(f"gsc{tag}", [128, 128], F32))
                    g8 = gc.enter_context(nc.sbuf_tensor(f"g8{tag}", [128, 8], F32))
                    gmask = gc.enter_context(nc.sbuf_tensor(f"gmask{tag}", [128, 128], F32))
                    smask = gc.enter_context(nc.sbuf_tensor(f"smask{tag}", [128, 512], F32))
                    masked = gc.enter_context(nc.sbuf_tensor(f"masked{tag}", [128, 512], F32))
                    m8 = gc.enter_context(nc.sbuf_tensor(f"m8{tag}", [128, 8], F32))
                    selw = gc.enter_context(nc.sbuf_tensor(f"selw{tag}", [128, 512], F32))
                    dnm = gc.enter_context(nc.sbuf_tensor(f"dnm{tag}", [128, 16], F32))
                    with contextlib.ExitStack() as pc:
                        pt0 = pc.enter_context(nc.psum_tensor(f"pt0{tag}", [128, 32], F32))
                        pt1 = pc.enter_context(nc.psum_tensor(f"pt1{tag}", [128, 32], F32))
                        ptc0 = pc.enter_context(nc.psum_tensor(f"ptc0{tag}", [32, 128], F32))
                        ptc1 = pc.enter_context(nc.psum_tensor(f"ptc1{tag}", [32, 128], F32))
                        for t16 in range(16):
                            pt = (pt0, pt1)[t16 % 2]
                            key = f"lnpt.{t16 % 2}"
                            s.op("pe", lambda e, t16=t16, pt=pt: e.transpose(
                                pt[:, :], sem_[0:32, t16 * 128:(t16 + 1) * 128],
                                id_f32[0:32, 0:32]),
                                reads=[f"sem.{t16 // 4}", "id_f32"], writes=[key])
                            s.op("act", lambda e, t16=t16, pt=pt: e.activation(
                                out=sc_all[:, t16 * 32:(t16 + 1) * 32], in_=pt[:, :],
                                func=AF.Copy),
                                reads=[key], writes=[f"scall.{t16}"])
                        scr = [f"scall.{t}" for t in range(16)]
                        for t16 in range(16):
                            s.op("dve", lambda e, t16=t16: e.tensor_add(
                                scb[:, t16 * 32:(t16 + 1) * 32],
                                sc_all[:, t16 * 32:(t16 + 1) * 32], bias32[:, :]),
                                reads=[f"scall.{t16}", "bias32"], writes=["scb"])

                        def v4(t, j):
                            return bass.AP(t, j, [[512, 128], [32, 16], [4, 8]])

                        pairs = [(0, 1), (0, 2), (0, 3), (1, 2), (1, 3), (2, 3)]
                        for pi, (a, b) in enumerate(pairs):
                            dst = u1 if pi == 0 else u2
                            dk = "u1" if pi == 0 else "u2"
                            s.op("dve", lambda e, a=a, b=b, dst=dst: e.tensor_add(
                                bass.AP(dst, 0, [[128, 128], [8, 16], [1, 8]]),
                                v4(scb, a), v4(scb, b)),
                                reads=["scb"], writes=[dk])
                            if pi == 0:
                                continue
                            tgt = u1 if pi < 5 else gsc
                            tk = "u1" if pi < 5 else "gsc"
                            s.op("dve", lambda e, tgt=tgt: e.tensor_max(
                                tgt[:, :], u1[:, :], u2[:, :]),
                                reads=["u1", "u2"], writes=[tk])
                        for t16 in range(16):
                            s.op("dve", lambda e, t16=t16: e.max(
                                out=g8[:, :], in_=gsc[:, t16 * 8:(t16 + 1) * 8]),
                                reads=["gsc"], writes=["g8"])
                            s.op("dve", lambda e, t16=t16: e.tensor_scalar(
                                out=gmask[:, t16 * 8:(t16 + 1) * 8],
                                in0=gsc[:, t16 * 8:(t16 + 1) * 8],
                                scalar1=g8[:, 3:4], scalar2=None, op0=ALU.is_ge),
                                reads=["gsc", "g8"], writes=["gmask"])
                        for j in range(4):
                            s.op("dve", lambda e, j=j: e.tensor_copy(
                                out=v4(smask, j),
                                in_=bass.AP(gmask, 0, [[128, 128], [8, 16], [1, 8]])),
                                reads=["gmask"], writes=["smask"])
                        s.op("dve", lambda e: e.tensor_mul(
                            masked[:, :], scb[:, :], smask[:, :]),
                            reads=["scb", "smask"], writes=["maskedg"])
                        for t16 in range(16):
                            s.op("dve", lambda e, t16=t16: e.max(
                                out=m8[:, :], in_=masked[:, t16 * 32:(t16 + 1) * 32]),
                                reads=["maskedg"], writes=["m8"])
                            s.op("dve", lambda e, t16=t16: e.tensor_scalar(
                                out=selw[:, t16 * 32:(t16 + 1) * 32],
                                in0=masked[:, t16 * 32:(t16 + 1) * 32],
                                scalar1=m8[:, 7:8], scalar2=None, op0=ALU.is_ge),
                                reads=["maskedg", "m8"], writes=["selw"])
                        s.op("dve", lambda e: e.tensor_mul(
                            selw[:, :], selw[:, :], sc_all[:, :]),
                            reads=["selw"] + scr, writes=["selw"])
                        for t16 in range(16):
                            s.op("dve", lambda e, t16=t16: e.reduce_sum(
                                out=dnm[:, t16:t16 + 1],
                                in_=selw[:, t16 * 32:(t16 + 1) * 32], axis=AX.X),
                                reads=["selw"], writes=["dnm"])
                        s.op("dve", lambda e: e.tensor_scalar_add(
                            dnm[:, :], dnm[:, :], 1e-20), reads=["dnm"], writes=["dnm"])
                        s.op("dve", lambda e: e.reciprocal(
                            dnm[:, :], dnm[:, :]), reads=["dnm"], writes=["dnm"])
                        for t16 in range(16):
                            s.op("dve", lambda e, t16=t16: e.tensor_scalar(
                                out=combine_sb[:, t16 * 32:(t16 + 1) * 32],
                                in0=selw[:, t16 * 32:(t16 + 1) * 32],
                                scalar1=dnm[:, t16:t16 + 1], scalar2=2.5,
                                op0=ALU.mult, op1=ALU.mult),
                                reads=["selw", "dnm"], writes=[f"comb.{t16}"])
                            ptc = (ptc0, ptc1)[t16 % 2]
                            pk = f"lnptc.{t16 % 2}"
                            s.op("pe", lambda e, t16=t16, ptc=ptc: e.transpose(
                                ptc[:, :], combine_sb[:, t16 * 32:(t16 + 1) * 32],
                                id_f32[:, :]),
                                reads=[f"comb.{t16}", "id_f32"], writes=[pk])
                            s.op("act", lambda e, t16=t16, ptc=ptc: e.activation(
                                out=combT_sb[0:32, t16 * 128:(t16 + 1) * 128],
                                in_=ptc[:, :], func=AF.Copy),
                                reads=[pk], writes=[f"combT.{t16 // 4}"])
                s.dma("sp", lambda e: e.dma_start(
                    out=combT_d[tag][:, :], in_=combT_sb[:, :]),
                    reads=[f"combT.{t}" for t in range(4)], writes=["combTd"],
                    tag="combTout")
                if _BUILD_DUMP[0] == "comb" and tag == "f1":
                    s.dma("sp", lambda e: e.dma_start(
                        out=dbg_d[:, 0:512], in_=combine_sb[:, :]),
                        reads=[f"comb.{t}" for t in range(16)], tag="dumpcomb")
                    s.dma("sp", lambda e: e.dma_start(
                        out=dbg_d[0:32, 512:512 + T], in_=sem_[:, :]),
                        reads=[f"sem.{t}" for t in range(4)], tag="dumpsem")
                s.barrier()

        def ffn_call(tag):
            with contextlib.ExitStack() as fc:
                combine_sb = fc.enter_context(nc.sbuf_tensor(f"combine{tag}", [128, 512], F32))
                combT_sb = fc.enter_context(nc.sbuf_tensor(f"combT{tag}", [32, T], BF16))

                layernorm(tag, gate=True, combine_sb=combine_sb, combT_sb=combT_sb)

                with contextlib.ExitStack() as ec:
                    wst = ec.enter_context(nc.sbuf_tensor(f"wst{tag}", [128, 2 * 2048], F32))
                    wg = ec.enter_context(nc.sbuf_tensor(f"wg{tag}", [128, 2 * 4096], BF16))
                    wu = ec.enter_context(nc.sbuf_tensor(f"wu{tag}", [128, 2 * 4096], BF16))
                    wd = ec.enter_context(nc.sbuf_tensor(f"wd{tag}", [128, 2 * 4096], BF16))
                    zb = ec.enter_context(nc.sbuf_tensor(f"zb{tag}", [128, 2 * 2048], BF16))
                    zt = ec.enter_context(nc.sbuf_tensor(f"zt{tag}", [128, 2 * TW], BF16))
                    gs = ec.enter_context(nc.sbuf_tensor(f"gs{tag}", [128, 2 * TW], F32))
                    cb = ec.enter_context(nc.sbuf_tensor(f"cb{tag}", [128, 2 * T], BF16))
                    with contextlib.ExitStack() as pc:
                        pgA = pc.enter_context(nc.psum_tensor(f"pgA{tag}", [128, TW], F32))
                        pgB = pc.enter_context(nc.psum_tensor(f"pgB{tag}", [128, TW], F32))
                        puA = pc.enter_context(nc.psum_tensor(f"puA{tag}", [128, TW], F32))
                        puB = pc.enter_context(nc.psum_tensor(f"puB{tag}", [128, TW], F32))
                        pd = [pc.enter_context(nc.psum_tensor(f"pd{j}{tag}", [128, TW], F32))
                              for j in range(4)]

                        nstage = [0]

                        def load_expert(e_):
                            sl = e_ % 2
                            if e_ < 32:
                                s.dma("act", lambda e, e_=e_, sl=sl: e.dma_start(
                                    out=cb[:, sl * T:(sl + 1) * T],
                                    in_=bass.AP(combT_d[tag].tensor, e_ * T,
                                                [[0, 128], [1, T]])),
                                    reads=["combTd"], writes=[f"cb.{sl}"],
                                    tag=f"cbdma{e_}")
                            if tag == "f2":
                                for dstbuf, dk in ((wg, "wg"), (wu, "wu"), (wd, "wd")):
                                    s.dma("sp", lambda e, dstbuf=dstbuf, dk=dk, e_=e_, sl=sl:
                                          e.dma_start(
                                        out=dstbuf[:, sl * 4096:(sl + 1) * 4096],
                                        in_=wbf_d[dk][e_]),
                                        writes=[f"{dk}.{sl}"], tag=f"wbfin{e_}")
                                return
                            for src, dstbuf, dk, scale, nch in (
                                    (gatew, wg, "wg", None, 8),
                                    (upw, wu, "wu", None, 8),
                                    (downw, wd, "wd", 0.5, 4)):
                                for half in range(2):
                                    ss = nstage[0] % 2
                                    nstage[0] += 1
                                    hc2 = nch // 2
                                    s.dma("sp", lambda e, src=src, e_=e_, half=half, ss=ss, hc2=hc2:
                                          e.dma_start(
                                        out=wst[:, ss * 2048:(ss + 1) * 2048].rearrange(
                                            "p (c i) -> p c i", c=hc2),
                                        in_=src[e_].rearrange(
                                            "(c p) i -> p c i", p=128)[:, half * hc2:(half + 1) * hc2, :]),
                                        writes=[f"wst.{ss}"], tag=f"wdma{e_}")
                                    cscale = 1.0 if scale is None else scale
                                    s.op("act", lambda e, dstbuf=dstbuf, sl=sl, half=half, ss=ss, cscale=cscale:
                                         e.activation(
                                        out=dstbuf[:, sl * 4096 + half * 2048:
                                                   sl * 4096 + (half + 1) * 2048],
                                        in_=wst[:, ss * 2048:(ss + 1) * 2048],
                                        func=AF.Copy, scale=cscale),
                                        reads=[f"wst.{ss}"], writes=[f"{dk}.{sl}"])
                                s.dma("act", lambda e, dstbuf=dstbuf, dk=dk, e_=e_, sl=sl:
                                      e.dma_start(
                                    out=wbf_d[dk][e_],
                                    in_=dstbuf[:, sl * 4096:(sl + 1) * 4096]),
                                    reads=[f"{dk}.{sl}"], tag=f"wbfout{e_}")

                        def emit_gu(e_, tc):
                            sl = e_ % 2
                            routed = e_ < 32
                            if True:
                                ztc = (4 * e_ + tc) % 2
                                for ic in range(NI):
                                    par = ic % 2
                                    pgp = (pgA, pgB)[par]
                                    pup = (puA, puB)[par]
                                    for kc in range(C):
                                        s.op("pe", lambda e, sl=sl, tc=tc, ic=ic, kc=kc, pgp=pgp:
                                             e.matmul(
                                            pgp[:, :],
                                            wg[:, sl * 4096 + kc * 512 + ic * 128:
                                               sl * 4096 + kc * 512 + (ic + 1) * 128],
                                            h_view(kc, tc),
                                            start=(kc == 0), stop=(kc == C - 1)),
                                            reads=[f"wg.{sl}", HR[kc][tc]],
                                            writes=[f"pg.{par}"])
                                    s.op("act", lambda e, par=par, pgp=pgp: e.activation(
                                        out=gs[:, par * TW:(par + 1) * TW],
                                        in_=pgp[:, :], func=AF.Silu),
                                        reads=[f"pg.{par}"], writes=[f"gs.{par}"])
                                    for kc in range(C):
                                        s.op("pe", lambda e, sl=sl, tc=tc, ic=ic, kc=kc, pup=pup:
                                             e.matmul(
                                            pup[:, :],
                                            wu[:, sl * 4096 + kc * 512 + ic * 128:
                                               sl * 4096 + kc * 512 + (ic + 1) * 128],
                                            h_view(kc, tc),
                                            start=(kc == 0), stop=(kc == C - 1)),
                                            reads=[f"wu.{sl}", HR[kc][tc]],
                                            writes=[f"pu.{par}"])
                                    if routed:
                                        s.op("dve", lambda e, par=par, pup=pup: e.tensor_mul(
                                            zt[:, par * TW:(par + 1) * TW],
                                            gs[:, par * TW:(par + 1) * TW], pup[:, :]),
                                            reads=[f"gs.{par}", f"pu.{par}"],
                                            writes=[f"zt.{par}"])
                                        s.op("dve", lambda e, par=par, ztc=ztc, ic=ic, tc=tc, sl=sl:
                                             e.tensor_mul(
                                            zb[:, ztc * 2048 + ic * TW:
                                               ztc * 2048 + (ic + 1) * TW],
                                            zt[:, par * TW:(par + 1) * TW],
                                            cb[:, sl * T + tc * TW: sl * T + (tc + 1) * TW]),
                                            reads=[f"zt.{par}", f"cb.{sl}"],
                                            writes=[f"z.{ztc}.{ic}"])
                                    else:
                                        s.op("dve", lambda e, par=par, ztc=ztc, ic=ic, pup=pup:
                                             e.tensor_mul(
                                            zb[:, ztc * 2048 + ic * TW:
                                               ztc * 2048 + (ic + 1) * TW],
                                            gs[:, par * TW:(par + 1) * TW], pup[:, :]),
                                            reads=[f"gs.{par}", f"pu.{par}"],
                                            writes=[f"z.{ztc}.{ic}"])

                        def emit_down_mms(e_, tc, hcs):
                            sl = e_ % 2
                            ztc = (4 * e_ + tc) % 2
                            for hc in hcs:
                                b = hc % 4
                                for ic in range(NI):
                                    s.op("pe", lambda e, sl=sl, hc=hc, ic=ic, ztc=ztc, b=b:
                                         e.matmul(
                                        pd[b][:, :],
                                        wd[:, sl * 4096 + ic * 1024 + hc * 128:
                                           sl * 4096 + ic * 1024 + (hc + 1) * 128],
                                        zb[:, ztc * 2048 + ic * TW:
                                           ztc * 2048 + (ic + 1) * TW],
                                        start=(ic == 0), stop=(ic == NI - 1)),
                                        reads=[f"wd.{sl}", f"z.{ztc}.{ic}"],
                                        writes=[f"pd.{b}"])

                        def emit_down_evac(e_, tc, hcs):
                            for hc in hcs:
                                b = hc % 4
                                s.op("dve", lambda e, hc=hc, tc=tc, b=b: e.tensor_add(
                                    x_view(hc, tc), pd[b][:, :], x_view(hc, tc)),
                                    reads=[f"pd.{b}", XR[hc][tc]],
                                    writes=[XR[hc][tc]])

                        load_expert(0)
                        load_expert(1)
                        seq = [(e_, tc) for e_ in range(NEXP) for tc in range(TC)]
                        LAG = 2

                        def emit_down(j):
                            d_e, d_tc = seq[j]
                            for hc in range(C):
                                emit_down_mms(d_e, d_tc, [hc])
                                emit_down_evac(d_e, d_tc, [hc])
                            if d_tc == TC - 1 and d_e + 2 < NEXP:
                                load_expert(d_e + 2)

                        for i, (e_, tc) in enumerate(seq):
                            if i >= LAG:
                                emit_down(i - LAG)
                            emit_gu(e_, tc)
                        for j in range(len(seq) - LAG, len(seq)):
                            emit_down(j)
            s.barrier()

        def attn_call(tag):
            layernorm(tag, gate=False)
            with contextlib.ExitStack() as ac:
                ao = ac.enter_context(nc.sbuf_tensor(f"ao{tag}", [128, C * T], BF16))
                qkvb = ac.enter_context(nc.sbuf_tensor(f"qkvb{tag}", [128, 3 * T], BF16))
                wqst = ac.enter_context(nc.sbuf_tensor(f"wqst{tag}", [128, 2 * 1024], F32))
                wqb = ac.enter_context(nc.sbuf_tensor(f"wqb{tag}", [128, 2 * 3 * 1024], BF16))
                vt = ac.enter_context(nc.sbuf_tensor(f"vt{tag}", [128, 16 * 128], BF16))
                pP = ac.enter_context(nc.sbuf_tensor(f"pP{tag}", [128, 2 * T], BF16))
                ptsb = ac.enter_context(nc.sbuf_tensor(f"ptsb{tag}", [128, 2 * 128], BF16))
                mx4 = ac.enter_context(nc.sbuf_tensor(f"mx4{tag}", [128, 8], F32))
                mbias = ac.enter_context(nc.sbuf_tensor(f"mbias{tag}", [128, 2], F32))
                rs4 = ac.enter_context(nc.sbuf_tensor(f"rs4{tag}", [128, 8], F32))
                rsum = ac.enter_context(nc.sbuf_tensor(f"rsum{tag}", [128, 2], F32))

                with contextlib.ExitStack() as pc:
                    pS = pc.enter_context(nc.psum_tensor(f"pS{tag}", [128, T], F32))
                    pPT = [pc.enter_context(nc.psum_tensor(f"pPT{j}{tag}", [128, 128], BF16))
                           for j in range(2)]
                    pO = [pc.enter_context(nc.psum_tensor(f"pO{j}{tag}", [128, 128], F32))
                          for j in range(2)]

                    nqst = [0]

                    def load_qkvw(c):
                        sl = c % 2
                        for m in range(3):
                            ss = nqst[0] % 2
                            nqst[0] += 1
                            oc = m * 8 + c
                            s.dma("sp", lambda e, oc=oc, ss=ss: e.dma_start(
                                out=wqst[:, ss * 1024:(ss + 1) * 1024].rearrange(
                                    "p (c m) -> p c m", c=8),
                                in_=qkvw[oc].rearrange("(c p) m -> p c m", p=128)),
                                writes=[f"wqst.{ss}"], tag=f"qkvw{oc}")
                            s.op("pool", lambda e, m=m, sl=sl, ss=ss: e.tensor_copy(
                                out=wqb[:, sl * 3072 + m * 1024:sl * 3072 + (m + 1) * 1024],
                                in_=wqst[:, ss * 1024:(ss + 1) * 1024]),
                                reads=[f"wqst.{ss}"], writes=[f"wqb.{sl}.{m}"])

                    def chunk(c):
                        sl = c % 2
                        for m in range(3):
                            for tc in range(TC):
                                for kc in range(C):
                                    s.op("pe", lambda e, m=m, tc=tc, kc=kc, sl=sl: e.matmul(
                                        pS[:, tc * TW:(tc + 1) * TW],
                                        wqb[:, sl * 3072 + m * 1024 + kc * 128:
                                            sl * 3072 + m * 1024 + (kc + 1) * 128],
                                        h_view(kc, tc),
                                        start=(kc == 0), stop=(kc == C - 1)),
                                        reads=[f"wqb.{sl}.{m}", HR[kc][tc]],
                                        writes=[f"pS.{tc}"])
                                s.op("act", lambda e, m=m, tc=tc: e.activation(
                                    out=qkvb[:, m * T + tc * TW: m * T + (tc + 1) * TW],
                                    in_=pS[:, tc * TW:(tc + 1) * TW], func=AF.Copy),
                                    reads=[f"pS.{tc}"], writes=[f"qkv.{m}.{tc}"])
                        for kc in range(16):
                            p3 = pPT[kc % 2]
                            k3 = f"pPT.{kc % 2}"
                            s.op("pe", lambda e, kc=kc, p3=p3: e.transpose(
                                p3[:, :], qkvb[:, 2 * T + kc * 128:2 * T + (kc + 1) * 128],
                                id_bf[:, :]),
                                reads=[f"qkv.2.{kc // 4}", "id_bf"], writes=[k3])
                            s.op("dve", lambda e, kc=kc, p3=p3: e.tensor_copy(
                                out=vt[:, kc * 128:(kc + 1) * 128], in_=p3[:, :]),
                                reads=[k3], writes=[f"vt.{kc}"])

                        def qk_pre(par, qt):
                            qoff = 64 * par
                            klen = (qt + 1) * 128
                            nk = (klen + TW - 1) // TW
                            jd = qt // 4
                            for j in range(nk):
                                w = min(TW, klen - j * TW)
                                s.op("pe", lambda e, qt=qt, j=j, w=w, qoff=qoff:
                                     e.matmul(
                                    pS[:, j * TW: j * TW + w],
                                    qkvb[qoff:qoff + 64, qt * 128:(qt + 1) * 128],
                                    qkvb[qoff:qoff + 64, T + j * TW: T + j * TW + w],
                                    start=True, stop=True),
                                    reads=[f"qkv.0.{qt // 4}", f"qkv.1.{j}"],
                                    writes=[f"pS.{j}"])
                                if j == jd:
                                    s.op("dve", lambda e, qt=qt: e.tensor_add(
                                        pS[:, qt * 128:(qt + 1) * 128],
                                        pS[:, qt * 128:(qt + 1) * 128], cmask[:, :]),
                                        reads=[f"pS.{jd}", "cmask"], writes=[f"pS.{jd}"])
                                s.op("dve", lambda e, j=j, w=w, par=par: e.reduce_max(
                                    out=mx4[:, par * 4 + j:par * 4 + j + 1],
                                    in_=pS[:, j * TW: j * TW + w], axis=AX.X),
                                    reads=[f"pS.{j}"], writes=[f"mx4.{par}"])
                            s.op("dve", lambda e, nk=nk, par=par: e.reduce_max(
                                out=mbias[:, par:par + 1], in_=mx4[:, par * 4:par * 4 + nk],
                                axis=AX.X),
                                reads=[f"mx4.{par}"], writes=[f"mbias.{par}"])
                            s.op("dve", lambda e, par=par: e.tensor_scalar_mul(
                                mbias[:, par:par + 1], mbias[:, par:par + 1], -0.125),
                                reads=[f"mbias.{par}"], writes=[f"mbias.{par}"])
                            for j in range(nk):
                                w = min(TW, klen - j * TW)
                                s.op("act", lambda e, j=j, w=w, par=par: e.activation(
                                    out=pP[:, par * T + j * TW: par * T + j * TW + w],
                                    in_=pS[:, j * TW: j * TW + w], func=AF.Exp,
                                    scale=0.125, bias=mbias[:, par:par + 1],
                                    accum_out=rs4[:, par * 4 + j:par * 4 + j + 1]),
                                    reads=[f"pS.{j}", f"mbias.{par}"],
                                    writes=[f"P.{par}.{j}", f"rs4.{par}.{j}"])

                        def smax_post(par, qt):
                            klen = (qt + 1) * 128
                            nk = (klen + TW - 1) // TW
                            s.op("dve", lambda e, nk=nk, par=par: e.reduce_sum(
                                out=rsum[:, par:par + 1], in_=rs4[:, par * 4:par * 4 + nk],
                                axis=AX.X),
                                reads=[f"rs4.{par}.{j}" for j in range(nk)],
                                writes=[f"rsum.{par}"])
                            s.op("dve", lambda e, par=par: e.reciprocal(
                                rsum[:, par:par + 1], rsum[:, par:par + 1]),
                                reads=[f"rsum.{par}"], writes=[f"rsum.{par}"])
                            pkeys = [f"P.{par}.{j}" for j in range(nk)]
                            s.op("dve", lambda e, klen=klen, par=par: e.tensor_scalar_mul(
                                pP[:, par * T:par * T + klen],
                                pP[:, par * T:par * T + klen], rsum[:, par:par + 1]),
                                reads=[f"rsum.{par}"] + pkeys, writes=pkeys)

                        HEADN = 3

                        def tpv(par, qt, c, part="all"):
                            qoff = 64 * par
                            tp = (0, 64) if par else None

                            def tr(kc):
                                pp = pPT[kc % 2]
                                ppk = f"pPT.{kc % 2}"
                                s.op("pe", lambda e, kc=kc, pp=pp, par=par: e.transpose(
                                    pp[:, :], pP[:, par * T + kc * 128:
                                                  par * T + (kc + 1) * 128], id_bf[:, :]),
                                    reads=[f"P.{par}.{kc // 4}", "id_bf"], writes=[ppk])
                                s.op("dve", lambda e, kc=kc, pp=pp: e.tensor_copy(
                                    out=ptsb[:, (kc % 2) * 128:(kc % 2 + 1) * 128],
                                    in_=pp[:, :]),
                                    reads=[ppk], writes=[f"ptsb.{kc % 2}"])

                            def pv(kc):
                                s.op("pe", lambda e, kc=kc, qt=qt, qoff=qoff, tp=tp, par=par:
                                     e.matmul(
                                    pO[par][qoff:qoff + 64, 0:128],
                                    vt[:, kc * 128 + qoff: kc * 128 + qoff + 64],
                                    ptsb[:, (kc % 2) * 128:(kc % 2 + 1) * 128],
                                    start=(kc == 0), stop=(kc == qt),
                                    tile_position=tp),
                                    reads=[f"vt.{kc}", f"ptsb.{kc % 2}"],
                                    writes=[f"pO.{par}"])

                            if part in ("head", "all"):
                                hi = min(HEADN, qt + 1)
                                tr(0)
                                for kc in range(1, hi):
                                    tr(kc)
                                    pv(kc - 1)
                                if part == "head" and hi <= qt:
                                    pass
                                else:
                                    pass
                            if part in ("tail", "all"):
                                hi = min(HEADN, qt + 1)
                                for kc in range(hi, qt + 1):
                                    tr(kc)
                                    pv(kc - 1)
                                pv(qt)
                                s.op("act", lambda e, c=c, qt=qt, qoff=qoff, par=par:
                                     e.activation(
                                    out=ao[qoff:qoff + 64, c * T + qt * 128:
                                           c * T + (qt + 1) * 128],
                                    in_=pO[par][qoff:qoff + 64, 0:128], func=AF.Copy),
                                    reads=[f"pO.{par}"], writes=[f"ao.{c}.{qt // 4}"])

                        for qt in range(QT):
                            if qt >= 1:
                                tpv(1, qt - 1, c, "head")
                            qk_pre(0, qt)
                            if qt >= 1:
                                tpv(1, qt - 1, c, "tail")
                            smax_post(0, qt)
                            tpv(0, qt, c, "head")
                            qk_pre(1, qt)
                            tpv(0, qt, c, "tail")
                            smax_post(1, qt)
                        tpv(1, QT - 1, c)

                    load_qkvw(0)
                    load_qkvw(1)
                    for c in range(C):
                        chunk(c)
                        if c + 2 < C:
                            load_qkvw(c + 2)

                if _BUILD_DUMP[0] == "ao":
                    s.dma("sp", lambda e: e.dma_start(
                        out=dbg_bf[:, 0:C * T], in_=ao[:, :]),
                        reads=[f"ao.{c}.{t}" for c in range(C) for t in range(TC)],
                        tag="dumpao")
                s.barrier()
                with contextlib.ExitStack() as pc:
                    pop = [pc.enter_context(nc.psum_tensor(f"pop{j}{tag}", [128, TW], F32))
                           for j in range(2)]
                    now = [0]
                    for hc in range(C):
                        ss = now[0] % 2
                        now[0] += 1
                        s.dma("sp", lambda e, hc=hc, ss=ss: e.dma_start(
                            out=wqst[:, ss * 1024:(ss + 1) * 1024].rearrange(
                                "p (c m) -> p c m", c=8),
                            in_=outw[hc].rearrange("(c p) m -> p c m", p=128)),
                            writes=[f"wqst.{ss}"], tag=f"outw{hc}")
                        s.op("pool", lambda e, hc=hc, ss=ss: e.tensor_copy(
                            out=wqb[:, (hc % 2) * 3072:(hc % 2) * 3072 + 1024],
                            in_=wqst[:, ss * 1024:(ss + 1) * 1024]),
                            reads=[f"wqst.{ss}"], writes=[f"wqb.{hc % 2}.0"])
                        for tc in range(TC):
                            pb = pop[tc % 2]
                            pbk = f"pop.{tc % 2}"
                            for kc in range(C):
                                s.op("pe", lambda e, hc=hc, tc=tc, kc=kc, pb=pb: e.matmul(
                                    pb[:, :],
                                    wqb[:, (hc % 2) * 3072 + kc * 128:
                                        (hc % 2) * 3072 + (kc + 1) * 128],
                                    ao[:, kc * T + tc * TW: kc * T + (tc + 1) * TW],
                                    start=(kc == 0), stop=(kc == C - 1)),
                                    reads=[f"wqb.{hc % 2}.0", f"ao.{kc}.{tc}"],
                                    writes=[pbk])
                            s.op("dve", lambda e, hc=hc, tc=tc, pb=pb: e.tensor_add(
                                x_view(hc, tc), pb[:, :], x_view(hc, tc)),
                                reads=[pbk, XR[hc][tc]], writes=[XR[hc][tc]])
            s.barrier()

        def dump_sbuf(t, nbytes_elems, rkeys, name):
            s.dma("sp", lambda e: e.dma_start(
                out=dbg_d[:, 0:nbytes_elems], in_=t),
                reads=rkeys, tag=f"dump{name}")
            s.barrier()

        calls = {"f1": lambda: ffn_call("f1"), "at": lambda: attn_call("at"),
                 "f2": lambda: ffn_call("f2")}
        for ph in phases:
            calls[ph]()

        s.dma("sp", lambda e: e.dma_start(
            out=out_d.rearrange("(c p) t -> p c t", p=128),
            in_=x8[:, :].rearrange("p (c t) -> p c t", c=C)),
            reads=allx, tag="store")
        s.barrier()

        with contextlib.ExitStack() as ms:
            eng_sems = {n: ms.enter_context(nc.semaphore(f"sem_{n}"))
                        for n in ("sp", "pe", "act", "dve", "pool")}
            dma_sems = {}
            for q in ("sp", "act"):
                for sl in range(N_DMA_SLOTS):
                    dma_sems[(q, sl)] = ms.enter_context(nc.semaphore(f"dsem_{q}{sl}"))
            with nc.Block() as block:
                s.materialize(nc, block, eng_sems, dma_sems)

    return nc


_NC_CACHE = {}
_BUILD_PHASES = [("f1", "at", "f2")]
_BUILD_DUMP = [None]
_LAST_RESULTS = [None]


def kernel(L0_attn_x, L0_attn_w_qkv, L0_attn_w_out, L0_ffn_gate_w, L0_ffn_gate_bias,
           L0_ffn_w_gate, L0_ffn_w_up, L0_ffn_w_down, L0_ffn_sh_gate, L0_ffn_sh_up,
           L0_ffn_sh_down):
    x = np.asarray(L0_attn_x, dtype=np.float32)
    w_qkv = np.asarray(L0_attn_w_qkv, dtype=np.float32)
    w_out = np.asarray(L0_attn_w_out, dtype=np.float32)
    gate_w = np.asarray(L0_ffn_gate_w, dtype=np.float32)
    gate_bias = np.asarray(L0_ffn_gate_bias, dtype=np.float32)
    w_gate = np.asarray(L0_ffn_w_gate, dtype=np.float32)
    w_up = np.asarray(L0_ffn_w_up, dtype=np.float32)
    w_down = np.asarray(L0_ffn_w_down, dtype=np.float32)
    sh_gate = np.asarray(L0_ffn_sh_gate, dtype=np.float32)
    sh_up = np.asarray(L0_ffn_sh_up, dtype=np.float32)
    sh_down = np.asarray(L0_ffn_sh_down, dtype=np.float32)

    qkvw = np.ascontiguousarray(
        w_qkv.T.reshape(D, 24, 128).transpose(1, 0, 2))
    outw = np.ascontiguousarray(
        w_out.T.reshape(D, 8, 128).transpose(1, 0, 2))
    gT = np.ascontiguousarray(sh_gate.T)
    uT = np.ascontiguousarray(sh_up.T)
    gatew = np.ascontiguousarray(np.concatenate(
        [w_gate.transpose(0, 2, 1), gT[None, :, 0:512], gT[None, :, 512:1024]], axis=0))
    upw = np.ascontiguousarray(np.concatenate(
        [w_up.transpose(0, 2, 1), uT[None, :, 0:512], uT[None, :, 512:1024]], axis=0))
    dT = sh_down.T
    downw = np.ascontiguousarray(np.concatenate(
        [w_down.transpose(0, 2, 1), dT[None, 0:512, :], dT[None, 512:1024, :]], axis=0))
    gw = np.ascontiguousarray(gate_w.T)
    gb = np.ascontiguousarray(gate_bias.reshape(1, 32))

    ph = (tuple(_BUILD_PHASES[0]), _BUILD_DUMP[0])
    if ph not in _NC_CACHE:
        _NC_CACHE[ph] = build_program(phases=ph[0])
    nc = _NC_CACHE[ph]

    in_maps = []
    for b in range(8):
        in_maps.append({
            "x": np.ascontiguousarray(x[b].T),
            "qkvw": qkvw, "outw": outw, "gatew": gatew, "upw": upw,
            "downw": downw, "gw": gw, "gbias": gb,
        })
    res = run_bass_kernel_spmd(nc, in_maps, core_ids=list(range(8)))
    _LAST_RESULTS[0] = res.results
    out = np.stack([np.asarray(r["out"]).T for r in res.results], axis=0)
    return np.ascontiguousarray(out.astype(np.float32))


# version 18
# speedup vs baseline: 1.0262x; 1.0262x over previous
import contextlib
from collections import defaultdict

import numpy as np

import concourse.bass as bass
import concourse.mybir as mybir
from concourse.bass_utils import run_bass_kernel_spmd

F32 = mybir.dt.float32
BF16 = mybir.dt.bfloat16
AF = mybir.ActivationFunctionType
ALU = mybir.AluOpType
AX = mybir.AxisListType

T = 2048
D = 1024
C = 8
TC = 4
TW = 512
NEXP = 34
NI = 4
NH = 16
QT = 16

STRICT = ("dve", "act", "pool")
N_DMA_SLOTS = 6

ENG_SECT = {"sp": "sync", "pe": "tensor", "act": "scalar", "dve": "vector", "pool": "gpsimd"}


class _Op:
    __slots__ = ("oid", "engine", "fn", "deps", "needs_inc", "inc_val",
                 "is_dma", "slot_key", "slot_use", "tag", "chain_ok")


class Sched:
    def __init__(self):
        self.ops = []
        self.eng = defaultdict(list)
        self.res = {}
        self.dma_q_count = defaultdict(int)
        self.dma_slot_uses = defaultdict(int)

    def op(self, engine, fn, reads=(), writes=(), extra_deps=(), tag="", chain_ok=False):
        deps = set()
        for r in reads:
            w, _ = self.res.get(r, (None, []))
            if w is not None:
                deps.add(w)
        for wk in writes:
            w, rd = self.res.get(wk, (None, []))
            if w is not None:
                deps.add(w)
            deps.update(rd)
        deps.update(e for e in extra_deps if e is not None)
        o = _Op()
        o.oid = len(self.ops)
        o.engine = engine
        o.fn = fn
        o.deps = deps
        o.needs_inc = False
        o.inc_val = 0
        o.is_dma = False
        o.slot_key = None
        o.slot_use = 0
        o.tag = tag
        o.chain_ok = chain_ok
        self.ops.append(o)
        self.eng[engine].append(o)
        for r in reads:
            w, rd = self.res.get(r, (None, []))
            self.res[r] = (w, rd + [o.oid])
        for wk in writes:
            self.res[wk] = (o.oid, [])
        return o.oid

    def dma(self, queue, fn, reads=(), writes=(), tag=""):
        oid = self.op(queue, fn, reads, writes, tag=tag)
        o = self.ops[oid]
        o.is_dma = True
        slot = self.dma_q_count[queue] % N_DMA_SLOTS
        self.dma_q_count[queue] += 1
        o.slot_key = (queue, slot)
        self.dma_slot_uses[o.slot_key] += 1
        o.slot_use = self.dma_slot_uses[o.slot_key]
        return oid

    def barrier(self):
        lasts = set()
        for lst in self.eng.values():
            if lst:
                lasts.add(lst[-1].oid)
        dma_last = {}
        for o in self.ops:
            if o.is_dma:
                dma_last[o.slot_key] = o.oid
        deps = lasts | set(dma_last.values())
        for e in ("sp", "pe", "act", "dve", "pool"):
            self.op(e, lambda eng: eng.nop(), extra_deps=sorted(deps), tag="barrier")

    def materialize(self, nc, block, eng_sems, dma_sems):
        for o in self.ops:
            for d in o.deps:
                p = self.ops[d]
                if p.is_dma:
                    continue
                if p.engine == o.engine and (
                        p.engine not in STRICT or (p.chain_ok and o.chain_ok)):
                    continue
                p.needs_inc = True
        for name, lst in self.eng.items():
            c = 0
            for o in lst:
                if o.is_dma:
                    o.inc_val = 16 * o.slot_use
                elif o.needs_inc:
                    c += 1
                    o.inc_val = c

        sched = self

        for name in ("sp", "pe", "act", "dve", "pool"):
            def emitter(eng, name=name):
                maxw = {}

                def wait(sem, val):
                    if maxw.get(sem.name, -1) >= val:
                        return
                    eng.wait_ge(sem, val)
                    maxw[sem.name] = val

                for o in sched.eng[name]:
                    for d in sorted(o.deps):
                        p = sched.ops[d]
                        if p.is_dma:
                            wait(dma_sems[p.slot_key], p.inc_val)
                        else:
                            if p.engine == o.engine and (
                                    p.engine not in STRICT
                                    or (p.chain_ok and o.chain_ok)):
                                continue
                            if not p.needs_inc:
                                continue
                            wait(eng_sems[p.engine], p.inc_val)
                    if o.is_dma and o.slot_use > 1:
                        wait(dma_sems[o.slot_key], 16 * (o.slot_use - 1))
                    inst = o.fn(eng)
                    if o.is_dma:
                        inst.then_inc(dma_sems[o.slot_key], 16)
                    elif o.needs_inc:
                        inst.then_inc(eng_sems[o.engine], 1)

            getattr(block, ENG_SECT[name])(emitter)


def build_program(phases=("f1", "at", "f2"), dump=None):
    nc = bass.Bass("TRN2", target_bir_lowering=False)

    x_in = nc.dram_tensor("x", [D, T], F32, kind="ExternalInput").ap()
    qkvw = nc.dram_tensor("qkvw", [24, D, 128], F32, kind="ExternalInput").ap()
    outw = nc.dram_tensor("outw", [8, D, 128], F32, kind="ExternalInput").ap()
    gatew = nc.dram_tensor("gatew", [NEXP, D, 512], F32, kind="ExternalInput").ap()
    upw = nc.dram_tensor("upw", [NEXP, D, 512], F32, kind="ExternalInput").ap()
    downw = nc.dram_tensor("downw", [NEXP, 512, D], F32, kind="ExternalInput").ap()
    gw = nc.dram_tensor("gw", [D, 32], F32, kind="ExternalInput").ap()
    gbias = nc.dram_tensor("gbias", [1, 32], F32, kind="ExternalInput").ap()
    out_d = nc.dram_tensor("out", [D, T], F32, kind="ExternalOutput").ap()
    combT_d = {t: nc.dram_tensor(f"combT_d{t}", [32, T], BF16, kind="Internal").ap()
               for t in ("f1", "f2")}
    wbf_d = {m: nc.dram_tensor(f"wbf_{m}", [NEXP, 128, 4096], BF16, kind="Internal").ap()
             for m in ("wg", "wu", "wd")}
    dbg_d = None
    dbg_bf = None
    dump = _BUILD_DUMP[0]
    if dump is not None:
        dbg_d = nc.dram_tensor("dbg", [128, 4 * 16384], F32, kind="ExternalOutput").ap()
        dbg_bf = nc.dram_tensor("dbgb", [128, 16384], BF16, kind="ExternalOutput").ap()

    s = Sched()
    est = contextlib.ExitStack()

    with est:
        x8 = est.enter_context(nc.sbuf_tensor("x8", [128, C * T], F32))
        hln = est.enter_context(nc.sbuf_tensor("hln", [128, C * T], BF16))
        id_bf = est.enter_context(nc.sbuf_tensor("id_bf", [128, 128], BF16))
        id_f32 = est.enter_context(nc.sbuf_tensor("id_f32", [128, 128], F32))
        cmask = est.enter_context(nc.sbuf_tensor("cmask", [128, 128], F32))
        ones_col_bf = est.enter_context(nc.sbuf_tensor("ones_col_bf", [128, 1], BF16))
        ones_row_bf = est.enter_context(nc.sbuf_tensor("ones_row_bf", [1, 128], BF16))
        ones_row_f = est.enter_context(nc.sbuf_tensor("ones_row_f", [1, 128], F32))
        gwT = est.enter_context(nc.sbuf_tensor("gwT", [128, C * 32], F32))
        bias32 = est.enter_context(nc.sbuf_tensor("bias32", [128, 32], F32))
        eps1 = est.enter_context(nc.sbuf_tensor("eps1", [1, 1], F32))

        XR = [[f"x.{c}.{t}" for t in range(TC)] for c in range(C)]
        HR = [[f"h.{c}.{t}" for t in range(TC)] for c in range(C)]
        allx = [k for row in XR for k in row]

        def x_view(c, tc, wdt=TW):
            return x8[:, c * T + tc * TW: c * T + tc * TW + wdt]

        def h_view(c, tc, wdt=TW):
            return hln[:, c * T + tc * TW: c * T + tc * TW + wdt]

        s.dma("act", lambda e: e.dma_start(
            out=x8[:, :].rearrange("p (c t) -> p c t", c=C),
            in_=x_in.rearrange("(c p) t -> p c t", p=128)),
            writes=allx, tag="xload")
        s.dma("act", lambda e: e.dma_start(
            out=gwT[:, :].rearrange("p (c e) -> p c e", c=C),
            in_=gw.rearrange("(c p) e -> p c e", p=128)),
            writes=["gwT"], tag="gwload")
        s.dma("act", lambda e: e.dma_start(
            out=bias32[:, :],
            in_=bass.AP(gbias.tensor, 0, [[0, 128], [1, 32]])),
            writes=["bias32"], tag="gbias")

        s.op("pool", lambda e: e.memset(id_bf[:, :], 0.0), writes=["id_bf"])
        s.op("pool", lambda e: e.affine_select(
            out=id_bf[:, :], in_=id_bf[:, :], compare_op=ALU.not_equal,
            fill=1.0, base=0, pattern=[[-1, 128]], channel_multiplier=1),
            reads=["id_bf"], writes=["id_bf"])
        s.op("pool", lambda e: e.memset(id_f32[:, :], 0.0), writes=["id_f32"])
        s.op("pool", lambda e: e.affine_select(
            out=id_f32[:, :], in_=id_f32[:, :], compare_op=ALU.not_equal,
            fill=1.0, base=0, pattern=[[-1, 128]], channel_multiplier=1),
            reads=["id_f32"], writes=["id_f32"])
        s.op("pool", lambda e: e.memset(cmask[:, :], 0.0), writes=["cmask"])
        s.op("pool", lambda e: e.affine_select(
            out=cmask[:, :], in_=cmask[:, :], compare_op=ALU.is_ge,
            fill=-1e9, base=0, pattern=[[-1, 128]], channel_multiplier=1),
            reads=["cmask"], writes=["cmask"])
        s.op("pool", lambda e: e.memset(ones_col_bf[:, :], 1.0), writes=["ones_col"])
        s.op("pool", lambda e: e.memset(ones_row_bf[:, :], 1.0), writes=["ones_row"])
        s.op("pool", lambda e: e.memset(ones_row_f[:, :], 1.0), writes=["ones_row_f"])
        s.op("pool", lambda e: e.memset(eps1[:, :], 1e-5), writes=["eps1"])

        def layernorm(tag, gate, combine_sb=None, combT_sb=None):
            with contextlib.ExitStack() as sc:
                xb = sc.enter_context(nc.sbuf_tensor(f"xb{tag}", [128, 2 * T], BF16))
                sq = sc.enter_context(nc.sbuf_tensor(f"sq{tag}", [128, 2 * T], BF16))
                mrow = sc.enter_context(nc.sbuf_tensor(f"mrow{tag}", [1, T], F32))
                trow = sc.enter_context(nc.sbuf_tensor(f"trow{tag}", [1, T], F32))
                rrow = sc.enter_context(nc.sbuf_tensor(f"rrow{tag}", [1, T], F32))
                xl = sc.enter_context(nc.sbuf_tensor(f"xl{tag}", [128, 2 * TW], F32))
                if gate:
                    sem_ = sc.enter_context(nc.sbuf_tensor(f"sem{tag}", [32, T], F32))

                with contextlib.ExitStack() as pc:
                    px = pc.enter_context(nc.psum_tensor(f"px{tag}", [1, T], F32))
                    pq = pc.enter_context(nc.psum_tensor(f"pq{tag}", [1, T], F32))
                    for c in range(C):
                        sl = c % 2
                        s.op("pool", lambda e, c=c, sl=sl: e.tensor_copy(
                            out=xb[:, sl * T:(sl + 1) * T], in_=x8[:, c * T:(c + 1) * T]),
                            reads=XR[c], writes=[f"lnxb.{sl}"], tag=f"lncast{c}")
                        s.op("act", lambda e, sl=sl: e.activation(
                            out=sq[:, sl * T:(sl + 1) * T], in_=xb[:, sl * T:(sl + 1) * T],
                            func=AF.Square),
                            reads=[f"lnxb.{sl}"], writes=[f"lnsq.{sl}"])
                        for tc in range(TC):
                            s.op("pe", lambda e, c=c, sl=sl, tc=tc: e.matmul(
                                px[0:1, tc * TW:(tc + 1) * TW], ones_col_bf[:, :],
                                xb[:, sl * T + tc * TW: sl * T + (tc + 1) * TW],
                                start=(c == 0), stop=(c == C - 1)),
                                reads=[f"lnxb.{sl}", "ones_col"], writes=[f"lnpx.{tc}"])
                            s.op("pe", lambda e, c=c, sl=sl, tc=tc: e.matmul(
                                pq[0:1, tc * TW:(tc + 1) * TW], ones_col_bf[:, :],
                                sq[:, sl * T + tc * TW: sl * T + (tc + 1) * TW],
                                start=(c == 0), stop=(c == C - 1)),
                                reads=[f"lnsq.{sl}", "ones_col"], writes=[f"lnpq.{tc}"])
                    pxr = [f"lnpx.{t}" for t in range(TC)]
                    pqr = [f"lnpq.{t}" for t in range(TC)]
                    s.op("dve", lambda e: e.tensor_scalar_mul(
                        mrow[:, :], px[0:1, :], 1.0 / D), reads=pxr, writes=["mrow"])
                    s.op("dve", lambda e: e.tensor_mul(
                        trow[:, :], mrow[:, :], mrow[:, :]), reads=["mrow"], writes=["trow"])
                    s.op("dve", lambda e: e.scalar_tensor_tensor(
                        out=trow[:, :], in0=pq[0:1, :], scalar=1.0 / D, in1=trow[:, :],
                        op0=ALU.mult, op1=ALU.subtract),
                        reads=pqr + ["trow"], writes=["trow"])
                    s.op("act", lambda e: e.activation(
                        out=trow[:, :], in_=trow[:, :], func=AF.Ln, bias=eps1[:, :]),
                        reads=["trow", "eps1"], writes=["trow"])
                    s.op("act", lambda e: e.activation(
                        out=rrow[:, :], in_=trow[:, :], func=AF.Exp, scale=-0.5),
                        reads=["trow"], writes=["rrow"])
                s.barrier()

                with contextlib.ExitStack() as pc:
                    pmb = pc.enter_context(nc.psum_tensor(f"pmb{tag}", [128, TW], F32))
                    prb = pc.enter_context(nc.psum_tensor(f"prb{tag}", [128, TW], F32))
                    if gate:
                        pg = pc.enter_context(nc.psum_tensor(f"pg{tag}", [32, T], F32))
                    for tc in range(TC):
                        s.op("pe", lambda e, tc=tc: e.matmul(
                            pmb[:, :], ones_row_f[:, :], mrow[0:1, tc * TW:(tc + 1) * TW],
                            start=True, stop=True),
                            reads=["mrow", "ones_row_f"], writes=["pmb"])
                        s.op("pe", lambda e, tc=tc: e.matmul(
                            prb[:, :], ones_row_f[:, :], rrow[0:1, tc * TW:(tc + 1) * TW],
                            start=True, stop=True),
                            reads=["rrow", "ones_row_f"], writes=["prb"])
                        for c in range(C):
                            sl = c % 2
                            s.op("dve", lambda e, c=c, tc=tc, sl=sl: e.tensor_sub(
                                xl[:, sl * TW:(sl + 1) * TW], x_view(c, tc), pmb[:, :]),
                                reads=XR[c][tc:tc + 1] + ["pmb"], writes=[f"lnxl.{sl}"])
                            s.op("dve", lambda e, sl=sl: e.tensor_mul(
                                xl[:, sl * TW:(sl + 1) * TW],
                                xl[:, sl * TW:(sl + 1) * TW], prb[:, :]),
                                reads=[f"lnxl.{sl}", "prb"], writes=[f"lnxl.{sl}"])
                            s.op("act", lambda e, c=c, tc=tc, sl=sl: e.activation(
                                out=h_view(c, tc), in_=xl[:, sl * TW:(sl + 1) * TW],
                                func=AF.Copy),
                                reads=[f"lnxl.{sl}"], writes=[HR[c][tc]])
                            if gate:
                                s.op("pe", lambda e, c=c, tc=tc, sl=sl: e.matmul(
                                    pg[0:32, tc * TW:(tc + 1) * TW],
                                    gwT[:, c * 32:(c + 1) * 32],
                                    xl[:, sl * TW:(sl + 1) * TW],
                                    start=(c == 0), stop=(c == C - 1)),
                                    reads=[f"lnxl.{sl}", "gwT"], writes=[f"lnpg.{tc}"])
                    if gate:
                        for tc in range(TC):
                            s.op("act", lambda e, tc=tc: e.activation(
                                out=sem_[0:32, tc * TW:(tc + 1) * TW],
                                in_=pg[0:32, tc * TW:(tc + 1) * TW], func=AF.Sigmoid),
                                reads=[f"lnpg.{tc}"], writes=[f"sem.{tc}"])
                s.barrier()

                if not gate:
                    return

                with contextlib.ExitStack() as gc:
                    sc_all = gc.enter_context(nc.sbuf_tensor(f"scall{tag}", [128, 512], F32))
                    scb = gc.enter_context(nc.sbuf_tensor(f"scb{tag}", [128, 512], F32))
                    u1 = gc.enter_context(nc.sbuf_tensor(f"u1{tag}", [128, 128], F32))
                    u2 = gc.enter_context(nc.sbuf_tensor(f"u2{tag}", [128, 128], F32))
                    gsc = gc.enter_context(nc.sbuf_tensor(f"gsc{tag}", [128, 128], F32))
                    g8 = gc.enter_context(nc.sbuf_tensor(f"g8{tag}", [128, 8], F32))
                    gmask = gc.enter_context(nc.sbuf_tensor(f"gmask{tag}", [128, 128], F32))
                    smask = gc.enter_context(nc.sbuf_tensor(f"smask{tag}", [128, 512], F32))
                    masked = gc.enter_context(nc.sbuf_tensor(f"masked{tag}", [128, 512], F32))
                    m8 = gc.enter_context(nc.sbuf_tensor(f"m8{tag}", [128, 8], F32))
                    selw = gc.enter_context(nc.sbuf_tensor(f"selw{tag}", [128, 512], F32))
                    dnm = gc.enter_context(nc.sbuf_tensor(f"dnm{tag}", [128, 16], F32))
                    with contextlib.ExitStack() as pc:
                        pt0 = pc.enter_context(nc.psum_tensor(f"pt0{tag}", [128, 32], F32))
                        pt1 = pc.enter_context(nc.psum_tensor(f"pt1{tag}", [128, 32], F32))
                        ptc0 = pc.enter_context(nc.psum_tensor(f"ptc0{tag}", [32, 128], F32))
                        ptc1 = pc.enter_context(nc.psum_tensor(f"ptc1{tag}", [32, 128], F32))
                        for t16 in range(16):
                            pt = (pt0, pt1)[t16 % 2]
                            key = f"lnpt.{t16 % 2}"
                            s.op("pe", lambda e, t16=t16, pt=pt: e.transpose(
                                pt[:, :], sem_[0:32, t16 * 128:(t16 + 1) * 128],
                                id_f32[0:32, 0:32]),
                                reads=[f"sem.{t16 // 4}", "id_f32"], writes=[key])
                            s.op("act", lambda e, t16=t16, pt=pt: e.activation(
                                out=sc_all[:, t16 * 32:(t16 + 1) * 32], in_=pt[:, :],
                                func=AF.Copy),
                                reads=[key], writes=[f"scall.{t16}"])
                        scr = [f"scall.{t}" for t in range(16)]
                        for t16 in range(16):
                            s.op("dve", lambda e, t16=t16: e.tensor_add(
                                scb[:, t16 * 32:(t16 + 1) * 32],
                                sc_all[:, t16 * 32:(t16 + 1) * 32], bias32[:, :]),
                                reads=[f"scall.{t16}", "bias32"], writes=["scb"])

                        def v4(t, j):
                            return bass.AP(t, j, [[512, 128], [32, 16], [4, 8]])

                        pairs = [(0, 1), (0, 2), (0, 3), (1, 2), (1, 3), (2, 3)]
                        for pi, (a, b) in enumerate(pairs):
                            dst = u1 if pi == 0 else u2
                            dk = "u1" if pi == 0 else "u2"
                            s.op("dve", lambda e, a=a, b=b, dst=dst: e.tensor_add(
                                bass.AP(dst, 0, [[128, 128], [8, 16], [1, 8]]),
                                v4(scb, a), v4(scb, b)),
                                reads=["scb"], writes=[dk])
                            if pi == 0:
                                continue
                            tgt = u1 if pi < 5 else gsc
                            tk = "u1" if pi < 5 else "gsc"
                            s.op("dve", lambda e, tgt=tgt: e.tensor_max(
                                tgt[:, :], u1[:, :], u2[:, :]),
                                reads=["u1", "u2"], writes=[tk])
                        for t16 in range(16):
                            s.op("dve", lambda e, t16=t16: e.max(
                                out=g8[:, :], in_=gsc[:, t16 * 8:(t16 + 1) * 8]),
                                reads=["gsc"], writes=["g8"])
                            s.op("dve", lambda e, t16=t16: e.tensor_scalar(
                                out=gmask[:, t16 * 8:(t16 + 1) * 8],
                                in0=gsc[:, t16 * 8:(t16 + 1) * 8],
                                scalar1=g8[:, 3:4], scalar2=None, op0=ALU.is_ge),
                                reads=["gsc", "g8"], writes=["gmask"])
                        for j in range(4):
                            s.op("dve", lambda e, j=j: e.tensor_copy(
                                out=v4(smask, j),
                                in_=bass.AP(gmask, 0, [[128, 128], [8, 16], [1, 8]])),
                                reads=["gmask"], writes=["smask"])
                        s.op("dve", lambda e: e.tensor_mul(
                            masked[:, :], scb[:, :], smask[:, :]),
                            reads=["scb", "smask"], writes=["maskedg"])
                        for t16 in range(16):
                            s.op("dve", lambda e, t16=t16: e.max(
                                out=m8[:, :], in_=masked[:, t16 * 32:(t16 + 1) * 32]),
                                reads=["maskedg"], writes=["m8"])
                            s.op("dve", lambda e, t16=t16: e.tensor_scalar(
                                out=selw[:, t16 * 32:(t16 + 1) * 32],
                                in0=masked[:, t16 * 32:(t16 + 1) * 32],
                                scalar1=m8[:, 7:8], scalar2=None, op0=ALU.is_ge),
                                reads=["maskedg", "m8"], writes=["selw"])
                        s.op("dve", lambda e: e.tensor_mul(
                            selw[:, :], selw[:, :], sc_all[:, :]),
                            reads=["selw"] + scr, writes=["selw"])
                        for t16 in range(16):
                            s.op("dve", lambda e, t16=t16: e.reduce_sum(
                                out=dnm[:, t16:t16 + 1],
                                in_=selw[:, t16 * 32:(t16 + 1) * 32], axis=AX.X),
                                reads=["selw"], writes=["dnm"])
                        s.op("dve", lambda e: e.tensor_scalar_add(
                            dnm[:, :], dnm[:, :], 1e-20), reads=["dnm"], writes=["dnm"])
                        s.op("dve", lambda e: e.reciprocal(
                            dnm[:, :], dnm[:, :]), reads=["dnm"], writes=["dnm"])
                        for t16 in range(16):
                            s.op("dve", lambda e, t16=t16: e.tensor_scalar(
                                out=combine_sb[:, t16 * 32:(t16 + 1) * 32],
                                in0=selw[:, t16 * 32:(t16 + 1) * 32],
                                scalar1=dnm[:, t16:t16 + 1], scalar2=2.5,
                                op0=ALU.mult, op1=ALU.mult),
                                reads=["selw", "dnm"], writes=[f"comb.{t16}"])
                            ptc = (ptc0, ptc1)[t16 % 2]
                            pk = f"lnptc.{t16 % 2}"
                            s.op("pe", lambda e, t16=t16, ptc=ptc: e.transpose(
                                ptc[:, :], combine_sb[:, t16 * 32:(t16 + 1) * 32],
                                id_f32[:, :]),
                                reads=[f"comb.{t16}", "id_f32"], writes=[pk])
                            s.op("act", lambda e, t16=t16, ptc=ptc: e.activation(
                                out=combT_sb[0:32, t16 * 128:(t16 + 1) * 128],
                                in_=ptc[:, :], func=AF.Copy),
                                reads=[pk], writes=[f"combT.{t16 // 4}"])
                s.dma("sp", lambda e: e.dma_start(
                    out=combT_d[tag][:, :], in_=combT_sb[:, :]),
                    reads=[f"combT.{t}" for t in range(4)], writes=["combTd"],
                    tag="combTout")
                if _BUILD_DUMP[0] == "comb" and tag == "f1":
                    s.dma("sp", lambda e: e.dma_start(
                        out=dbg_d[:, 0:512], in_=combine_sb[:, :]),
                        reads=[f"comb.{t}" for t in range(16)], tag="dumpcomb")
                    s.dma("sp", lambda e: e.dma_start(
                        out=dbg_d[0:32, 512:512 + T], in_=sem_[:, :]),
                        reads=[f"sem.{t}" for t in range(4)], tag="dumpsem")
                s.barrier()

        def ffn_call(tag):
            with contextlib.ExitStack() as fc:
                combine_sb = fc.enter_context(nc.sbuf_tensor(f"combine{tag}", [128, 512], F32))
                combT_sb = fc.enter_context(nc.sbuf_tensor(f"combT{tag}", [32, T], BF16))

                layernorm(tag, gate=True, combine_sb=combine_sb, combT_sb=combT_sb)

                with contextlib.ExitStack() as ec:
                    wst = ec.enter_context(nc.sbuf_tensor(f"wst{tag}", [128, 2 * 2048], F32))
                    wg = ec.enter_context(nc.sbuf_tensor(f"wg{tag}", [128, 2 * 4096], BF16))
                    wu = ec.enter_context(nc.sbuf_tensor(f"wu{tag}", [128, 2 * 4096], BF16))
                    wd = ec.enter_context(nc.sbuf_tensor(f"wd{tag}", [128, 2 * 4096], BF16))
                    zb = ec.enter_context(nc.sbuf_tensor(f"zb{tag}", [128, 3 * 2048], BF16))
                    zt = ec.enter_context(nc.sbuf_tensor(f"zt{tag}", [128, 2 * TW], BF16))
                    gs = ec.enter_context(nc.sbuf_tensor(f"gs{tag}", [128, 2 * TW], F32))
                    cb = ec.enter_context(nc.sbuf_tensor(f"cb{tag}", [128, 2 * T], BF16))
                    with contextlib.ExitStack() as pc:
                        pgA = pc.enter_context(nc.psum_tensor(f"pgA{tag}", [128, TW], F32))
                        pgB = pc.enter_context(nc.psum_tensor(f"pgB{tag}", [128, TW], F32))
                        puA = pc.enter_context(nc.psum_tensor(f"puA{tag}", [128, TW], F32))
                        puB = pc.enter_context(nc.psum_tensor(f"puB{tag}", [128, TW], F32))
                        pd = [pc.enter_context(nc.psum_tensor(f"pd{j}{tag}", [128, TW], F32))
                              for j in range(4)]

                        nstage = [0]

                        def load_expert(e_):
                            sl = e_ % 2
                            if e_ < 32:
                                s.dma("act", lambda e, e_=e_, sl=sl: e.dma_start(
                                    out=cb[:, sl * T:(sl + 1) * T],
                                    in_=bass.AP(combT_d[tag].tensor, e_ * T,
                                                [[0, 128], [1, T]])),
                                    reads=["combTd"], writes=[f"cb.{sl}"],
                                    tag=f"cbdma{e_}")
                            if tag == "f2":
                                for dstbuf, dk in ((wg, "wg"), (wu, "wu"), (wd, "wd")):
                                    s.dma("sp", lambda e, dstbuf=dstbuf, dk=dk, e_=e_, sl=sl:
                                          e.dma_start(
                                        out=dstbuf[:, sl * 4096:(sl + 1) * 4096],
                                        in_=wbf_d[dk][e_]),
                                        writes=[f"{dk}.{sl}"], tag=f"wbfin{e_}")
                                return
                            for src, dstbuf, dk, scale, nch in (
                                    (gatew, wg, "wg", None, 8),
                                    (upw, wu, "wu", None, 8),
                                    (downw, wd, "wd", 0.5, 4)):
                                for half in range(2):
                                    ss = nstage[0] % 2
                                    nstage[0] += 1
                                    hc2 = nch // 2
                                    s.dma("sp", lambda e, src=src, e_=e_, half=half, ss=ss, hc2=hc2:
                                          e.dma_start(
                                        out=wst[:, ss * 2048:(ss + 1) * 2048].rearrange(
                                            "p (c i) -> p c i", c=hc2),
                                        in_=src[e_].rearrange(
                                            "(c p) i -> p c i", p=128)[:, half * hc2:(half + 1) * hc2, :]),
                                        writes=[f"wst.{ss}"], tag=f"wdma{e_}")
                                    cscale = 1.0 if scale is None else scale
                                    s.op("act", lambda e, dstbuf=dstbuf, sl=sl, half=half, ss=ss, cscale=cscale:
                                         e.activation(
                                        out=dstbuf[:, sl * 4096 + half * 2048:
                                                   sl * 4096 + (half + 1) * 2048],
                                        in_=wst[:, ss * 2048:(ss + 1) * 2048],
                                        func=AF.Copy, scale=cscale),
                                        reads=[f"wst.{ss}"], writes=[f"{dk}.{sl}"])
                                s.dma("act", lambda e, dstbuf=dstbuf, dk=dk, e_=e_, sl=sl:
                                      e.dma_start(
                                    out=wbf_d[dk][e_],
                                    in_=dstbuf[:, sl * 4096:(sl + 1) * 4096]),
                                    reads=[f"{dk}.{sl}"], tag=f"wbfout{e_}")

                        def emit_gu(e_, tc):
                            sl = e_ % 2
                            routed = e_ < 32
                            if True:
                                ztc = (4 * e_ + tc) % 3
                                for ic in range(NI):
                                    par = ic % 2
                                    pgp = (pgA, pgB)[par]
                                    pup = (puA, puB)[par]
                                    for kc in range(C):
                                        s.op("pe", lambda e, sl=sl, tc=tc, ic=ic, kc=kc, pgp=pgp:
                                             e.matmul(
                                            pgp[:, :],
                                            wg[:, sl * 4096 + kc * 512 + ic * 128:
                                               sl * 4096 + kc * 512 + (ic + 1) * 128],
                                            h_view(kc, tc),
                                            start=(kc == 0), stop=(kc == C - 1)),
                                            reads=[f"wg.{sl}", HR[kc][tc]],
                                            writes=[f"pg.{par}"])
                                    s.op("act", lambda e, par=par, pgp=pgp: e.activation(
                                        out=gs[:, par * TW:(par + 1) * TW],
                                        in_=pgp[:, :], func=AF.Silu),
                                        reads=[f"pg.{par}"], writes=[f"gs.{par}"])
                                    for kc in range(C):
                                        s.op("pe", lambda e, sl=sl, tc=tc, ic=ic, kc=kc, pup=pup:
                                             e.matmul(
                                            pup[:, :],
                                            wu[:, sl * 4096 + kc * 512 + ic * 128:
                                               sl * 4096 + kc * 512 + (ic + 1) * 128],
                                            h_view(kc, tc),
                                            start=(kc == 0), stop=(kc == C - 1)),
                                            reads=[f"wu.{sl}", HR[kc][tc]],
                                            writes=[f"pu.{par}"])
                                    if routed:
                                        s.op("dve", lambda e, par=par, pup=pup: e.tensor_mul(
                                            zt[:, par * TW:(par + 1) * TW],
                                            gs[:, par * TW:(par + 1) * TW], pup[:, :]),
                                            reads=[f"gs.{par}", f"pu.{par}"],
                                            writes=[f"zt.{par}"])
                                        s.op("dve", lambda e, par=par, ztc=ztc, ic=ic, tc=tc, sl=sl:
                                             e.tensor_mul(
                                            zb[:, ztc * 2048 + ic * TW:
                                               ztc * 2048 + (ic + 1) * TW],
                                            zt[:, par * TW:(par + 1) * TW],
                                            cb[:, sl * T + tc * TW: sl * T + (tc + 1) * TW]),
                                            reads=[f"zt.{par}", f"cb.{sl}"],
                                            writes=[f"z.{ztc}.{ic}"])
                                    else:
                                        s.op("dve", lambda e, par=par, ztc=ztc, ic=ic, pup=pup:
                                             e.tensor_mul(
                                            zb[:, ztc * 2048 + ic * TW:
                                               ztc * 2048 + (ic + 1) * TW],
                                            gs[:, par * TW:(par + 1) * TW], pup[:, :]),
                                            reads=[f"gs.{par}", f"pu.{par}"],
                                            writes=[f"z.{ztc}.{ic}"])

                        def emit_down_mms(e_, tc, hcs):
                            sl = e_ % 2
                            ztc = (4 * e_ + tc) % 3
                            for hc in hcs:
                                b = hc % 4
                                for ic in range(NI):
                                    s.op("pe", lambda e, sl=sl, hc=hc, ic=ic, ztc=ztc, b=b:
                                         e.matmul(
                                        pd[b][:, :],
                                        wd[:, sl * 4096 + ic * 1024 + hc * 128:
                                           sl * 4096 + ic * 1024 + (hc + 1) * 128],
                                        zb[:, ztc * 2048 + ic * TW:
                                           ztc * 2048 + (ic + 1) * TW],
                                        start=(ic == 0), stop=(ic == NI - 1)),
                                        reads=[f"wd.{sl}", f"z.{ztc}.{ic}"],
                                        writes=[f"pd.{b}"])

                        def emit_down_evac(e_, tc, hcs):
                            for hc in hcs:
                                b = hc % 4
                                s.op("dve", lambda e, hc=hc, tc=tc, b=b: e.tensor_add(
                                    x_view(hc, tc), pd[b][:, :], x_view(hc, tc)),
                                    reads=[f"pd.{b}", XR[hc][tc]],
                                    writes=[XR[hc][tc]])

                        load_expert(0)
                        load_expert(1)
                        seq = [(e_, tc) for e_ in range(NEXP) for tc in range(TC)]
                        LAG = 3

                        def emit_down(j):
                            d_e, d_tc = seq[j]
                            for hc in range(C):
                                emit_down_mms(d_e, d_tc, [hc])
                                emit_down_evac(d_e, d_tc, [hc])
                            if d_tc == TC - 1 and d_e + 2 < NEXP:
                                load_expert(d_e + 2)

                        for i, (e_, tc) in enumerate(seq):
                            if i >= LAG:
                                emit_down(i - LAG)
                            emit_gu(e_, tc)
                        for j in range(len(seq) - LAG, len(seq)):
                            emit_down(j)
            s.barrier()

        def attn_call(tag):
            layernorm(tag, gate=False)
            with contextlib.ExitStack() as ac:
                ao = ac.enter_context(nc.sbuf_tensor(f"ao{tag}", [128, C * T], BF16))
                qkvb = ac.enter_context(nc.sbuf_tensor(f"qkvb{tag}", [128, 3 * T], BF16))
                wqst = ac.enter_context(nc.sbuf_tensor(f"wqst{tag}", [128, 2 * 1024], F32))
                wqb = ac.enter_context(nc.sbuf_tensor(f"wqb{tag}", [128, 2 * 3 * 1024], BF16))
                vt = ac.enter_context(nc.sbuf_tensor(f"vt{tag}", [128, 16 * 128], BF16))
                pP = ac.enter_context(nc.sbuf_tensor(f"pP{tag}", [128, 2 * T], BF16))
                ptsb = ac.enter_context(nc.sbuf_tensor(f"ptsb{tag}", [128, 2 * 128], BF16))
                mx4 = ac.enter_context(nc.sbuf_tensor(f"mx4{tag}", [128, 8], F32))
                mbias = ac.enter_context(nc.sbuf_tensor(f"mbias{tag}", [128, 2], F32))
                rs4 = ac.enter_context(nc.sbuf_tensor(f"rs4{tag}", [128, 8], F32))
                rsum = ac.enter_context(nc.sbuf_tensor(f"rsum{tag}", [128, 2], F32))

                with contextlib.ExitStack() as pc:
                    pS = pc.enter_context(nc.psum_tensor(f"pS{tag}", [128, T], F32))
                    pPT = [pc.enter_context(nc.psum_tensor(f"pPT{j}{tag}", [128, 128], BF16))
                           for j in range(2)]
                    pO = [pc.enter_context(nc.psum_tensor(f"pO{j}{tag}", [128, 128], F32))
                          for j in range(2)]

                    nqst = [0]

                    def load_qkvw(c):
                        sl = c % 2
                        for m in range(3):
                            ss = nqst[0] % 2
                            nqst[0] += 1
                            oc = m * 8 + c
                            s.dma("sp", lambda e, oc=oc, ss=ss: e.dma_start(
                                out=wqst[:, ss * 1024:(ss + 1) * 1024].rearrange(
                                    "p (c m) -> p c m", c=8),
                                in_=qkvw[oc].rearrange("(c p) m -> p c m", p=128)),
                                writes=[f"wqst.{ss}"], tag=f"qkvw{oc}")
                            s.op("pool", lambda e, m=m, sl=sl, ss=ss: e.tensor_copy(
                                out=wqb[:, sl * 3072 + m * 1024:sl * 3072 + (m + 1) * 1024],
                                in_=wqst[:, ss * 1024:(ss + 1) * 1024]),
                                reads=[f"wqst.{ss}"], writes=[f"wqb.{sl}.{m}"])

                    def chunk(c):
                        sl = c % 2
                        for m in range(3):
                            for tc in range(TC):
                                for kc in range(C):
                                    s.op("pe", lambda e, m=m, tc=tc, kc=kc, sl=sl: e.matmul(
                                        pS[:, tc * TW:(tc + 1) * TW],
                                        wqb[:, sl * 3072 + m * 1024 + kc * 128:
                                            sl * 3072 + m * 1024 + (kc + 1) * 128],
                                        h_view(kc, tc),
                                        start=(kc == 0), stop=(kc == C - 1)),
                                        reads=[f"wqb.{sl}.{m}", HR[kc][tc]],
                                        writes=[f"pS.{tc}"])
                                s.op("act", lambda e, m=m, tc=tc: e.activation(
                                    out=qkvb[:, m * T + tc * TW: m * T + (tc + 1) * TW],
                                    in_=pS[:, tc * TW:(tc + 1) * TW], func=AF.Copy),
                                    reads=[f"pS.{tc}"], writes=[f"qkv.{m}.{tc}"])
                        for kc in range(16):
                            p3 = pPT[kc % 2]
                            k3 = f"pPT.{kc % 2}"
                            s.op("pe", lambda e, kc=kc, p3=p3: e.transpose(
                                p3[:, :], qkvb[:, 2 * T + kc * 128:2 * T + (kc + 1) * 128],
                                id_bf[:, :]),
                                reads=[f"qkv.2.{kc // 4}", "id_bf"], writes=[k3])
                            s.op("dve", lambda e, kc=kc, p3=p3: e.tensor_copy(
                                out=vt[:, kc * 128:(kc + 1) * 128], in_=p3[:, :]),
                                reads=[k3], writes=[f"vt.{kc}"])

                        def qk_pre(par, qt):
                            qoff = 64 * par
                            klen = (qt + 1) * 128
                            nk = (klen + TW - 1) // TW
                            jd = qt // 4
                            for j in range(nk):
                                w = min(TW, klen - j * TW)
                                s.op("pe", lambda e, qt=qt, j=j, w=w, qoff=qoff:
                                     e.matmul(
                                    pS[:, j * TW: j * TW + w],
                                    qkvb[qoff:qoff + 64, qt * 128:(qt + 1) * 128],
                                    qkvb[qoff:qoff + 64, T + j * TW: T + j * TW + w],
                                    start=True, stop=True),
                                    reads=[f"qkv.0.{qt // 4}", f"qkv.1.{j}"],
                                    writes=[f"pS.{j}"])
                                if j == jd:
                                    s.op("dve", lambda e, qt=qt: e.tensor_add(
                                        pS[:, qt * 128:(qt + 1) * 128],
                                        pS[:, qt * 128:(qt + 1) * 128], cmask[:, :]),
                                        reads=[f"pS.{jd}", "cmask"], writes=[f"pS.{jd}"])
                                s.op("dve", lambda e, j=j, w=w, par=par: e.reduce_max(
                                    out=mx4[:, par * 4 + j:par * 4 + j + 1],
                                    in_=pS[:, j * TW: j * TW + w], axis=AX.X),
                                    reads=[f"pS.{j}"], writes=[f"mx4.{par}"])
                            s.op("dve", lambda e, nk=nk, par=par: e.reduce_max(
                                out=mbias[:, par:par + 1], in_=mx4[:, par * 4:par * 4 + nk],
                                axis=AX.X),
                                reads=[f"mx4.{par}"], writes=[f"mbias.{par}"])
                            s.op("dve", lambda e, par=par: e.tensor_scalar_mul(
                                mbias[:, par:par + 1], mbias[:, par:par + 1], -0.125),
                                reads=[f"mbias.{par}"], writes=[f"mbias.{par}"])
                            for j in range(nk):
                                w = min(TW, klen - j * TW)
                                s.op("act", lambda e, j=j, w=w, par=par: e.activation(
                                    out=pP[:, par * T + j * TW: par * T + j * TW + w],
                                    in_=pS[:, j * TW: j * TW + w], func=AF.Exp,
                                    scale=0.125, bias=mbias[:, par:par + 1],
                                    accum_out=rs4[:, par * 4 + j:par * 4 + j + 1]),
                                    reads=[f"pS.{j}", f"mbias.{par}"],
                                    writes=[f"P.{par}.{j}", f"rs4.{par}.{j}"])

                        def smax_post(par, qt):
                            klen = (qt + 1) * 128
                            nk = (klen + TW - 1) // TW
                            s.op("dve", lambda e, nk=nk, par=par: e.reduce_sum(
                                out=rsum[:, par:par + 1], in_=rs4[:, par * 4:par * 4 + nk],
                                axis=AX.X),
                                reads=[f"rs4.{par}.{j}" for j in range(nk)],
                                writes=[f"rsum.{par}"])
                            s.op("dve", lambda e, par=par: e.reciprocal(
                                rsum[:, par:par + 1], rsum[:, par:par + 1]),
                                reads=[f"rsum.{par}"], writes=[f"rsum.{par}"])
                            pkeys = [f"P.{par}.{j}" for j in range(nk)]
                            s.op("dve", lambda e, klen=klen, par=par: e.tensor_scalar_mul(
                                pP[:, par * T:par * T + klen],
                                pP[:, par * T:par * T + klen], rsum[:, par:par + 1]),
                                reads=[f"rsum.{par}"] + pkeys, writes=pkeys)

                        HEADN = 3

                        def tpv(par, qt, c, part="all"):
                            qoff = 64 * par
                            tp = (0, 64) if par else None

                            def tr(kc):
                                pp = pPT[kc % 2]
                                ppk = f"pPT.{kc % 2}"
                                s.op("pe", lambda e, kc=kc, pp=pp, par=par: e.transpose(
                                    pp[:, :], pP[:, par * T + kc * 128:
                                                  par * T + (kc + 1) * 128], id_bf[:, :]),
                                    reads=[f"P.{par}.{kc // 4}", "id_bf"], writes=[ppk])
                                s.op("dve", lambda e, kc=kc, pp=pp: e.tensor_copy(
                                    out=ptsb[:, (kc % 2) * 128:(kc % 2 + 1) * 128],
                                    in_=pp[:, :]),
                                    reads=[ppk], writes=[f"ptsb.{kc % 2}"])

                            def pv(kc):
                                s.op("pe", lambda e, kc=kc, qt=qt, qoff=qoff, tp=tp, par=par:
                                     e.matmul(
                                    pO[par][qoff:qoff + 64, 0:128],
                                    vt[:, kc * 128 + qoff: kc * 128 + qoff + 64],
                                    ptsb[:, (kc % 2) * 128:(kc % 2 + 1) * 128],
                                    start=(kc == 0), stop=(kc == qt),
                                    tile_position=tp),
                                    reads=[f"vt.{kc}", f"ptsb.{kc % 2}"],
                                    writes=[f"pO.{par}"])

                            if part in ("head", "all"):
                                hi = min(HEADN, qt + 1)
                                tr(0)
                                for kc in range(1, hi):
                                    tr(kc)
                                    pv(kc - 1)
                                if part == "head" and hi <= qt:
                                    pass
                                else:
                                    pass
                            if part in ("tail", "all"):
                                hi = min(HEADN, qt + 1)
                                for kc in range(hi, qt + 1):
                                    tr(kc)
                                    pv(kc - 1)
                                pv(qt)
                                s.op("act", lambda e, c=c, qt=qt, qoff=qoff, par=par:
                                     e.activation(
                                    out=ao[qoff:qoff + 64, c * T + qt * 128:
                                           c * T + (qt + 1) * 128],
                                    in_=pO[par][qoff:qoff + 64, 0:128], func=AF.Copy),
                                    reads=[f"pO.{par}"], writes=[f"ao.{c}.{qt // 4}"])

                        for qt in range(QT):
                            if qt >= 1:
                                tpv(1, qt - 1, c, "head")
                            qk_pre(0, qt)
                            if qt >= 1:
                                tpv(1, qt - 1, c, "tail")
                            smax_post(0, qt)
                            tpv(0, qt, c, "head")
                            qk_pre(1, qt)
                            tpv(0, qt, c, "tail")
                            smax_post(1, qt)
                        tpv(1, QT - 1, c)

                    load_qkvw(0)
                    load_qkvw(1)
                    for c in range(C):
                        chunk(c)
                        if c + 2 < C:
                            load_qkvw(c + 2)

                if _BUILD_DUMP[0] == "ao":
                    s.dma("sp", lambda e: e.dma_start(
                        out=dbg_bf[:, 0:C * T], in_=ao[:, :]),
                        reads=[f"ao.{c}.{t}" for c in range(C) for t in range(TC)],
                        tag="dumpao")
                s.barrier()
                with contextlib.ExitStack() as pc:
                    pop = [pc.enter_context(nc.psum_tensor(f"pop{j}{tag}", [128, TW], F32))
                           for j in range(2)]
                    now = [0]
                    for hc in range(C):
                        ss = now[0] % 2
                        now[0] += 1
                        s.dma("sp", lambda e, hc=hc, ss=ss: e.dma_start(
                            out=wqst[:, ss * 1024:(ss + 1) * 1024].rearrange(
                                "p (c m) -> p c m", c=8),
                            in_=outw[hc].rearrange("(c p) m -> p c m", p=128)),
                            writes=[f"wqst.{ss}"], tag=f"outw{hc}")
                        s.op("pool", lambda e, hc=hc, ss=ss: e.tensor_copy(
                            out=wqb[:, (hc % 2) * 3072:(hc % 2) * 3072 + 1024],
                            in_=wqst[:, ss * 1024:(ss + 1) * 1024]),
                            reads=[f"wqst.{ss}"], writes=[f"wqb.{hc % 2}.0"])
                        for tc in range(TC):
                            pb = pop[tc % 2]
                            pbk = f"pop.{tc % 2}"
                            for kc in range(C):
                                s.op("pe", lambda e, hc=hc, tc=tc, kc=kc, pb=pb: e.matmul(
                                    pb[:, :],
                                    wqb[:, (hc % 2) * 3072 + kc * 128:
                                        (hc % 2) * 3072 + (kc + 1) * 128],
                                    ao[:, kc * T + tc * TW: kc * T + (tc + 1) * TW],
                                    start=(kc == 0), stop=(kc == C - 1)),
                                    reads=[f"wqb.{hc % 2}.0", f"ao.{kc}.{tc}"],
                                    writes=[pbk])
                            s.op("dve", lambda e, hc=hc, tc=tc, pb=pb: e.tensor_add(
                                x_view(hc, tc), pb[:, :], x_view(hc, tc)),
                                reads=[pbk, XR[hc][tc]], writes=[XR[hc][tc]])
            s.barrier()

        def dump_sbuf(t, nbytes_elems, rkeys, name):
            s.dma("sp", lambda e: e.dma_start(
                out=dbg_d[:, 0:nbytes_elems], in_=t),
                reads=rkeys, tag=f"dump{name}")
            s.barrier()

        calls = {"f1": lambda: ffn_call("f1"), "at": lambda: attn_call("at"),
                 "f2": lambda: ffn_call("f2")}
        for ph in phases:
            calls[ph]()

        s.dma("sp", lambda e: e.dma_start(
            out=out_d.rearrange("(c p) t -> p c t", p=128),
            in_=x8[:, :].rearrange("p (c t) -> p c t", c=C)),
            reads=allx, tag="store")
        s.barrier()

        with contextlib.ExitStack() as ms:
            eng_sems = {n: ms.enter_context(nc.semaphore(f"sem_{n}"))
                        for n in ("sp", "pe", "act", "dve", "pool")}
            dma_sems = {}
            for q in ("sp", "act"):
                for sl in range(N_DMA_SLOTS):
                    dma_sems[(q, sl)] = ms.enter_context(nc.semaphore(f"dsem_{q}{sl}"))
            with nc.Block() as block:
                s.materialize(nc, block, eng_sems, dma_sems)

    return nc


_NC_CACHE = {}
_BUILD_PHASES = [("f1", "at", "f2")]
_BUILD_DUMP = [None]
_LAST_RESULTS = [None]


def kernel(L0_attn_x, L0_attn_w_qkv, L0_attn_w_out, L0_ffn_gate_w, L0_ffn_gate_bias,
           L0_ffn_w_gate, L0_ffn_w_up, L0_ffn_w_down, L0_ffn_sh_gate, L0_ffn_sh_up,
           L0_ffn_sh_down):
    x = np.asarray(L0_attn_x, dtype=np.float32)
    w_qkv = np.asarray(L0_attn_w_qkv, dtype=np.float32)
    w_out = np.asarray(L0_attn_w_out, dtype=np.float32)
    gate_w = np.asarray(L0_ffn_gate_w, dtype=np.float32)
    gate_bias = np.asarray(L0_ffn_gate_bias, dtype=np.float32)
    w_gate = np.asarray(L0_ffn_w_gate, dtype=np.float32)
    w_up = np.asarray(L0_ffn_w_up, dtype=np.float32)
    w_down = np.asarray(L0_ffn_w_down, dtype=np.float32)
    sh_gate = np.asarray(L0_ffn_sh_gate, dtype=np.float32)
    sh_up = np.asarray(L0_ffn_sh_up, dtype=np.float32)
    sh_down = np.asarray(L0_ffn_sh_down, dtype=np.float32)

    qkvw = np.ascontiguousarray(
        w_qkv.T.reshape(D, 24, 128).transpose(1, 0, 2))
    outw = np.ascontiguousarray(
        w_out.T.reshape(D, 8, 128).transpose(1, 0, 2))
    gT = np.ascontiguousarray(sh_gate.T)
    uT = np.ascontiguousarray(sh_up.T)
    gatew = np.ascontiguousarray(np.concatenate(
        [w_gate.transpose(0, 2, 1), gT[None, :, 0:512], gT[None, :, 512:1024]], axis=0))
    upw = np.ascontiguousarray(np.concatenate(
        [w_up.transpose(0, 2, 1), uT[None, :, 0:512], uT[None, :, 512:1024]], axis=0))
    dT = sh_down.T
    downw = np.ascontiguousarray(np.concatenate(
        [w_down.transpose(0, 2, 1), dT[None, 0:512, :], dT[None, 512:1024, :]], axis=0))
    gw = np.ascontiguousarray(gate_w.T)
    gb = np.ascontiguousarray(gate_bias.reshape(1, 32))

    ph = (tuple(_BUILD_PHASES[0]), _BUILD_DUMP[0])
    if ph not in _NC_CACHE:
        _NC_CACHE[ph] = build_program(phases=ph[0])
    nc = _NC_CACHE[ph]

    in_maps = []
    for b in range(8):
        in_maps.append({
            "x": np.ascontiguousarray(x[b].T),
            "qkvw": qkvw, "outw": outw, "gatew": gatew, "upw": upw,
            "downw": downw, "gw": gw, "gbias": gb,
        })
    res = run_bass_kernel_spmd(nc, in_maps, core_ids=list(range(8)))
    _LAST_RESULTS[0] = res.results
    out = np.stack([np.asarray(r["out"]).T for r in res.results], axis=0)
    return np.ascontiguousarray(out.astype(np.float32))


# version 19
# speedup vs baseline: 1.0322x; 1.0059x over previous
import contextlib
from collections import defaultdict

import numpy as np

import concourse.bass as bass
import concourse.mybir as mybir
from concourse.bass_utils import run_bass_kernel_spmd

F32 = mybir.dt.float32
BF16 = mybir.dt.bfloat16
AF = mybir.ActivationFunctionType
ALU = mybir.AluOpType
AX = mybir.AxisListType

T = 2048
D = 1024
C = 8
TC = 4
TW = 512
NEXP = 34
NI = 4
NH = 16
QT = 16

STRICT = ("dve", "act", "pool")
N_DMA_SLOTS = 6

ENG_SECT = {"sp": "sync", "pe": "tensor", "act": "scalar", "dve": "vector", "pool": "gpsimd"}


class _Op:
    __slots__ = ("oid", "engine", "fn", "deps", "needs_inc", "inc_val",
                 "is_dma", "slot_key", "slot_use", "tag")


class Sched:
    def __init__(self):
        self.ops = []
        self.eng = defaultdict(list)
        self.res = {}
        self.dma_q_count = defaultdict(int)
        self.dma_slot_uses = defaultdict(int)

    def op(self, engine, fn, reads=(), writes=(), extra_deps=(), tag=""):
        deps = set()
        for r in reads:
            w, _ = self.res.get(r, (None, []))
            if w is not None:
                deps.add(w)
        for wk in writes:
            w, rd = self.res.get(wk, (None, []))
            if w is not None:
                deps.add(w)
            deps.update(rd)
        deps.update(e for e in extra_deps if e is not None)
        o = _Op()
        o.oid = len(self.ops)
        o.engine = engine
        o.fn = fn
        o.deps = deps
        o.needs_inc = False
        o.inc_val = 0
        o.is_dma = False
        o.slot_key = None
        o.slot_use = 0
        o.tag = tag
        self.ops.append(o)
        self.eng[engine].append(o)
        for r in reads:
            w, rd = self.res.get(r, (None, []))
            self.res[r] = (w, rd + [o.oid])
        for wk in writes:
            self.res[wk] = (o.oid, [])
        return o.oid

    def dma(self, queue, fn, reads=(), writes=(), tag=""):
        oid = self.op(queue, fn, reads, writes, tag=tag)
        o = self.ops[oid]
        o.is_dma = True
        slot = self.dma_q_count[queue] % N_DMA_SLOTS
        self.dma_q_count[queue] += 1
        o.slot_key = (queue, slot)
        self.dma_slot_uses[o.slot_key] += 1
        o.slot_use = self.dma_slot_uses[o.slot_key]
        return oid

    def barrier(self):
        lasts = set()
        for lst in self.eng.values():
            if lst:
                lasts.add(lst[-1].oid)
        dma_last = {}
        for o in self.ops:
            if o.is_dma:
                dma_last[o.slot_key] = o.oid
        deps = lasts | set(dma_last.values())
        for e in ("sp", "pe", "act", "dve", "pool"):
            self.op(e, lambda eng: eng.nop(), extra_deps=sorted(deps), tag="barrier")

    def materialize(self, nc, block, eng_sems, dma_sems):
        for o in self.ops:
            for d in o.deps:
                p = self.ops[d]
                if p.is_dma:
                    continue
                if p.engine == o.engine and p.engine not in STRICT:
                    continue
                p.needs_inc = True
        for name, lst in self.eng.items():
            c = 0
            for o in lst:
                if o.is_dma:
                    o.inc_val = 16 * o.slot_use
                elif o.needs_inc:
                    c += 1
                    o.inc_val = c

        sched = self

        for name in ("sp", "pe", "act", "dve", "pool"):
            def emitter(eng, name=name):
                maxw = {}

                def wait(sem, val):
                    if maxw.get(sem.name, -1) >= val:
                        return
                    eng.wait_ge(sem, val)
                    maxw[sem.name] = val

                for o in sched.eng[name]:
                    for d in sorted(o.deps):
                        p = sched.ops[d]
                        if p.is_dma:
                            wait(dma_sems[p.slot_key], p.inc_val)
                        else:
                            if p.engine == o.engine and p.engine not in STRICT:
                                continue
                            if not p.needs_inc:
                                continue
                            wait(eng_sems[p.engine], p.inc_val)
                    if o.is_dma and o.slot_use > 1:
                        wait(dma_sems[o.slot_key], 16 * (o.slot_use - 1))
                    inst = o.fn(eng)
                    if o.is_dma:
                        inst.then_inc(dma_sems[o.slot_key], 16)
                    elif o.needs_inc:
                        inst.then_inc(eng_sems[o.engine], 1)

            getattr(block, ENG_SECT[name])(emitter)


def build_program(phases=("f1", "at", "f2"), dump=None):
    nc = bass.Bass("TRN2", target_bir_lowering=False)

    x_in = nc.dram_tensor("x", [D, T], F32, kind="ExternalInput").ap()
    qkvw = nc.dram_tensor("qkvw", [24, D, 128], F32, kind="ExternalInput").ap()
    outw = nc.dram_tensor("outw", [8, D, 128], F32, kind="ExternalInput").ap()
    gatew = nc.dram_tensor("gatew", [NEXP, D, 512], F32, kind="ExternalInput").ap()
    upw = nc.dram_tensor("upw", [NEXP, D, 512], F32, kind="ExternalInput").ap()
    downw = nc.dram_tensor("downw", [NEXP, 512, D], F32, kind="ExternalInput").ap()
    gw = nc.dram_tensor("gw", [D, 32], F32, kind="ExternalInput").ap()
    gbias = nc.dram_tensor("gbias", [1, 32], F32, kind="ExternalInput").ap()
    out_d = nc.dram_tensor("out", [D, T], F32, kind="ExternalOutput").ap()
    combT_d = {t: nc.dram_tensor(f"combT_d{t}", [32, T], BF16, kind="Internal").ap()
               for t in ("f1", "f2")}
    wbf_d = {m: nc.dram_tensor(f"wbf_{m}", [NEXP, 128, 4096], BF16, kind="Internal").ap()
             for m in ("wg", "wu", "wd")}
    dbg_d = None
    dbg_bf = None
    dump = _BUILD_DUMP[0]
    if dump is not None:
        dbg_d = nc.dram_tensor("dbg", [128, 4 * 16384], F32, kind="ExternalOutput").ap()
        dbg_bf = nc.dram_tensor("dbgb", [128, 16384], BF16, kind="ExternalOutput").ap()

    s = Sched()
    est = contextlib.ExitStack()

    with est:
        x8 = est.enter_context(nc.sbuf_tensor("x8", [128, C * T], F32))
        hln = est.enter_context(nc.sbuf_tensor("hln", [128, C * T], BF16))
        id_bf = est.enter_context(nc.sbuf_tensor("id_bf", [128, 128], BF16))
        id_f32 = est.enter_context(nc.sbuf_tensor("id_f32", [128, 128], F32))
        cmask = est.enter_context(nc.sbuf_tensor("cmask", [128, 128], F32))
        ones_col_bf = est.enter_context(nc.sbuf_tensor("ones_col_bf", [128, 1], BF16))
        ones_row_bf = est.enter_context(nc.sbuf_tensor("ones_row_bf", [1, 128], BF16))
        ones_row_f = est.enter_context(nc.sbuf_tensor("ones_row_f", [1, 128], F32))
        gwT = est.enter_context(nc.sbuf_tensor("gwT", [128, C * 32], F32))
        bias32 = est.enter_context(nc.sbuf_tensor("bias32", [128, 32], F32))
        eps1 = est.enter_context(nc.sbuf_tensor("eps1", [1, 1], F32))

        XR = [[f"x.{c}.{t}" for t in range(TC)] for c in range(C)]
        HR = [[f"h.{c}.{t}" for t in range(TC)] for c in range(C)]
        allx = [k for row in XR for k in row]

        def x_view(c, tc, wdt=TW):
            return x8[:, c * T + tc * TW: c * T + tc * TW + wdt]

        def h_view(c, tc, wdt=TW):
            return hln[:, c * T + tc * TW: c * T + tc * TW + wdt]

        s.dma("act", lambda e: e.dma_start(
            out=x8[:, :].rearrange("p (c t) -> p c t", c=C),
            in_=x_in.rearrange("(c p) t -> p c t", p=128)),
            writes=allx, tag="xload")
        s.dma("act", lambda e: e.dma_start(
            out=gwT[:, :].rearrange("p (c e) -> p c e", c=C),
            in_=gw.rearrange("(c p) e -> p c e", p=128)),
            writes=["gwT"], tag="gwload")
        s.dma("act", lambda e: e.dma_start(
            out=bias32[:, :],
            in_=bass.AP(gbias.tensor, 0, [[0, 128], [1, 32]])),
            writes=["bias32"], tag="gbias")

        s.op("pool", lambda e: e.memset(id_bf[:, :], 0.0), writes=["id_bf"])
        s.op("pool", lambda e: e.affine_select(
            out=id_bf[:, :], in_=id_bf[:, :], compare_op=ALU.not_equal,
            fill=1.0, base=0, pattern=[[-1, 128]], channel_multiplier=1),
            reads=["id_bf"], writes=["id_bf"])
        s.op("pool", lambda e: e.memset(id_f32[:, :], 0.0), writes=["id_f32"])
        s.op("pool", lambda e: e.affine_select(
            out=id_f32[:, :], in_=id_f32[:, :], compare_op=ALU.not_equal,
            fill=1.0, base=0, pattern=[[-1, 128]], channel_multiplier=1),
            reads=["id_f32"], writes=["id_f32"])
        s.op("pool", lambda e: e.memset(cmask[:, :], 0.0), writes=["cmask"])
        s.op("pool", lambda e: e.affine_select(
            out=cmask[:, :], in_=cmask[:, :], compare_op=ALU.is_ge,
            fill=-1e9, base=0, pattern=[[-1, 128]], channel_multiplier=1),
            reads=["cmask"], writes=["cmask"])
        s.op("pool", lambda e: e.memset(ones_col_bf[:, :], 1.0), writes=["ones_col"])
        s.op("pool", lambda e: e.memset(ones_row_bf[:, :], 1.0), writes=["ones_row"])
        s.op("pool", lambda e: e.memset(ones_row_f[:, :], 1.0), writes=["ones_row_f"])
        s.op("pool", lambda e: e.memset(eps1[:, :], 1e-5), writes=["eps1"])

        def layernorm(tag, gate, combine_sb=None, combT_sb=None):
            with contextlib.ExitStack() as sc:
                xb = sc.enter_context(nc.sbuf_tensor(f"xb{tag}", [128, 2 * T], BF16))
                sq = sc.enter_context(nc.sbuf_tensor(f"sq{tag}", [128, 2 * T], BF16))
                mrow = sc.enter_context(nc.sbuf_tensor(f"mrow{tag}", [1, T], F32))
                trow = sc.enter_context(nc.sbuf_tensor(f"trow{tag}", [1, T], F32))
                rrow = sc.enter_context(nc.sbuf_tensor(f"rrow{tag}", [1, T], F32))
                xl = sc.enter_context(nc.sbuf_tensor(f"xl{tag}", [128, 2 * TW], F32))
                if gate:
                    sem_ = sc.enter_context(nc.sbuf_tensor(f"sem{tag}", [32, T], F32))

                with contextlib.ExitStack() as pc:
                    px = pc.enter_context(nc.psum_tensor(f"px{tag}", [1, T], F32))
                    pq = pc.enter_context(nc.psum_tensor(f"pq{tag}", [1, T], F32))
                    for c in range(C):
                        sl = c % 2
                        s.op("pool", lambda e, c=c, sl=sl: e.tensor_copy(
                            out=xb[:, sl * T:(sl + 1) * T], in_=x8[:, c * T:(c + 1) * T]),
                            reads=XR[c], writes=[f"lnxb.{sl}"], tag=f"lncast{c}")
                        s.op("act", lambda e, sl=sl: e.activation(
                            out=sq[:, sl * T:(sl + 1) * T], in_=xb[:, sl * T:(sl + 1) * T],
                            func=AF.Square),
                            reads=[f"lnxb.{sl}"], writes=[f"lnsq.{sl}"])
                        for tc in range(TC):
                            s.op("pe", lambda e, c=c, sl=sl, tc=tc: e.matmul(
                                px[0:1, tc * TW:(tc + 1) * TW], ones_col_bf[:, :],
                                xb[:, sl * T + tc * TW: sl * T + (tc + 1) * TW],
                                start=(c == 0), stop=(c == C - 1)),
                                reads=[f"lnxb.{sl}", "ones_col"], writes=[f"lnpx.{tc}"])
                            s.op("pe", lambda e, c=c, sl=sl, tc=tc: e.matmul(
                                pq[0:1, tc * TW:(tc + 1) * TW], ones_col_bf[:, :],
                                sq[:, sl * T + tc * TW: sl * T + (tc + 1) * TW],
                                start=(c == 0), stop=(c == C - 1)),
                                reads=[f"lnsq.{sl}", "ones_col"], writes=[f"lnpq.{tc}"])
                    pxr = [f"lnpx.{t}" for t in range(TC)]
                    pqr = [f"lnpq.{t}" for t in range(TC)]
                    s.op("dve", lambda e: e.tensor_scalar_mul(
                        mrow[:, :], px[0:1, :], 1.0 / D), reads=pxr, writes=["mrow"])
                    s.op("dve", lambda e: e.tensor_mul(
                        trow[:, :], mrow[:, :], mrow[:, :]), reads=["mrow"], writes=["trow"])
                    s.op("dve", lambda e: e.scalar_tensor_tensor(
                        out=trow[:, :], in0=pq[0:1, :], scalar=1.0 / D, in1=trow[:, :],
                        op0=ALU.mult, op1=ALU.subtract),
                        reads=pqr + ["trow"], writes=["trow"])
                    s.op("act", lambda e: e.activation(
                        out=trow[:, :], in_=trow[:, :], func=AF.Ln, bias=eps1[:, :]),
                        reads=["trow", "eps1"], writes=["trow"])
                    s.op("act", lambda e: e.activation(
                        out=rrow[:, :], in_=trow[:, :], func=AF.Exp, scale=-0.5),
                        reads=["trow"], writes=["rrow"])
                s.barrier()

                with contextlib.ExitStack() as pc:
                    pmb = pc.enter_context(nc.psum_tensor(f"pmb{tag}", [128, TW], F32))
                    prb = pc.enter_context(nc.psum_tensor(f"prb{tag}", [128, TW], F32))
                    if gate:
                        pg = pc.enter_context(nc.psum_tensor(f"pg{tag}", [32, T], F32))
                    for tc in range(TC):
                        s.op("pe", lambda e, tc=tc: e.matmul(
                            pmb[:, :], ones_row_f[:, :], mrow[0:1, tc * TW:(tc + 1) * TW],
                            start=True, stop=True),
                            reads=["mrow", "ones_row_f"], writes=["pmb"])
                        s.op("pe", lambda e, tc=tc: e.matmul(
                            prb[:, :], ones_row_f[:, :], rrow[0:1, tc * TW:(tc + 1) * TW],
                            start=True, stop=True),
                            reads=["rrow", "ones_row_f"], writes=["prb"])
                        for c in range(C):
                            sl = c % 2
                            s.op("dve", lambda e, c=c, tc=tc, sl=sl: e.tensor_sub(
                                xl[:, sl * TW:(sl + 1) * TW], x_view(c, tc), pmb[:, :]),
                                reads=XR[c][tc:tc + 1] + ["pmb"], writes=[f"lnxl.{sl}"])
                            s.op("dve", lambda e, sl=sl: e.tensor_mul(
                                xl[:, sl * TW:(sl + 1) * TW],
                                xl[:, sl * TW:(sl + 1) * TW], prb[:, :]),
                                reads=[f"lnxl.{sl}", "prb"], writes=[f"lnxl.{sl}"])
                            s.op("act", lambda e, c=c, tc=tc, sl=sl: e.activation(
                                out=h_view(c, tc), in_=xl[:, sl * TW:(sl + 1) * TW],
                                func=AF.Copy),
                                reads=[f"lnxl.{sl}"], writes=[HR[c][tc]])
                            if gate:
                                s.op("pe", lambda e, c=c, tc=tc, sl=sl: e.matmul(
                                    pg[0:32, tc * TW:(tc + 1) * TW],
                                    gwT[:, c * 32:(c + 1) * 32],
                                    xl[:, sl * TW:(sl + 1) * TW],
                                    start=(c == 0), stop=(c == C - 1)),
                                    reads=[f"lnxl.{sl}", "gwT"], writes=[f"lnpg.{tc}"])
                    if gate:
                        for tc in range(TC):
                            s.op("act", lambda e, tc=tc: e.activation(
                                out=sem_[0:32, tc * TW:(tc + 1) * TW],
                                in_=pg[0:32, tc * TW:(tc + 1) * TW], func=AF.Sigmoid),
                                reads=[f"lnpg.{tc}"], writes=[f"sem.{tc}"])
                s.barrier()

                if not gate:
                    return

                with contextlib.ExitStack() as gc:
                    sc_all = gc.enter_context(nc.sbuf_tensor(f"scall{tag}", [128, 512], F32))
                    scb = gc.enter_context(nc.sbuf_tensor(f"scb{tag}", [128, 512], F32))
                    u1 = gc.enter_context(nc.sbuf_tensor(f"u1{tag}", [128, 128], F32))
                    u2 = gc.enter_context(nc.sbuf_tensor(f"u2{tag}", [128, 128], F32))
                    gsc = gc.enter_context(nc.sbuf_tensor(f"gsc{tag}", [128, 128], F32))
                    g8 = gc.enter_context(nc.sbuf_tensor(f"g8{tag}", [128, 8], F32))
                    gmask = gc.enter_context(nc.sbuf_tensor(f"gmask{tag}", [128, 128], F32))
                    smask = gc.enter_context(nc.sbuf_tensor(f"smask{tag}", [128, 512], F32))
                    masked = gc.enter_context(nc.sbuf_tensor(f"masked{tag}", [128, 512], F32))
                    m8 = gc.enter_context(nc.sbuf_tensor(f"m8{tag}", [128, 8], F32))
                    selw = gc.enter_context(nc.sbuf_tensor(f"selw{tag}", [128, 512], F32))
                    dnm = gc.enter_context(nc.sbuf_tensor(f"dnm{tag}", [128, 16], F32))
                    with contextlib.ExitStack() as pc:
                        pt0 = pc.enter_context(nc.psum_tensor(f"pt0{tag}", [128, 32], F32))
                        pt1 = pc.enter_context(nc.psum_tensor(f"pt1{tag}", [128, 32], F32))
                        ptc0 = pc.enter_context(nc.psum_tensor(f"ptc0{tag}", [32, 128], F32))
                        ptc1 = pc.enter_context(nc.psum_tensor(f"ptc1{tag}", [32, 128], F32))
                        for t16 in range(16):
                            pt = (pt0, pt1)[t16 % 2]
                            key = f"lnpt.{t16 % 2}"
                            s.op("pe", lambda e, t16=t16, pt=pt: e.transpose(
                                pt[:, :], sem_[0:32, t16 * 128:(t16 + 1) * 128],
                                id_f32[0:32, 0:32]),
                                reads=[f"sem.{t16 // 4}", "id_f32"], writes=[key])
                            s.op("act", lambda e, t16=t16, pt=pt: e.activation(
                                out=sc_all[:, t16 * 32:(t16 + 1) * 32], in_=pt[:, :],
                                func=AF.Copy),
                                reads=[key], writes=[f"scall.{t16}"])
                        scr = [f"scall.{t}" for t in range(16)]
                        for t16 in range(16):
                            s.op("dve", lambda e, t16=t16: e.tensor_add(
                                scb[:, t16 * 32:(t16 + 1) * 32],
                                sc_all[:, t16 * 32:(t16 + 1) * 32], bias32[:, :]),
                                reads=[f"scall.{t16}", "bias32"], writes=["scb"])

                        def v4(t, j):
                            return bass.AP(t, j, [[512, 128], [32, 16], [4, 8]])

                        pairs = [(0, 1), (0, 2), (0, 3), (1, 2), (1, 3), (2, 3)]
                        for pi, (a, b) in enumerate(pairs):
                            dst = u1 if pi == 0 else u2
                            dk = "u1" if pi == 0 else "u2"
                            s.op("dve", lambda e, a=a, b=b, dst=dst: e.tensor_add(
                                bass.AP(dst, 0, [[128, 128], [8, 16], [1, 8]]),
                                v4(scb, a), v4(scb, b)),
                                reads=["scb"], writes=[dk])
                            if pi == 0:
                                continue
                            tgt = u1 if pi < 5 else gsc
                            tk = "u1" if pi < 5 else "gsc"
                            s.op("dve", lambda e, tgt=tgt: e.tensor_max(
                                tgt[:, :], u1[:, :], u2[:, :]),
                                reads=["u1", "u2"], writes=[tk])
                        for t16 in range(16):
                            s.op("dve", lambda e, t16=t16: e.max(
                                out=g8[:, :], in_=gsc[:, t16 * 8:(t16 + 1) * 8]),
                                reads=["gsc"], writes=["g8"])
                            s.op("dve", lambda e, t16=t16: e.tensor_scalar(
                                out=gmask[:, t16 * 8:(t16 + 1) * 8],
                                in0=gsc[:, t16 * 8:(t16 + 1) * 8],
                                scalar1=g8[:, 3:4], scalar2=None, op0=ALU.is_ge),
                                reads=["gsc", "g8"], writes=["gmask"])
                        for j in range(4):
                            s.op("dve", lambda e, j=j: e.tensor_copy(
                                out=v4(smask, j),
                                in_=bass.AP(gmask, 0, [[128, 128], [8, 16], [1, 8]])),
                                reads=["gmask"], writes=["smask"])
                        s.op("dve", lambda e: e.tensor_mul(
                            masked[:, :], scb[:, :], smask[:, :]),
                            reads=["scb", "smask"], writes=["maskedg"])
                        for t16 in range(16):
                            s.op("dve", lambda e, t16=t16: e.max(
                                out=m8[:, :], in_=masked[:, t16 * 32:(t16 + 1) * 32]),
                                reads=["maskedg"], writes=["m8"])
                            s.op("dve", lambda e, t16=t16: e.tensor_scalar(
                                out=selw[:, t16 * 32:(t16 + 1) * 32],
                                in0=masked[:, t16 * 32:(t16 + 1) * 32],
                                scalar1=m8[:, 7:8], scalar2=None, op0=ALU.is_ge),
                                reads=["maskedg", "m8"], writes=["selw"])
                        s.op("dve", lambda e: e.tensor_mul(
                            selw[:, :], selw[:, :], sc_all[:, :]),
                            reads=["selw"] + scr, writes=["selw"])
                        for t16 in range(16):
                            s.op("dve", lambda e, t16=t16: e.reduce_sum(
                                out=dnm[:, t16:t16 + 1],
                                in_=selw[:, t16 * 32:(t16 + 1) * 32], axis=AX.X),
                                reads=["selw"], writes=["dnm"])
                        s.op("dve", lambda e: e.tensor_scalar_add(
                            dnm[:, :], dnm[:, :], 1e-20), reads=["dnm"], writes=["dnm"])
                        s.op("dve", lambda e: e.reciprocal(
                            dnm[:, :], dnm[:, :]), reads=["dnm"], writes=["dnm"])
                        for t16 in range(16):
                            s.op("dve", lambda e, t16=t16: e.tensor_scalar(
                                out=combine_sb[:, t16 * 32:(t16 + 1) * 32],
                                in0=selw[:, t16 * 32:(t16 + 1) * 32],
                                scalar1=dnm[:, t16:t16 + 1], scalar2=2.5,
                                op0=ALU.mult, op1=ALU.mult),
                                reads=["selw", "dnm"], writes=[f"comb.{t16}"])
                            ptc = (ptc0, ptc1)[t16 % 2]
                            pk = f"lnptc.{t16 % 2}"
                            s.op("pe", lambda e, t16=t16, ptc=ptc: e.transpose(
                                ptc[:, :], combine_sb[:, t16 * 32:(t16 + 1) * 32],
                                id_f32[:, :]),
                                reads=[f"comb.{t16}", "id_f32"], writes=[pk])
                            s.op("act", lambda e, t16=t16, ptc=ptc: e.activation(
                                out=combT_sb[0:32, t16 * 128:(t16 + 1) * 128],
                                in_=ptc[:, :], func=AF.Copy),
                                reads=[pk], writes=[f"combT.{t16 // 4}"])
                s.dma("sp", lambda e: e.dma_start(
                    out=combT_d[tag][:, :], in_=combT_sb[:, :]),
                    reads=[f"combT.{t}" for t in range(4)], writes=["combTd"],
                    tag="combTout")
                if _BUILD_DUMP[0] == "comb" and tag == "f1":
                    s.dma("sp", lambda e: e.dma_start(
                        out=dbg_d[:, 0:512], in_=combine_sb[:, :]),
                        reads=[f"comb.{t}" for t in range(16)], tag="dumpcomb")
                    s.dma("sp", lambda e: e.dma_start(
                        out=dbg_d[0:32, 512:512 + T], in_=sem_[:, :]),
                        reads=[f"sem.{t}" for t in range(4)], tag="dumpsem")
                s.barrier()

        def ffn_call(tag):
            with contextlib.ExitStack() as fc:
                combine_sb = fc.enter_context(nc.sbuf_tensor(f"combine{tag}", [128, 512], F32))
                combT_sb = fc.enter_context(nc.sbuf_tensor(f"combT{tag}", [32, T], BF16))

                layernorm(tag, gate=True, combine_sb=combine_sb, combT_sb=combT_sb)

                with contextlib.ExitStack() as ec:
                    wst = ec.enter_context(nc.sbuf_tensor(f"wst{tag}", [128, 2 * 2048], F32))
                    wg = ec.enter_context(nc.sbuf_tensor(f"wg{tag}", [128, 2 * 4096], BF16))
                    wu = ec.enter_context(nc.sbuf_tensor(f"wu{tag}", [128, 2 * 4096], BF16))
                    wd = ec.enter_context(nc.sbuf_tensor(f"wd{tag}", [128, 2 * 4096], BF16))
                    zb = ec.enter_context(nc.sbuf_tensor(f"zb{tag}", [128, 2 * 2048], BF16))
                    zt = ec.enter_context(nc.sbuf_tensor(f"zt{tag}", [128, 2 * TW], BF16))
                    gs = ec.enter_context(nc.sbuf_tensor(f"gs{tag}", [128, 2 * TW], F32))
                    cb = ec.enter_context(nc.sbuf_tensor(f"cb{tag}", [128, 2 * T], BF16))
                    with contextlib.ExitStack() as pc:
                        pgA = pc.enter_context(nc.psum_tensor(f"pgA{tag}", [128, TW], F32))
                        pgB = pc.enter_context(nc.psum_tensor(f"pgB{tag}", [128, TW], F32))
                        puA = pc.enter_context(nc.psum_tensor(f"puA{tag}", [128, TW], F32))
                        puB = pc.enter_context(nc.psum_tensor(f"puB{tag}", [128, TW], F32))
                        pd = [pc.enter_context(nc.psum_tensor(f"pd{j}{tag}", [128, 2 * TW], F32))
                              for j in range(2)]

                        nstage = [0]

                        def load_expert(e_):
                            sl = e_ % 2
                            if e_ < 32:
                                s.dma("act", lambda e, e_=e_, sl=sl: e.dma_start(
                                    out=cb[:, sl * T:(sl + 1) * T],
                                    in_=bass.AP(combT_d[tag].tensor, e_ * T,
                                                [[0, 128], [1, T]])),
                                    reads=["combTd"], writes=[f"cb.{sl}"],
                                    tag=f"cbdma{e_}")
                            if tag == "f2":
                                for dstbuf, dk in ((wg, "wg"), (wu, "wu"), (wd, "wd")):
                                    s.dma("sp", lambda e, dstbuf=dstbuf, dk=dk, e_=e_, sl=sl:
                                          e.dma_start(
                                        out=dstbuf[:, sl * 4096:(sl + 1) * 4096],
                                        in_=wbf_d[dk][e_]),
                                        writes=[f"{dk}.{sl}"], tag=f"wbfin{e_}")
                                return
                            for src, dstbuf, dk, scale, nch in (
                                    (gatew, wg, "wg", None, 8),
                                    (upw, wu, "wu", None, 8),
                                    (downw, wd, "wd", 0.5, 4)):
                                for half in range(2):
                                    ss = nstage[0] % 2
                                    nstage[0] += 1
                                    hc2 = nch // 2
                                    s.dma("sp", lambda e, src=src, e_=e_, half=half, ss=ss, hc2=hc2:
                                          e.dma_start(
                                        out=wst[:, ss * 2048:(ss + 1) * 2048].rearrange(
                                            "p (c i) -> p c i", c=hc2),
                                        in_=src[e_].rearrange(
                                            "(c p) i -> p c i", p=128)[:, half * hc2:(half + 1) * hc2, :]),
                                        writes=[f"wst.{ss}"], tag=f"wdma{e_}")
                                    if scale is None:
                                        s.op("dve", lambda e, dstbuf=dstbuf, sl=sl, half=half, ss=ss:
                                             e.tensor_copy(
                                            out=dstbuf[:, sl * 4096 + half * 2048:
                                                       sl * 4096 + (half + 1) * 2048],
                                            in_=wst[:, ss * 2048:(ss + 1) * 2048]),
                                            reads=[f"wst.{ss}"], writes=[f"{dk}.{sl}"])
                                    else:
                                        s.op("dve", lambda e, dstbuf=dstbuf, sl=sl, half=half, ss=ss:
                                             e.tensor_scalar_mul(
                                            dstbuf[:, sl * 4096 + half * 2048:
                                                   sl * 4096 + (half + 1) * 2048],
                                            wst[:, ss * 2048:(ss + 1) * 2048], 0.5),
                                            reads=[f"wst.{ss}"], writes=[f"{dk}.{sl}"])
                                s.dma("act", lambda e, dstbuf=dstbuf, dk=dk, e_=e_, sl=sl:
                                      e.dma_start(
                                    out=wbf_d[dk][e_],
                                    in_=dstbuf[:, sl * 4096:(sl + 1) * 4096]),
                                    reads=[f"{dk}.{sl}"], tag=f"wbfout{e_}")

                        def emit_gu(e_, tc):
                            sl = e_ % 2
                            routed = e_ < 32
                            if True:
                                ztc = (4 * e_ + tc) % 2
                                for ic in range(NI):
                                    par = ic % 2
                                    pgp = (pgA, pgB)[par]
                                    pup = (puA, puB)[par]
                                    for kc in range(C):
                                        s.op("pe", lambda e, sl=sl, tc=tc, ic=ic, kc=kc, pgp=pgp:
                                             e.matmul(
                                            pgp[:, :],
                                            wg[:, sl * 4096 + kc * 512 + ic * 128:
                                               sl * 4096 + kc * 512 + (ic + 1) * 128],
                                            h_view(kc, tc),
                                            start=(kc == 0), stop=(kc == C - 1)),
                                            reads=[f"wg.{sl}", HR[kc][tc]],
                                            writes=[f"pg.{par}"])
                                    s.op("act", lambda e, par=par, pgp=pgp: e.activation(
                                        out=gs[:, par * TW:(par + 1) * TW],
                                        in_=pgp[:, :], func=AF.Silu),
                                        reads=[f"pg.{par}"], writes=[f"gs.{par}"])
                                    for kc in range(C):
                                        s.op("pe", lambda e, sl=sl, tc=tc, ic=ic, kc=kc, pup=pup:
                                             e.matmul(
                                            pup[:, :],
                                            wu[:, sl * 4096 + kc * 512 + ic * 128:
                                               sl * 4096 + kc * 512 + (ic + 1) * 128],
                                            h_view(kc, tc),
                                            start=(kc == 0), stop=(kc == C - 1)),
                                            reads=[f"wu.{sl}", HR[kc][tc]],
                                            writes=[f"pu.{par}"])
                                    if routed:
                                        s.op("dve", lambda e, par=par, pup=pup: e.tensor_mul(
                                            zt[:, par * TW:(par + 1) * TW],
                                            gs[:, par * TW:(par + 1) * TW], pup[:, :]),
                                            reads=[f"gs.{par}", f"pu.{par}"],
                                            writes=[f"zt.{par}"])
                                        s.op("dve", lambda e, par=par, ztc=ztc, ic=ic, tc=tc, sl=sl:
                                             e.tensor_mul(
                                            zb[:, ztc * 2048 + ic * TW:
                                               ztc * 2048 + (ic + 1) * TW],
                                            zt[:, par * TW:(par + 1) * TW],
                                            cb[:, sl * T + tc * TW: sl * T + (tc + 1) * TW]),
                                            reads=[f"zt.{par}", f"cb.{sl}"],
                                            writes=[f"z.{ztc}.{ic}"])
                                    else:
                                        s.op("dve", lambda e, par=par, ztc=ztc, ic=ic, pup=pup:
                                             e.tensor_mul(
                                            zb[:, ztc * 2048 + ic * TW:
                                               ztc * 2048 + (ic + 1) * TW],
                                            gs[:, par * TW:(par + 1) * TW], pup[:, :]),
                                            reads=[f"gs.{par}", f"pu.{par}"],
                                            writes=[f"z.{ztc}.{ic}"])

                        def x_pair_view(hp, tc):
                            return bass.AP(x8, (2 * hp) * T + tc * TW,
                                           [[C * T, 128], [T, 2], [1, TW]])

                        def emit_down(e_, tc):
                            sl = e_ % 2
                            ztc = (4 * e_ + tc) % 2
                            for hp in range(4):
                                b = hp % 2
                                for half in range(2):
                                    hc = 2 * hp + half
                                    for ic in range(NI):
                                        s.op("pe", lambda e, sl=sl, hc=hc, ic=ic, ztc=ztc, b=b, half=half:
                                             e.matmul(
                                            pd[b][:, half * TW:(half + 1) * TW],
                                            wd[:, sl * 4096 + ic * 1024 + hc * 128:
                                               sl * 4096 + ic * 1024 + (hc + 1) * 128],
                                            zb[:, ztc * 2048 + ic * TW:
                                               ztc * 2048 + (ic + 1) * TW],
                                            start=(ic == 0), stop=(ic == NI - 1)),
                                            reads=[f"wd.{sl}", f"z.{ztc}.{ic}"],
                                            writes=[f"pd.{b}"])
                                s.op("dve", lambda e, hp=hp, tc=tc, b=b: e.tensor_add(
                                    x_pair_view(hp, tc),
                                    pd[b][:, :].rearrange("p (h t) -> p h t", h=2),
                                    x_pair_view(hp, tc)),
                                    reads=[f"pd.{b}", XR[2 * hp][tc], XR[2 * hp + 1][tc]],
                                    writes=[XR[2 * hp][tc], XR[2 * hp + 1][tc]])

                        load_expert(0)
                        load_expert(1)
                        seq = [(e_, tc) for e_ in range(NEXP) for tc in range(TC)]
                        LAG = 2

                        def emit_down_and_load(j):
                            d_e, d_tc = seq[j]
                            emit_down(d_e, d_tc)
                            if d_tc == TC - 1 and d_e + 2 < NEXP:
                                load_expert(d_e + 2)

                        for i, (e_, tc) in enumerate(seq):
                            if i >= LAG:
                                emit_down_and_load(i - LAG)
                            emit_gu(e_, tc)
                        for j in range(len(seq) - LAG, len(seq)):
                            emit_down_and_load(j)
            s.barrier()

        def attn_call(tag):
            layernorm(tag, gate=False)
            with contextlib.ExitStack() as ac:
                ao = ac.enter_context(nc.sbuf_tensor(f"ao{tag}", [128, C * T], BF16))
                qkvb = ac.enter_context(nc.sbuf_tensor(f"qkvb{tag}", [128, 3 * T], BF16))
                wqst = ac.enter_context(nc.sbuf_tensor(f"wqst{tag}", [128, 2 * 1024], F32))
                wqb = ac.enter_context(nc.sbuf_tensor(f"wqb{tag}", [128, 2 * 3 * 1024], BF16))
                vt = ac.enter_context(nc.sbuf_tensor(f"vt{tag}", [128, 16 * 128], BF16))
                pP = ac.enter_context(nc.sbuf_tensor(f"pP{tag}", [128, 2 * T], BF16))
                ptsb = ac.enter_context(nc.sbuf_tensor(f"ptsb{tag}", [128, 2 * 128], BF16))
                mx4 = ac.enter_context(nc.sbuf_tensor(f"mx4{tag}", [128, 8], F32))
                mbias = ac.enter_context(nc.sbuf_tensor(f"mbias{tag}", [128, 2], F32))
                rs4 = ac.enter_context(nc.sbuf_tensor(f"rs4{tag}", [128, 8], F32))
                rsum = ac.enter_context(nc.sbuf_tensor(f"rsum{tag}", [128, 2], F32))

                with contextlib.ExitStack() as pc:
                    pS = pc.enter_context(nc.psum_tensor(f"pS{tag}", [128, T], F32))
                    pPT = [pc.enter_context(nc.psum_tensor(f"pPT{j}{tag}", [128, 128], BF16))
                           for j in range(2)]
                    pO = [pc.enter_context(nc.psum_tensor(f"pO{j}{tag}", [128, 128], F32))
                          for j in range(2)]

                    nqst = [0]

                    def load_qkvw(c):
                        sl = c % 2
                        for m in range(3):
                            ss = nqst[0] % 2
                            nqst[0] += 1
                            oc = m * 8 + c
                            s.dma("sp", lambda e, oc=oc, ss=ss: e.dma_start(
                                out=wqst[:, ss * 1024:(ss + 1) * 1024].rearrange(
                                    "p (c m) -> p c m", c=8),
                                in_=qkvw[oc].rearrange("(c p) m -> p c m", p=128)),
                                writes=[f"wqst.{ss}"], tag=f"qkvw{oc}")
                            s.op("pool", lambda e, m=m, sl=sl, ss=ss: e.tensor_copy(
                                out=wqb[:, sl * 3072 + m * 1024:sl * 3072 + (m + 1) * 1024],
                                in_=wqst[:, ss * 1024:(ss + 1) * 1024]),
                                reads=[f"wqst.{ss}"], writes=[f"wqb.{sl}.{m}"])

                    def chunk(c):
                        sl = c % 2
                        for m in range(3):
                            for tc in range(TC):
                                for kc in range(C):
                                    s.op("pe", lambda e, m=m, tc=tc, kc=kc, sl=sl: e.matmul(
                                        pS[:, tc * TW:(tc + 1) * TW],
                                        wqb[:, sl * 3072 + m * 1024 + kc * 128:
                                            sl * 3072 + m * 1024 + (kc + 1) * 128],
                                        h_view(kc, tc),
                                        start=(kc == 0), stop=(kc == C - 1)),
                                        reads=[f"wqb.{sl}.{m}", HR[kc][tc]],
                                        writes=[f"pS.{tc}"])
                                s.op("act", lambda e, m=m, tc=tc: e.activation(
                                    out=qkvb[:, m * T + tc * TW: m * T + (tc + 1) * TW],
                                    in_=pS[:, tc * TW:(tc + 1) * TW], func=AF.Copy),
                                    reads=[f"pS.{tc}"], writes=[f"qkv.{m}.{tc}"])
                        for kc in range(16):
                            p3 = pPT[kc % 2]
                            k3 = f"pPT.{kc % 2}"
                            s.op("pe", lambda e, kc=kc, p3=p3: e.transpose(
                                p3[:, :], qkvb[:, 2 * T + kc * 128:2 * T + (kc + 1) * 128],
                                id_bf[:, :]),
                                reads=[f"qkv.2.{kc // 4}", "id_bf"], writes=[k3])
                            s.op("dve", lambda e, kc=kc, p3=p3: e.tensor_copy(
                                out=vt[:, kc * 128:(kc + 1) * 128], in_=p3[:, :]),
                                reads=[k3], writes=[f"vt.{kc}"])

                        def qk_pre(par, qt):
                            qoff = 64 * par
                            klen = (qt + 1) * 128
                            nk = (klen + TW - 1) // TW
                            jd = qt // 4
                            for j in range(nk):
                                w = min(TW, klen - j * TW)
                                s.op("pe", lambda e, qt=qt, j=j, w=w, qoff=qoff:
                                     e.matmul(
                                    pS[:, j * TW: j * TW + w],
                                    qkvb[qoff:qoff + 64, qt * 128:(qt + 1) * 128],
                                    qkvb[qoff:qoff + 64, T + j * TW: T + j * TW + w],
                                    start=True, stop=True),
                                    reads=[f"qkv.0.{qt // 4}", f"qkv.1.{j}"],
                                    writes=[f"pS.{j}"])
                                if j == jd:
                                    s.op("dve", lambda e, qt=qt: e.tensor_add(
                                        pS[:, qt * 128:(qt + 1) * 128],
                                        pS[:, qt * 128:(qt + 1) * 128], cmask[:, :]),
                                        reads=[f"pS.{jd}", "cmask"], writes=[f"pS.{jd}"])
                                s.op("dve", lambda e, j=j, w=w, par=par: e.reduce_max(
                                    out=mx4[:, par * 4 + j:par * 4 + j + 1],
                                    in_=pS[:, j * TW: j * TW + w], axis=AX.X),
                                    reads=[f"pS.{j}"], writes=[f"mx4.{par}"])
                            s.op("dve", lambda e, nk=nk, par=par: e.reduce_max(
                                out=mbias[:, par:par + 1], in_=mx4[:, par * 4:par * 4 + nk],
                                axis=AX.X),
                                reads=[f"mx4.{par}"], writes=[f"mbias.{par}"])
                            s.op("dve", lambda e, par=par: e.tensor_scalar_mul(
                                mbias[:, par:par + 1], mbias[:, par:par + 1], -0.125),
                                reads=[f"mbias.{par}"], writes=[f"mbias.{par}"])
                            for j in range(nk):
                                w = min(TW, klen - j * TW)
                                s.op("act", lambda e, j=j, w=w, par=par: e.activation(
                                    out=pP[:, par * T + j * TW: par * T + j * TW + w],
                                    in_=pS[:, j * TW: j * TW + w], func=AF.Exp,
                                    scale=0.125, bias=mbias[:, par:par + 1],
                                    accum_out=rs4[:, par * 4 + j:par * 4 + j + 1]),
                                    reads=[f"pS.{j}", f"mbias.{par}"],
                                    writes=[f"P.{par}.{j}", f"rs4.{par}.{j}"])

                        def smax_post(par, qt):
                            klen = (qt + 1) * 128
                            nk = (klen + TW - 1) // TW
                            s.op("dve", lambda e, nk=nk, par=par: e.reduce_sum(
                                out=rsum[:, par:par + 1], in_=rs4[:, par * 4:par * 4 + nk],
                                axis=AX.X),
                                reads=[f"rs4.{par}.{j}" for j in range(nk)],
                                writes=[f"rsum.{par}"])
                            s.op("dve", lambda e, par=par: e.reciprocal(
                                rsum[:, par:par + 1], rsum[:, par:par + 1]),
                                reads=[f"rsum.{par}"], writes=[f"rsum.{par}"])
                            pkeys = [f"P.{par}.{j}" for j in range(nk)]
                            s.op("dve", lambda e, klen=klen, par=par: e.tensor_scalar_mul(
                                pP[:, par * T:par * T + klen],
                                pP[:, par * T:par * T + klen], rsum[:, par:par + 1]),
                                reads=[f"rsum.{par}"] + pkeys, writes=pkeys)

                        def tpv(par, qt, c):
                            qoff = 64 * par
                            tp = (0, 64) if par else None

                            def tr(kc):
                                pp = pPT[kc % 2]
                                ppk = f"pPT.{kc % 2}"
                                s.op("pe", lambda e, kc=kc, pp=pp, par=par: e.transpose(
                                    pp[:, :], pP[:, par * T + kc * 128:
                                                  par * T + (kc + 1) * 128], id_bf[:, :]),
                                    reads=[f"P.{par}.{kc // 4}", "id_bf"], writes=[ppk])
                                s.op("dve", lambda e, kc=kc, pp=pp: e.tensor_copy(
                                    out=ptsb[:, (kc % 2) * 128:(kc % 2 + 1) * 128],
                                    in_=pp[:, :]),
                                    reads=[ppk], writes=[f"ptsb.{kc % 2}"])

                            def pv(kc):
                                s.op("pe", lambda e, kc=kc, qt=qt, qoff=qoff, tp=tp, par=par:
                                     e.matmul(
                                    pO[par][qoff:qoff + 64, 0:128],
                                    vt[:, kc * 128 + qoff: kc * 128 + qoff + 64],
                                    ptsb[:, (kc % 2) * 128:(kc % 2 + 1) * 128],
                                    start=(kc == 0), stop=(kc == qt),
                                    tile_position=tp),
                                    reads=[f"vt.{kc}", f"ptsb.{kc % 2}"],
                                    writes=[f"pO.{par}"])

                            tr(0)
                            for kc in range(1, qt + 1):
                                tr(kc)
                                pv(kc - 1)
                            pv(qt)
                            s.op("act", lambda e, c=c, qt=qt, qoff=qoff, par=par:
                                 e.activation(
                                out=ao[qoff:qoff + 64, c * T + qt * 128:
                                       c * T + (qt + 1) * 128],
                                in_=pO[par][qoff:qoff + 64, 0:128], func=AF.Copy),
                                reads=[f"pO.{par}"], writes=[f"ao.{c}.{qt // 4}"])

                        for qt in range(QT):
                            qk_pre(0, qt)
                            if qt >= 1:
                                tpv(1, qt - 1, c)
                            smax_post(0, qt)
                            qk_pre(1, qt)
                            tpv(0, qt, c)
                            smax_post(1, qt)
                        tpv(1, QT - 1, c)

                    load_qkvw(0)
                    load_qkvw(1)
                    for c in range(C):
                        chunk(c)
                        if c + 2 < C:
                            load_qkvw(c + 2)

                if _BUILD_DUMP[0] == "ao":
                    s.dma("sp", lambda e: e.dma_start(
                        out=dbg_bf[:, 0:C * T], in_=ao[:, :]),
                        reads=[f"ao.{c}.{t}" for c in range(C) for t in range(TC)],
                        tag="dumpao")
                s.barrier()
                with contextlib.ExitStack() as pc:
                    pop = [pc.enter_context(nc.psum_tensor(f"pop{j}{tag}", [128, TW], F32))
                           for j in range(2)]
                    now = [0]
                    for hc in range(C):
                        ss = now[0] % 2
                        now[0] += 1
                        s.dma("sp", lambda e, hc=hc, ss=ss: e.dma_start(
                            out=wqst[:, ss * 1024:(ss + 1) * 1024].rearrange(
                                "p (c m) -> p c m", c=8),
                            in_=outw[hc].rearrange("(c p) m -> p c m", p=128)),
                            writes=[f"wqst.{ss}"], tag=f"outw{hc}")
                        s.op("pool", lambda e, hc=hc, ss=ss: e.tensor_copy(
                            out=wqb[:, (hc % 2) * 3072:(hc % 2) * 3072 + 1024],
                            in_=wqst[:, ss * 1024:(ss + 1) * 1024]),
                            reads=[f"wqst.{ss}"], writes=[f"wqb.{hc % 2}.0"])
                        for tc in range(TC):
                            pb = pop[tc % 2]
                            pbk = f"pop.{tc % 2}"
                            for kc in range(C):
                                s.op("pe", lambda e, hc=hc, tc=tc, kc=kc, pb=pb: e.matmul(
                                    pb[:, :],
                                    wqb[:, (hc % 2) * 3072 + kc * 128:
                                        (hc % 2) * 3072 + (kc + 1) * 128],
                                    ao[:, kc * T + tc * TW: kc * T + (tc + 1) * TW],
                                    start=(kc == 0), stop=(kc == C - 1)),
                                    reads=[f"wqb.{hc % 2}.0", f"ao.{kc}.{tc}"],
                                    writes=[pbk])
                            s.op("dve", lambda e, hc=hc, tc=tc, pb=pb: e.tensor_add(
                                x_view(hc, tc), pb[:, :], x_view(hc, tc)),
                                reads=[pbk, XR[hc][tc]], writes=[XR[hc][tc]])
            s.barrier()

        def dump_sbuf(t, nbytes_elems, rkeys, name):
            s.dma("sp", lambda e: e.dma_start(
                out=dbg_d[:, 0:nbytes_elems], in_=t),
                reads=rkeys, tag=f"dump{name}")
            s.barrier()

        calls = {"f1": lambda: ffn_call("f1"), "at": lambda: attn_call("at"),
                 "f2": lambda: ffn_call("f2")}
        for ph in phases:
            calls[ph]()

        s.dma("sp", lambda e: e.dma_start(
            out=out_d.rearrange("(c p) t -> p c t", p=128),
            in_=x8[:, :].rearrange("p (c t) -> p c t", c=C)),
            reads=allx, tag="store")
        s.barrier()

        with contextlib.ExitStack() as ms:
            eng_sems = {n: ms.enter_context(nc.semaphore(f"sem_{n}"))
                        for n in ("sp", "pe", "act", "dve", "pool")}
            dma_sems = {}
            for q in ("sp", "act"):
                for sl in range(N_DMA_SLOTS):
                    dma_sems[(q, sl)] = ms.enter_context(nc.semaphore(f"dsem_{q}{sl}"))
            with nc.Block() as block:
                s.materialize(nc, block, eng_sems, dma_sems)

    return nc


_NC_CACHE = {}
_BUILD_PHASES = [("f1", "at", "f2")]
_BUILD_DUMP = [None]
_LAST_RESULTS = [None]


def kernel(L0_attn_x, L0_attn_w_qkv, L0_attn_w_out, L0_ffn_gate_w, L0_ffn_gate_bias,
           L0_ffn_w_gate, L0_ffn_w_up, L0_ffn_w_down, L0_ffn_sh_gate, L0_ffn_sh_up,
           L0_ffn_sh_down):
    x = np.asarray(L0_attn_x, dtype=np.float32)
    w_qkv = np.asarray(L0_attn_w_qkv, dtype=np.float32)
    w_out = np.asarray(L0_attn_w_out, dtype=np.float32)
    gate_w = np.asarray(L0_ffn_gate_w, dtype=np.float32)
    gate_bias = np.asarray(L0_ffn_gate_bias, dtype=np.float32)
    w_gate = np.asarray(L0_ffn_w_gate, dtype=np.float32)
    w_up = np.asarray(L0_ffn_w_up, dtype=np.float32)
    w_down = np.asarray(L0_ffn_w_down, dtype=np.float32)
    sh_gate = np.asarray(L0_ffn_sh_gate, dtype=np.float32)
    sh_up = np.asarray(L0_ffn_sh_up, dtype=np.float32)
    sh_down = np.asarray(L0_ffn_sh_down, dtype=np.float32)

    qkvw = np.ascontiguousarray(
        w_qkv.T.reshape(D, 24, 128).transpose(1, 0, 2))
    outw = np.ascontiguousarray(
        w_out.T.reshape(D, 8, 128).transpose(1, 0, 2))
    gT = np.ascontiguousarray(sh_gate.T)
    uT = np.ascontiguousarray(sh_up.T)
    gatew = np.ascontiguousarray(np.concatenate(
        [w_gate.transpose(0, 2, 1), gT[None, :, 0:512], gT[None, :, 512:1024]], axis=0))
    upw = np.ascontiguousarray(np.concatenate(
        [w_up.transpose(0, 2, 1), uT[None, :, 0:512], uT[None, :, 512:1024]], axis=0))
    dT = sh_down.T
    downw = np.ascontiguousarray(np.concatenate(
        [w_down.transpose(0, 2, 1), dT[None, 0:512, :], dT[None, 512:1024, :]], axis=0))
    gw = np.ascontiguousarray(gate_w.T)
    gb = np.ascontiguousarray(gate_bias.reshape(1, 32))

    ph = (tuple(_BUILD_PHASES[0]), _BUILD_DUMP[0])
    if ph not in _NC_CACHE:
        _NC_CACHE[ph] = build_program(phases=ph[0])
    nc = _NC_CACHE[ph]

    in_maps = []
    for b in range(8):
        in_maps.append({
            "x": np.ascontiguousarray(x[b].T),
            "qkvw": qkvw, "outw": outw, "gatew": gatew, "upw": upw,
            "downw": downw, "gw": gw, "gbias": gb,
        })
    res = run_bass_kernel_spmd(nc, in_maps, core_ids=list(range(8)))
    _LAST_RESULTS[0] = res.results
    out = np.stack([np.asarray(r["out"]).T for r in res.results], axis=0)
    return np.ascontiguousarray(out.astype(np.float32))


# version 20
# speedup vs baseline: 1.0412x; 1.0088x over previous
import contextlib
from collections import defaultdict

import numpy as np

import concourse.bass as bass
import concourse.mybir as mybir
from concourse.bass_utils import run_bass_kernel_spmd

F32 = mybir.dt.float32
BF16 = mybir.dt.bfloat16
AF = mybir.ActivationFunctionType
ALU = mybir.AluOpType
AX = mybir.AxisListType

T = 2048
D = 1024
C = 8
TC = 4
TW = 512
NEXP = 34
NI = 4
NH = 16
QT = 16

STRICT = ("dve", "act", "pool")
N_DMA_SLOTS = 6

ENG_SECT = {"sp": "sync", "pe": "tensor", "act": "scalar", "dve": "vector", "pool": "gpsimd"}


class _Op:
    __slots__ = ("oid", "engine", "fn", "deps", "needs_inc", "inc_val",
                 "is_dma", "slot_key", "slot_use", "tag")


class Sched:
    def __init__(self):
        self.ops = []
        self.eng = defaultdict(list)
        self.res = {}
        self.dma_q_count = defaultdict(int)
        self.dma_slot_uses = defaultdict(int)

    def op(self, engine, fn, reads=(), writes=(), extra_deps=(), tag=""):
        deps = set()
        for r in reads:
            w, _ = self.res.get(r, (None, []))
            if w is not None:
                deps.add(w)
        for wk in writes:
            w, rd = self.res.get(wk, (None, []))
            if w is not None:
                deps.add(w)
            deps.update(rd)
        deps.update(e for e in extra_deps if e is not None)
        o = _Op()
        o.oid = len(self.ops)
        o.engine = engine
        o.fn = fn
        o.deps = deps
        o.needs_inc = False
        o.inc_val = 0
        o.is_dma = False
        o.slot_key = None
        o.slot_use = 0
        o.tag = tag
        self.ops.append(o)
        self.eng[engine].append(o)
        for r in reads:
            w, rd = self.res.get(r, (None, []))
            self.res[r] = (w, rd + [o.oid])
        for wk in writes:
            self.res[wk] = (o.oid, [])
        return o.oid

    def dma(self, queue, fn, reads=(), writes=(), tag=""):
        oid = self.op(queue, fn, reads, writes, tag=tag)
        o = self.ops[oid]
        o.is_dma = True
        slot = self.dma_q_count[queue] % N_DMA_SLOTS
        self.dma_q_count[queue] += 1
        o.slot_key = (queue, slot)
        self.dma_slot_uses[o.slot_key] += 1
        o.slot_use = self.dma_slot_uses[o.slot_key]
        return oid

    def barrier(self):
        lasts = set()
        for lst in self.eng.values():
            if lst:
                lasts.add(lst[-1].oid)
        dma_last = {}
        for o in self.ops:
            if o.is_dma:
                dma_last[o.slot_key] = o.oid
        deps = lasts | set(dma_last.values())
        for e in ("sp", "pe", "act", "dve", "pool"):
            self.op(e, lambda eng: eng.nop(), extra_deps=sorted(deps), tag="barrier")

    def materialize(self, nc, block, eng_sems, dma_sems):
        for o in self.ops:
            for d in o.deps:
                p = self.ops[d]
                if p.is_dma:
                    continue
                if p.engine == o.engine and p.engine not in STRICT:
                    continue
                p.needs_inc = True
        for name, lst in self.eng.items():
            c = 0
            for o in lst:
                if o.is_dma:
                    o.inc_val = 16 * o.slot_use
                elif o.needs_inc:
                    c += 1
                    o.inc_val = c

        sched = self

        for name in ("sp", "pe", "act", "dve", "pool"):
            def emitter(eng, name=name):
                maxw = {}

                def wait(sem, val):
                    if maxw.get(sem.name, -1) >= val:
                        return
                    eng.wait_ge(sem, val)
                    maxw[sem.name] = val

                for o in sched.eng[name]:
                    for d in sorted(o.deps):
                        p = sched.ops[d]
                        if p.is_dma:
                            wait(dma_sems[p.slot_key], p.inc_val)
                        else:
                            if p.engine == o.engine and p.engine not in STRICT:
                                continue
                            if not p.needs_inc:
                                continue
                            wait(eng_sems[p.engine], p.inc_val)
                    if o.is_dma and o.slot_use > 1:
                        wait(dma_sems[o.slot_key], 16 * (o.slot_use - 1))
                    inst = o.fn(eng)
                    if o.is_dma:
                        inst.then_inc(dma_sems[o.slot_key], 16)
                    elif o.needs_inc:
                        inst.then_inc(eng_sems[o.engine], 1)

            getattr(block, ENG_SECT[name])(emitter)


def build_program(phases=("f1", "at", "f2"), dump=None):
    nc = bass.Bass("TRN2", target_bir_lowering=False)

    x_in = nc.dram_tensor("x", [D, T], F32, kind="ExternalInput").ap()
    qkvw = nc.dram_tensor("qkvw", [24, D, 128], F32, kind="ExternalInput").ap()
    outw = nc.dram_tensor("outw", [8, D, 128], F32, kind="ExternalInput").ap()
    gatew = nc.dram_tensor("gatew", [NEXP, D, 512], F32, kind="ExternalInput").ap()
    upw = nc.dram_tensor("upw", [NEXP, D, 512], F32, kind="ExternalInput").ap()
    downw = nc.dram_tensor("downw", [NEXP, 512, D], F32, kind="ExternalInput").ap()
    gw = nc.dram_tensor("gw", [D, 32], F32, kind="ExternalInput").ap()
    gbias = nc.dram_tensor("gbias", [1, 32], F32, kind="ExternalInput").ap()
    out_d = nc.dram_tensor("out", [D, T], F32, kind="ExternalOutput").ap()
    combT_d = {t: nc.dram_tensor(f"combT_d{t}", [32, T], BF16, kind="Internal").ap()
               for t in ("f1", "f2")}
    wbf_d = {m: nc.dram_tensor(f"wbf_{m}", [NEXP, 128, 4096], BF16, kind="Internal").ap()
             for m in ("wg", "wu", "wd")}
    dbg_d = None
    dbg_bf = None
    dump = _BUILD_DUMP[0]
    if dump is not None:
        dbg_d = nc.dram_tensor("dbg", [128, 4 * 16384], F32, kind="ExternalOutput").ap()
        dbg_bf = nc.dram_tensor("dbgb", [128, 16384], BF16, kind="ExternalOutput").ap()

    s = Sched()
    est = contextlib.ExitStack()

    with est:
        x8 = est.enter_context(nc.sbuf_tensor("x8", [128, C * T], F32))
        hln = est.enter_context(nc.sbuf_tensor("hln", [128, C * T], BF16))
        id_bf = est.enter_context(nc.sbuf_tensor("id_bf", [128, 128], BF16))
        id_f32 = est.enter_context(nc.sbuf_tensor("id_f32", [128, 128], F32))
        cmask = est.enter_context(nc.sbuf_tensor("cmask", [128, 128], F32))
        ones_col_bf = est.enter_context(nc.sbuf_tensor("ones_col_bf", [128, 1], BF16))
        ones_row_bf = est.enter_context(nc.sbuf_tensor("ones_row_bf", [1, 128], BF16))
        ones_row_f = est.enter_context(nc.sbuf_tensor("ones_row_f", [1, 128], F32))
        gwT = est.enter_context(nc.sbuf_tensor("gwT", [128, C * 32], F32))
        bias32 = est.enter_context(nc.sbuf_tensor("bias32", [128, 32], F32))
        eps1 = est.enter_context(nc.sbuf_tensor("eps1", [1, 1], F32))

        XR = [[f"x.{c}.{t}" for t in range(TC)] for c in range(C)]
        HR = [[f"h.{c}.{t}" for t in range(TC)] for c in range(C)]
        allx = [k for row in XR for k in row]

        def x_view(c, tc, wdt=TW):
            return x8[:, c * T + tc * TW: c * T + tc * TW + wdt]

        def h_view(c, tc, wdt=TW):
            return hln[:, c * T + tc * TW: c * T + tc * TW + wdt]

        s.dma("act", lambda e: e.dma_start(
            out=x8[:, :].rearrange("p (c t) -> p c t", c=C),
            in_=x_in.rearrange("(c p) t -> p c t", p=128)),
            writes=allx, tag="xload")
        s.dma("act", lambda e: e.dma_start(
            out=gwT[:, :].rearrange("p (c e) -> p c e", c=C),
            in_=gw.rearrange("(c p) e -> p c e", p=128)),
            writes=["gwT"], tag="gwload")
        s.dma("act", lambda e: e.dma_start(
            out=bias32[:, :],
            in_=bass.AP(gbias.tensor, 0, [[0, 128], [1, 32]])),
            writes=["bias32"], tag="gbias")

        s.op("pool", lambda e: e.memset(id_bf[:, :], 0.0), writes=["id_bf"])
        s.op("pool", lambda e: e.affine_select(
            out=id_bf[:, :], in_=id_bf[:, :], compare_op=ALU.not_equal,
            fill=1.0, base=0, pattern=[[-1, 128]], channel_multiplier=1),
            reads=["id_bf"], writes=["id_bf"])
        s.op("pool", lambda e: e.memset(id_f32[:, :], 0.0), writes=["id_f32"])
        s.op("pool", lambda e: e.affine_select(
            out=id_f32[:, :], in_=id_f32[:, :], compare_op=ALU.not_equal,
            fill=1.0, base=0, pattern=[[-1, 128]], channel_multiplier=1),
            reads=["id_f32"], writes=["id_f32"])
        s.op("pool", lambda e: e.memset(cmask[:, :], 0.0), writes=["cmask"])
        s.op("pool", lambda e: e.affine_select(
            out=cmask[:, :], in_=cmask[:, :], compare_op=ALU.is_ge,
            fill=-1e9, base=0, pattern=[[-1, 128]], channel_multiplier=1),
            reads=["cmask"], writes=["cmask"])
        s.op("pool", lambda e: e.memset(ones_col_bf[:, :], 1.0), writes=["ones_col"])
        s.op("pool", lambda e: e.memset(ones_row_bf[:, :], 1.0), writes=["ones_row"])
        s.op("pool", lambda e: e.memset(ones_row_f[:, :], 1.0), writes=["ones_row_f"])
        s.op("pool", lambda e: e.memset(eps1[:, :], 1e-5), writes=["eps1"])

        def layernorm(tag, gate, combine_sb=None, combT_sb=None):
            with contextlib.ExitStack() as sc:
                xb = sc.enter_context(nc.sbuf_tensor(f"xb{tag}", [128, 2 * T], BF16))
                sq = sc.enter_context(nc.sbuf_tensor(f"sq{tag}", [128, 2 * T], BF16))
                mrow = sc.enter_context(nc.sbuf_tensor(f"mrow{tag}", [1, T], F32))
                trow = sc.enter_context(nc.sbuf_tensor(f"trow{tag}", [1, T], F32))
                rrow = sc.enter_context(nc.sbuf_tensor(f"rrow{tag}", [1, T], F32))
                xl = sc.enter_context(nc.sbuf_tensor(f"xl{tag}", [128, 2 * TW], F32))
                if gate:
                    sem_ = sc.enter_context(nc.sbuf_tensor(f"sem{tag}", [32, T], F32))

                with contextlib.ExitStack() as pc:
                    px = pc.enter_context(nc.psum_tensor(f"px{tag}", [1, T], F32))
                    pq = pc.enter_context(nc.psum_tensor(f"pq{tag}", [1, T], F32))
                    for c in range(C):
                        sl = c % 2
                        s.op("pool", lambda e, c=c, sl=sl: e.tensor_copy(
                            out=xb[:, sl * T:(sl + 1) * T], in_=x8[:, c * T:(c + 1) * T]),
                            reads=XR[c], writes=[f"lnxb.{sl}"], tag=f"lncast{c}")
                        s.op("act", lambda e, sl=sl: e.activation(
                            out=sq[:, sl * T:(sl + 1) * T], in_=xb[:, sl * T:(sl + 1) * T],
                            func=AF.Square),
                            reads=[f"lnxb.{sl}"], writes=[f"lnsq.{sl}"])
                        for tc in range(TC):
                            s.op("pe", lambda e, c=c, sl=sl, tc=tc: e.matmul(
                                px[0:1, tc * TW:(tc + 1) * TW], ones_col_bf[:, :],
                                xb[:, sl * T + tc * TW: sl * T + (tc + 1) * TW],
                                start=(c == 0), stop=(c == C - 1)),
                                reads=[f"lnxb.{sl}", "ones_col"], writes=[f"lnpx.{tc}"])
                            s.op("pe", lambda e, c=c, sl=sl, tc=tc: e.matmul(
                                pq[0:1, tc * TW:(tc + 1) * TW], ones_col_bf[:, :],
                                sq[:, sl * T + tc * TW: sl * T + (tc + 1) * TW],
                                start=(c == 0), stop=(c == C - 1)),
                                reads=[f"lnsq.{sl}", "ones_col"], writes=[f"lnpq.{tc}"])
                    pxr = [f"lnpx.{t}" for t in range(TC)]
                    pqr = [f"lnpq.{t}" for t in range(TC)]
                    s.op("dve", lambda e: e.tensor_scalar_mul(
                        mrow[:, :], px[0:1, :], 1.0 / D), reads=pxr, writes=["mrow"])
                    s.op("dve", lambda e: e.tensor_mul(
                        trow[:, :], mrow[:, :], mrow[:, :]), reads=["mrow"], writes=["trow"])
                    s.op("dve", lambda e: e.scalar_tensor_tensor(
                        out=trow[:, :], in0=pq[0:1, :], scalar=1.0 / D, in1=trow[:, :],
                        op0=ALU.mult, op1=ALU.subtract),
                        reads=pqr + ["trow"], writes=["trow"])
                    s.op("act", lambda e: e.activation(
                        out=trow[:, :], in_=trow[:, :], func=AF.Ln, bias=eps1[:, :]),
                        reads=["trow", "eps1"], writes=["trow"])
                    s.op("act", lambda e: e.activation(
                        out=rrow[:, :], in_=trow[:, :], func=AF.Exp, scale=-0.5),
                        reads=["trow"], writes=["rrow"])
                s.barrier()

                with contextlib.ExitStack() as pc:
                    pmb = pc.enter_context(nc.psum_tensor(f"pmb{tag}", [128, TW], F32))
                    prb = pc.enter_context(nc.psum_tensor(f"prb{tag}", [128, TW], F32))
                    pwarm = pc.enter_context(nc.psum_tensor(f"pwarm{tag}", [128, 128], F32))
                    if gate:
                        pg = pc.enter_context(nc.psum_tensor(f"pg{tag}", [32, T], F32))

                    def warm(n):
                        for _ in range(n):
                            s.op("pe", lambda e: e.matmul(
                                pwarm[0:1, 0:128], ones_col_bf[:, :], id_bf[:, :],
                                start=True, stop=True),
                                reads=["ones_col", "id_bf"], writes=["pwarm"])

                    warm(60)
                    for tc in range(TC):
                        if tc:
                            warm(50)
                        s.op("pe", lambda e, tc=tc: e.matmul(
                            pmb[:, :], ones_row_f[:, :], mrow[0:1, tc * TW:(tc + 1) * TW],
                            start=True, stop=True),
                            reads=["mrow", "ones_row_f"], writes=["pmb"])
                        s.op("pe", lambda e, tc=tc: e.matmul(
                            prb[:, :], ones_row_f[:, :], rrow[0:1, tc * TW:(tc + 1) * TW],
                            start=True, stop=True),
                            reads=["rrow", "ones_row_f"], writes=["prb"])
                        for c in range(C):
                            sl = c % 2
                            s.op("dve", lambda e, c=c, tc=tc, sl=sl: e.tensor_sub(
                                xl[:, sl * TW:(sl + 1) * TW], x_view(c, tc), pmb[:, :]),
                                reads=XR[c][tc:tc + 1] + ["pmb"], writes=[f"lnxl.{sl}"])
                            s.op("dve", lambda e, sl=sl: e.tensor_mul(
                                xl[:, sl * TW:(sl + 1) * TW],
                                xl[:, sl * TW:(sl + 1) * TW], prb[:, :]),
                                reads=[f"lnxl.{sl}", "prb"], writes=[f"lnxl.{sl}"])
                            s.op("act", lambda e, c=c, tc=tc, sl=sl: e.activation(
                                out=h_view(c, tc), in_=xl[:, sl * TW:(sl + 1) * TW],
                                func=AF.Copy),
                                reads=[f"lnxl.{sl}"], writes=[HR[c][tc]])
                            if gate:
                                s.op("pe", lambda e, c=c, tc=tc, sl=sl: e.matmul(
                                    pg[0:32, tc * TW:(tc + 1) * TW],
                                    gwT[:, c * 32:(c + 1) * 32],
                                    xl[:, sl * TW:(sl + 1) * TW],
                                    start=(c == 0), stop=(c == C - 1)),
                                    reads=[f"lnxl.{sl}", "gwT"], writes=[f"lnpg.{tc}"])
                    if gate:
                        for tc in range(TC):
                            s.op("act", lambda e, tc=tc: e.activation(
                                out=sem_[0:32, tc * TW:(tc + 1) * TW],
                                in_=pg[0:32, tc * TW:(tc + 1) * TW], func=AF.Sigmoid),
                                reads=[f"lnpg.{tc}"], writes=[f"sem.{tc}"])
                s.barrier()

                if not gate:
                    return

                with contextlib.ExitStack() as gc:
                    sc_all = gc.enter_context(nc.sbuf_tensor(f"scall{tag}", [128, 512], F32))
                    scb = gc.enter_context(nc.sbuf_tensor(f"scb{tag}", [128, 512], F32))
                    u1 = gc.enter_context(nc.sbuf_tensor(f"u1{tag}", [128, 128], F32))
                    u2 = gc.enter_context(nc.sbuf_tensor(f"u2{tag}", [128, 128], F32))
                    gsc = gc.enter_context(nc.sbuf_tensor(f"gsc{tag}", [128, 128], F32))
                    g8 = gc.enter_context(nc.sbuf_tensor(f"g8{tag}", [128, 8], F32))
                    gmask = gc.enter_context(nc.sbuf_tensor(f"gmask{tag}", [128, 128], F32))
                    smask = gc.enter_context(nc.sbuf_tensor(f"smask{tag}", [128, 512], F32))
                    masked = gc.enter_context(nc.sbuf_tensor(f"masked{tag}", [128, 512], F32))
                    m8 = gc.enter_context(nc.sbuf_tensor(f"m8{tag}", [128, 8], F32))
                    selw = gc.enter_context(nc.sbuf_tensor(f"selw{tag}", [128, 512], F32))
                    dnm = gc.enter_context(nc.sbuf_tensor(f"dnm{tag}", [128, 16], F32))
                    with contextlib.ExitStack() as pc:
                        pt0 = pc.enter_context(nc.psum_tensor(f"pt0{tag}", [128, 32], F32))
                        pt1 = pc.enter_context(nc.psum_tensor(f"pt1{tag}", [128, 32], F32))
                        ptc0 = pc.enter_context(nc.psum_tensor(f"ptc0{tag}", [32, 128], F32))
                        ptc1 = pc.enter_context(nc.psum_tensor(f"ptc1{tag}", [32, 128], F32))
                        for t16 in range(16):
                            pt = (pt0, pt1)[t16 % 2]
                            key = f"lnpt.{t16 % 2}"
                            s.op("pe", lambda e, t16=t16, pt=pt: e.transpose(
                                pt[:, :], sem_[0:32, t16 * 128:(t16 + 1) * 128],
                                id_f32[0:32, 0:32]),
                                reads=[f"sem.{t16 // 4}", "id_f32"], writes=[key])
                            s.op("act", lambda e, t16=t16, pt=pt: e.activation(
                                out=sc_all[:, t16 * 32:(t16 + 1) * 32], in_=pt[:, :],
                                func=AF.Copy),
                                reads=[key], writes=[f"scall.{t16}"])
                        scr = [f"scall.{t}" for t in range(16)]
                        for t16 in range(16):
                            s.op("dve", lambda e, t16=t16: e.tensor_add(
                                scb[:, t16 * 32:(t16 + 1) * 32],
                                sc_all[:, t16 * 32:(t16 + 1) * 32], bias32[:, :]),
                                reads=[f"scall.{t16}", "bias32"], writes=["scb"])

                        def v4(t, j):
                            return bass.AP(t, j, [[512, 128], [32, 16], [4, 8]])

                        pairs = [(0, 1), (0, 2), (0, 3), (1, 2), (1, 3), (2, 3)]
                        for pi, (a, b) in enumerate(pairs):
                            dst = u1 if pi == 0 else u2
                            dk = "u1" if pi == 0 else "u2"
                            s.op("dve", lambda e, a=a, b=b, dst=dst: e.tensor_add(
                                bass.AP(dst, 0, [[128, 128], [8, 16], [1, 8]]),
                                v4(scb, a), v4(scb, b)),
                                reads=["scb"], writes=[dk])
                            if pi == 0:
                                continue
                            tgt = u1 if pi < 5 else gsc
                            tk = "u1" if pi < 5 else "gsc"
                            s.op("dve", lambda e, tgt=tgt: e.tensor_max(
                                tgt[:, :], u1[:, :], u2[:, :]),
                                reads=["u1", "u2"], writes=[tk])
                        for t16 in range(16):
                            s.op("dve", lambda e, t16=t16: e.max(
                                out=g8[:, :], in_=gsc[:, t16 * 8:(t16 + 1) * 8]),
                                reads=["gsc"], writes=["g8"])
                            s.op("dve", lambda e, t16=t16: e.tensor_scalar(
                                out=gmask[:, t16 * 8:(t16 + 1) * 8],
                                in0=gsc[:, t16 * 8:(t16 + 1) * 8],
                                scalar1=g8[:, 3:4], scalar2=None, op0=ALU.is_ge),
                                reads=["gsc", "g8"], writes=["gmask"])
                        for j in range(4):
                            s.op("dve", lambda e, j=j: e.tensor_copy(
                                out=v4(smask, j),
                                in_=bass.AP(gmask, 0, [[128, 128], [8, 16], [1, 8]])),
                                reads=["gmask"], writes=["smask"])
                        s.op("dve", lambda e: e.tensor_mul(
                            masked[:, :], scb[:, :], smask[:, :]),
                            reads=["scb", "smask"], writes=["maskedg"])
                        for t16 in range(16):
                            s.op("dve", lambda e, t16=t16: e.max(
                                out=m8[:, :], in_=masked[:, t16 * 32:(t16 + 1) * 32]),
                                reads=["maskedg"], writes=["m8"])
                            s.op("dve", lambda e, t16=t16: e.tensor_scalar(
                                out=selw[:, t16 * 32:(t16 + 1) * 32],
                                in0=masked[:, t16 * 32:(t16 + 1) * 32],
                                scalar1=m8[:, 7:8], scalar2=None, op0=ALU.is_ge),
                                reads=["maskedg", "m8"], writes=["selw"])
                        s.op("dve", lambda e: e.tensor_mul(
                            selw[:, :], selw[:, :], sc_all[:, :]),
                            reads=["selw"] + scr, writes=["selw"])
                        for t16 in range(16):
                            s.op("dve", lambda e, t16=t16: e.reduce_sum(
                                out=dnm[:, t16:t16 + 1],
                                in_=selw[:, t16 * 32:(t16 + 1) * 32], axis=AX.X),
                                reads=["selw"], writes=["dnm"])
                        s.op("dve", lambda e: e.tensor_scalar_add(
                            dnm[:, :], dnm[:, :], 1e-20), reads=["dnm"], writes=["dnm"])
                        s.op("dve", lambda e: e.reciprocal(
                            dnm[:, :], dnm[:, :]), reads=["dnm"], writes=["dnm"])
                        for t16 in range(16):
                            s.op("dve", lambda e, t16=t16: e.tensor_scalar(
                                out=combine_sb[:, t16 * 32:(t16 + 1) * 32],
                                in0=selw[:, t16 * 32:(t16 + 1) * 32],
                                scalar1=dnm[:, t16:t16 + 1], scalar2=2.5,
                                op0=ALU.mult, op1=ALU.mult),
                                reads=["selw", "dnm"], writes=[f"comb.{t16}"])
                            ptc = (ptc0, ptc1)[t16 % 2]
                            pk = f"lnptc.{t16 % 2}"
                            s.op("pe", lambda e, t16=t16, ptc=ptc: e.transpose(
                                ptc[:, :], combine_sb[:, t16 * 32:(t16 + 1) * 32],
                                id_f32[:, :]),
                                reads=[f"comb.{t16}", "id_f32"], writes=[pk])
                            s.op("act", lambda e, t16=t16, ptc=ptc: e.activation(
                                out=combT_sb[0:32, t16 * 128:(t16 + 1) * 128],
                                in_=ptc[:, :], func=AF.Copy),
                                reads=[pk], writes=[f"combT.{t16 // 4}"])
                s.dma("sp", lambda e: e.dma_start(
                    out=combT_d[tag][:, :], in_=combT_sb[:, :]),
                    reads=[f"combT.{t}" for t in range(4)], writes=["combTd"],
                    tag="combTout")
                if _BUILD_DUMP[0] == "comb" and tag == "f1":
                    s.dma("sp", lambda e: e.dma_start(
                        out=dbg_d[:, 0:512], in_=combine_sb[:, :]),
                        reads=[f"comb.{t}" for t in range(16)], tag="dumpcomb")
                    s.dma("sp", lambda e: e.dma_start(
                        out=dbg_d[0:32, 512:512 + T], in_=sem_[:, :]),
                        reads=[f"sem.{t}" for t in range(4)], tag="dumpsem")
                s.barrier()

        def ffn_call(tag):
            with contextlib.ExitStack() as fc:
                combine_sb = fc.enter_context(nc.sbuf_tensor(f"combine{tag}", [128, 512], F32))
                combT_sb = fc.enter_context(nc.sbuf_tensor(f"combT{tag}", [32, T], BF16))

                layernorm(tag, gate=True, combine_sb=combine_sb, combT_sb=combT_sb)

                with contextlib.ExitStack() as ec:
                    wst = ec.enter_context(nc.sbuf_tensor(f"wst{tag}", [128, 2 * 2048], F32))
                    wg = ec.enter_context(nc.sbuf_tensor(f"wg{tag}", [128, 2 * 4096], BF16))
                    wu = ec.enter_context(nc.sbuf_tensor(f"wu{tag}", [128, 2 * 4096], BF16))
                    wd = ec.enter_context(nc.sbuf_tensor(f"wd{tag}", [128, 2 * 4096], BF16))
                    zb = ec.enter_context(nc.sbuf_tensor(f"zb{tag}", [128, 2 * 2048], BF16))
                    zt = ec.enter_context(nc.sbuf_tensor(f"zt{tag}", [128, 2 * TW], BF16))
                    gs = ec.enter_context(nc.sbuf_tensor(f"gs{tag}", [128, 2 * TW], F32))
                    cb = ec.enter_context(nc.sbuf_tensor(f"cb{tag}", [128, 2 * T], BF16))
                    with contextlib.ExitStack() as pc:
                        pgA = pc.enter_context(nc.psum_tensor(f"pgA{tag}", [128, TW], F32))
                        pgB = pc.enter_context(nc.psum_tensor(f"pgB{tag}", [128, TW], F32))
                        puA = pc.enter_context(nc.psum_tensor(f"puA{tag}", [128, TW], F32))
                        puB = pc.enter_context(nc.psum_tensor(f"puB{tag}", [128, TW], F32))
                        pd = [pc.enter_context(nc.psum_tensor(f"pd{j}{tag}", [128, TW], F32))
                              for j in range(4)]

                        nstage = [0]

                        def load_expert(e_):
                            sl = e_ % 2
                            if e_ < 32:
                                s.dma("act", lambda e, e_=e_, sl=sl: e.dma_start(
                                    out=cb[:, sl * T:(sl + 1) * T],
                                    in_=bass.AP(combT_d[tag].tensor, e_ * T,
                                                [[0, 128], [1, T]])),
                                    reads=["combTd"], writes=[f"cb.{sl}"],
                                    tag=f"cbdma{e_}")
                            if tag == "f2":
                                for dstbuf, dk in ((wg, "wg"), (wu, "wu"), (wd, "wd")):
                                    s.dma("sp", lambda e, dstbuf=dstbuf, dk=dk, e_=e_, sl=sl:
                                          e.dma_start(
                                        out=dstbuf[:, sl * 4096:(sl + 1) * 4096],
                                        in_=wbf_d[dk][e_]),
                                        writes=[f"{dk}.{sl}"], tag=f"wbfin{e_}")
                                return
                            for src, dstbuf, dk, scale, nch in (
                                    (gatew, wg, "wg", None, 8),
                                    (upw, wu, "wu", None, 8),
                                    (downw, wd, "wd", 0.5, 4)):
                                for half in range(2):
                                    ss = nstage[0] % 2
                                    nstage[0] += 1
                                    hc2 = nch // 2
                                    q_ = "act" if dk == "wd" else "sp"
                                    s.dma(q_, lambda e, src=src, e_=e_, half=half, ss=ss, hc2=hc2:
                                          e.dma_start(
                                        out=wst[:, ss * 2048:(ss + 1) * 2048].rearrange(
                                            "p (c i) -> p c i", c=hc2),
                                        in_=src[e_].rearrange(
                                            "(c p) i -> p c i", p=128)[:, half * hc2:(half + 1) * hc2, :]),
                                        writes=[f"wst.{ss}"], tag=f"wdma{e_}")
                                    if scale is None:
                                        s.op("dve", lambda e, dstbuf=dstbuf, sl=sl, half=half, ss=ss:
                                             e.tensor_copy(
                                            out=dstbuf[:, sl * 4096 + half * 2048:
                                                       sl * 4096 + (half + 1) * 2048],
                                            in_=wst[:, ss * 2048:(ss + 1) * 2048]),
                                            reads=[f"wst.{ss}"], writes=[f"{dk}.{sl}"])
                                    else:
                                        s.op("dve", lambda e, dstbuf=dstbuf, sl=sl, half=half, ss=ss:
                                             e.tensor_scalar_mul(
                                            dstbuf[:, sl * 4096 + half * 2048:
                                                   sl * 4096 + (half + 1) * 2048],
                                            wst[:, ss * 2048:(ss + 1) * 2048], 0.5),
                                            reads=[f"wst.{ss}"], writes=[f"{dk}.{sl}"])
                                s.dma("act", lambda e, dstbuf=dstbuf, dk=dk, e_=e_, sl=sl:
                                      e.dma_start(
                                    out=wbf_d[dk][e_],
                                    in_=dstbuf[:, sl * 4096:(sl + 1) * 4096]),
                                    reads=[f"{dk}.{sl}"], tag=f"wbfout{e_}")

                        def emit_gu(e_, tc):
                            sl = e_ % 2
                            routed = e_ < 32
                            if True:
                                ztc = (4 * e_ + tc) % 2
                                for ic in range(NI):
                                    par = ic % 2
                                    pgp = (pgA, pgB)[par]
                                    pup = (puA, puB)[par]
                                    for kc in range(C):
                                        s.op("pe", lambda e, sl=sl, tc=tc, ic=ic, kc=kc, pgp=pgp:
                                             e.matmul(
                                            pgp[:, :],
                                            wg[:, sl * 4096 + kc * 512 + ic * 128:
                                               sl * 4096 + kc * 512 + (ic + 1) * 128],
                                            h_view(kc, tc),
                                            start=(kc == 0), stop=(kc == C - 1)),
                                            reads=[f"wg.{sl}", HR[kc][tc]],
                                            writes=[f"pg.{par}"])
                                    s.op("act", lambda e, par=par, pgp=pgp: e.activation(
                                        out=gs[:, par * TW:(par + 1) * TW],
                                        in_=pgp[:, :], func=AF.Silu),
                                        reads=[f"pg.{par}"], writes=[f"gs.{par}"])
                                    for kc in range(C):
                                        s.op("pe", lambda e, sl=sl, tc=tc, ic=ic, kc=kc, pup=pup:
                                             e.matmul(
                                            pup[:, :],
                                            wu[:, sl * 4096 + kc * 512 + ic * 128:
                                               sl * 4096 + kc * 512 + (ic + 1) * 128],
                                            h_view(kc, tc),
                                            start=(kc == 0), stop=(kc == C - 1)),
                                            reads=[f"wu.{sl}", HR[kc][tc]],
                                            writes=[f"pu.{par}"])
                                    if routed:
                                        s.op("dve", lambda e, par=par, pup=pup: e.tensor_mul(
                                            zt[:, par * TW:(par + 1) * TW],
                                            gs[:, par * TW:(par + 1) * TW], pup[:, :]),
                                            reads=[f"gs.{par}", f"pu.{par}"],
                                            writes=[f"zt.{par}"])
                                        s.op("dve", lambda e, par=par, ztc=ztc, ic=ic, tc=tc, sl=sl:
                                             e.tensor_mul(
                                            zb[:, ztc * 2048 + ic * TW:
                                               ztc * 2048 + (ic + 1) * TW],
                                            zt[:, par * TW:(par + 1) * TW],
                                            cb[:, sl * T + tc * TW: sl * T + (tc + 1) * TW]),
                                            reads=[f"zt.{par}", f"cb.{sl}"],
                                            writes=[f"z.{ztc}.{ic}"])
                                    else:
                                        s.op("dve", lambda e, par=par, ztc=ztc, ic=ic, pup=pup:
                                             e.tensor_mul(
                                            zb[:, ztc * 2048 + ic * TW:
                                               ztc * 2048 + (ic + 1) * TW],
                                            gs[:, par * TW:(par + 1) * TW], pup[:, :]),
                                            reads=[f"gs.{par}", f"pu.{par}"],
                                            writes=[f"z.{ztc}.{ic}"])

                        def emit_down(e_, tc):
                            sl = e_ % 2
                            if True:
                                ztc = (4 * e_ + tc) % 2
                                for hc in range(C):
                                    b = hc % 4
                                    for ic in range(NI):
                                        s.op("pe", lambda e, sl=sl, hc=hc, ic=ic, ztc=ztc, b=b:
                                             e.matmul(
                                            pd[b][:, :],
                                            wd[:, sl * 4096 + ic * 1024 + hc * 128:
                                               sl * 4096 + ic * 1024 + (hc + 1) * 128],
                                            zb[:, ztc * 2048 + ic * TW:
                                               ztc * 2048 + (ic + 1) * TW],
                                            start=(ic == 0), stop=(ic == NI - 1)),
                                            reads=[f"wd.{sl}", f"z.{ztc}.{ic}"],
                                            writes=[f"pd.{b}"])
                                    s.op("dve", lambda e, hc=hc, tc=tc, b=b: e.tensor_add(
                                        x_view(hc, tc), pd[b][:, :], x_view(hc, tc)),
                                        reads=[f"pd.{b}", XR[hc][tc]],
                                        writes=[XR[hc][tc]])

                        load_expert(0)
                        load_expert(1)
                        seq = [(e_, tc) for e_ in range(NEXP) for tc in range(TC)]
                        LAG = 2

                        def emit_down_and_load(j):
                            d_e, d_tc = seq[j]
                            emit_down(d_e, d_tc)
                            if d_tc == TC - 1 and d_e + 2 < NEXP:
                                load_expert(d_e + 2)

                        for i, (e_, tc) in enumerate(seq):
                            if i >= LAG:
                                emit_down_and_load(i - LAG)
                            emit_gu(e_, tc)
                        for j in range(len(seq) - LAG, len(seq)):
                            emit_down_and_load(j)
            s.barrier()

        def attn_call(tag):
            layernorm(tag, gate=False)
            with contextlib.ExitStack() as ac:
                ao = ac.enter_context(nc.sbuf_tensor(f"ao{tag}", [128, C * T], BF16))
                qkvb = ac.enter_context(nc.sbuf_tensor(f"qkvb{tag}", [128, 3 * T], BF16))
                wqst = ac.enter_context(nc.sbuf_tensor(f"wqst{tag}", [128, 2 * 1024], F32))
                wqb = ac.enter_context(nc.sbuf_tensor(f"wqb{tag}", [128, 2 * 3 * 1024], BF16))
                vt = ac.enter_context(nc.sbuf_tensor(f"vt{tag}", [128, 16 * 128], BF16))
                pP = ac.enter_context(nc.sbuf_tensor(f"pP{tag}", [128, 2 * T], BF16))
                ptsb = ac.enter_context(nc.sbuf_tensor(f"ptsb{tag}", [128, 2 * 128], BF16))
                mx4 = ac.enter_context(nc.sbuf_tensor(f"mx4{tag}", [128, 8], F32))
                mbias = ac.enter_context(nc.sbuf_tensor(f"mbias{tag}", [128, 2], F32))
                rs4 = ac.enter_context(nc.sbuf_tensor(f"rs4{tag}", [128, 8], F32))
                rsum = ac.enter_context(nc.sbuf_tensor(f"rsum{tag}", [128, 2], F32))

                with contextlib.ExitStack() as pc:
                    pS = pc.enter_context(nc.psum_tensor(f"pS{tag}", [128, T], F32))
                    pPT = [pc.enter_context(nc.psum_tensor(f"pPT{j}{tag}", [128, 128], BF16))
                           for j in range(2)]
                    pO = [pc.enter_context(nc.psum_tensor(f"pO{j}{tag}", [128, 128], F32))
                          for j in range(2)]

                    nqst = [0]

                    def load_qkvw(c):
                        sl = c % 2
                        for m in range(3):
                            ss = nqst[0] % 2
                            nqst[0] += 1
                            oc = m * 8 + c
                            s.dma("sp", lambda e, oc=oc, ss=ss: e.dma_start(
                                out=wqst[:, ss * 1024:(ss + 1) * 1024].rearrange(
                                    "p (c m) -> p c m", c=8),
                                in_=qkvw[oc].rearrange("(c p) m -> p c m", p=128)),
                                writes=[f"wqst.{ss}"], tag=f"qkvw{oc}")
                            s.op("pool", lambda e, m=m, sl=sl, ss=ss: e.tensor_copy(
                                out=wqb[:, sl * 3072 + m * 1024:sl * 3072 + (m + 1) * 1024],
                                in_=wqst[:, ss * 1024:(ss + 1) * 1024]),
                                reads=[f"wqst.{ss}"], writes=[f"wqb.{sl}.{m}"])

                    def chunk(c):
                        sl = c % 2
                        for m in range(3):
                            for tc in range(TC):
                                for kc in range(C):
                                    s.op("pe", lambda e, m=m, tc=tc, kc=kc, sl=sl: e.matmul(
                                        pS[:, tc * TW:(tc + 1) * TW],
                                        wqb[:, sl * 3072 + m * 1024 + kc * 128:
                                            sl * 3072 + m * 1024 + (kc + 1) * 128],
                                        h_view(kc, tc),
                                        start=(kc == 0), stop=(kc == C - 1)),
                                        reads=[f"wqb.{sl}.{m}", HR[kc][tc]],
                                        writes=[f"pS.{tc}"])
                                s.op("act", lambda e, m=m, tc=tc: e.activation(
                                    out=qkvb[:, m * T + tc * TW: m * T + (tc + 1) * TW],
                                    in_=pS[:, tc * TW:(tc + 1) * TW], func=AF.Copy),
                                    reads=[f"pS.{tc}"], writes=[f"qkv.{m}.{tc}"])
                        for kc in range(16):
                            p3 = pPT[kc % 2]
                            k3 = f"pPT.{kc % 2}"
                            s.op("pe", lambda e, kc=kc, p3=p3: e.transpose(
                                p3[:, :], qkvb[:, 2 * T + kc * 128:2 * T + (kc + 1) * 128],
                                id_bf[:, :]),
                                reads=[f"qkv.2.{kc // 4}", "id_bf"], writes=[k3])
                            s.op("dve", lambda e, kc=kc, p3=p3: e.tensor_copy(
                                out=vt[:, kc * 128:(kc + 1) * 128], in_=p3[:, :]),
                                reads=[k3], writes=[f"vt.{kc}"])

                        def qk_pre(par, qt):
                            qoff = 64 * par
                            klen = (qt + 1) * 128
                            nk = (klen + TW - 1) // TW
                            jd = qt // 4
                            for j in range(nk):
                                w = min(TW, klen - j * TW)
                                s.op("pe", lambda e, qt=qt, j=j, w=w, qoff=qoff:
                                     e.matmul(
                                    pS[:, j * TW: j * TW + w],
                                    qkvb[qoff:qoff + 64, qt * 128:(qt + 1) * 128],
                                    qkvb[qoff:qoff + 64, T + j * TW: T + j * TW + w],
                                    start=True, stop=True),
                                    reads=[f"qkv.0.{qt // 4}", f"qkv.1.{j}"],
                                    writes=[f"pS.{j}"])
                                if j == jd:
                                    s.op("dve", lambda e, qt=qt: e.tensor_add(
                                        pS[:, qt * 128:(qt + 1) * 128],
                                        pS[:, qt * 128:(qt + 1) * 128], cmask[:, :]),
                                        reads=[f"pS.{jd}", "cmask"], writes=[f"pS.{jd}"])
                                s.op("dve", lambda e, j=j, w=w, par=par: e.reduce_max(
                                    out=mx4[:, par * 4 + j:par * 4 + j + 1],
                                    in_=pS[:, j * TW: j * TW + w], axis=AX.X),
                                    reads=[f"pS.{j}"], writes=[f"mx4.{par}"])
                            s.op("dve", lambda e, nk=nk, par=par: e.reduce_max(
                                out=mbias[:, par:par + 1], in_=mx4[:, par * 4:par * 4 + nk],
                                axis=AX.X),
                                reads=[f"mx4.{par}"], writes=[f"mbias.{par}"])
                            s.op("dve", lambda e, par=par: e.tensor_scalar_mul(
                                mbias[:, par:par + 1], mbias[:, par:par + 1], -0.125),
                                reads=[f"mbias.{par}"], writes=[f"mbias.{par}"])
                            for j in range(nk):
                                w = min(TW, klen - j * TW)
                                s.op("act", lambda e, j=j, w=w, par=par: e.activation(
                                    out=pP[:, par * T + j * TW: par * T + j * TW + w],
                                    in_=pS[:, j * TW: j * TW + w], func=AF.Exp,
                                    scale=0.125, bias=mbias[:, par:par + 1],
                                    accum_out=rs4[:, par * 4 + j:par * 4 + j + 1]),
                                    reads=[f"pS.{j}", f"mbias.{par}"],
                                    writes=[f"P.{par}.{j}", f"rs4.{par}.{j}"])

                        def smax_post(par, qt):
                            klen = (qt + 1) * 128
                            nk = (klen + TW - 1) // TW
                            s.op("dve", lambda e, nk=nk, par=par: e.reduce_sum(
                                out=rsum[:, par:par + 1], in_=rs4[:, par * 4:par * 4 + nk],
                                axis=AX.X),
                                reads=[f"rs4.{par}.{j}" for j in range(nk)],
                                writes=[f"rsum.{par}"])
                            s.op("dve", lambda e, par=par: e.reciprocal(
                                rsum[:, par:par + 1], rsum[:, par:par + 1]),
                                reads=[f"rsum.{par}"], writes=[f"rsum.{par}"])
                            pkeys = [f"P.{par}.{j}" for j in range(nk)]
                            s.op("dve", lambda e, klen=klen, par=par: e.tensor_scalar_mul(
                                pP[:, par * T:par * T + klen],
                                pP[:, par * T:par * T + klen], rsum[:, par:par + 1]),
                                reads=[f"rsum.{par}"] + pkeys, writes=pkeys)

                        def tpv(par, qt, c):
                            qoff = 64 * par
                            tp = (0, 64) if par else None

                            def tr(kc):
                                pp = pPT[kc % 2]
                                ppk = f"pPT.{kc % 2}"
                                s.op("pe", lambda e, kc=kc, pp=pp, par=par: e.transpose(
                                    pp[:, :], pP[:, par * T + kc * 128:
                                                  par * T + (kc + 1) * 128], id_bf[:, :]),
                                    reads=[f"P.{par}.{kc // 4}", "id_bf"], writes=[ppk])
                                s.op("dve", lambda e, kc=kc, pp=pp: e.tensor_copy(
                                    out=ptsb[:, (kc % 2) * 128:(kc % 2 + 1) * 128],
                                    in_=pp[:, :]),
                                    reads=[ppk], writes=[f"ptsb.{kc % 2}"])

                            def pv(kc):
                                s.op("pe", lambda e, kc=kc, qt=qt, qoff=qoff, tp=tp, par=par:
                                     e.matmul(
                                    pO[par][qoff:qoff + 64, 0:128],
                                    vt[:, kc * 128 + qoff: kc * 128 + qoff + 64],
                                    ptsb[:, (kc % 2) * 128:(kc % 2 + 1) * 128],
                                    start=(kc == 0), stop=(kc == qt),
                                    tile_position=tp),
                                    reads=[f"vt.{kc}", f"ptsb.{kc % 2}"],
                                    writes=[f"pO.{par}"])

                            tr(0)
                            for kc in range(1, qt + 1):
                                tr(kc)
                                pv(kc - 1)
                            pv(qt)
                            s.op("act", lambda e, c=c, qt=qt, qoff=qoff, par=par:
                                 e.activation(
                                out=ao[qoff:qoff + 64, c * T + qt * 128:
                                       c * T + (qt + 1) * 128],
                                in_=pO[par][qoff:qoff + 64, 0:128], func=AF.Copy),
                                reads=[f"pO.{par}"], writes=[f"ao.{c}.{qt // 4}"])

                        for qt in range(QT):
                            qk_pre(0, qt)
                            if qt >= 1:
                                tpv(1, qt - 1, c)
                            smax_post(0, qt)
                            qk_pre(1, qt)
                            tpv(0, qt, c)
                            smax_post(1, qt)
                        tpv(1, QT - 1, c)

                    load_qkvw(0)
                    load_qkvw(1)
                    for c in range(C):
                        chunk(c)
                        if c + 2 < C:
                            load_qkvw(c + 2)

                if _BUILD_DUMP[0] == "ao":
                    s.dma("sp", lambda e: e.dma_start(
                        out=dbg_bf[:, 0:C * T], in_=ao[:, :]),
                        reads=[f"ao.{c}.{t}" for c in range(C) for t in range(TC)],
                        tag="dumpao")
                s.barrier()
                with contextlib.ExitStack() as pc:
                    pop = [pc.enter_context(nc.psum_tensor(f"pop{j}{tag}", [128, TW], F32))
                           for j in range(2)]
                    now = [0]
                    for hc in range(C):
                        ss = now[0] % 2
                        now[0] += 1
                        s.dma("sp", lambda e, hc=hc, ss=ss: e.dma_start(
                            out=wqst[:, ss * 1024:(ss + 1) * 1024].rearrange(
                                "p (c m) -> p c m", c=8),
                            in_=outw[hc].rearrange("(c p) m -> p c m", p=128)),
                            writes=[f"wqst.{ss}"], tag=f"outw{hc}")
                        s.op("pool", lambda e, hc=hc, ss=ss: e.tensor_copy(
                            out=wqb[:, (hc % 2) * 3072:(hc % 2) * 3072 + 1024],
                            in_=wqst[:, ss * 1024:(ss + 1) * 1024]),
                            reads=[f"wqst.{ss}"], writes=[f"wqb.{hc % 2}.0"])
                        for tc in range(TC):
                            pb = pop[tc % 2]
                            pbk = f"pop.{tc % 2}"
                            for kc in range(C):
                                s.op("pe", lambda e, hc=hc, tc=tc, kc=kc, pb=pb: e.matmul(
                                    pb[:, :],
                                    wqb[:, (hc % 2) * 3072 + kc * 128:
                                        (hc % 2) * 3072 + (kc + 1) * 128],
                                    ao[:, kc * T + tc * TW: kc * T + (tc + 1) * TW],
                                    start=(kc == 0), stop=(kc == C - 1)),
                                    reads=[f"wqb.{hc % 2}.0", f"ao.{kc}.{tc}"],
                                    writes=[pbk])
                            s.op("dve", lambda e, hc=hc, tc=tc, pb=pb: e.tensor_add(
                                x_view(hc, tc), pb[:, :], x_view(hc, tc)),
                                reads=[pbk, XR[hc][tc]], writes=[XR[hc][tc]])
            s.barrier()

        def dump_sbuf(t, nbytes_elems, rkeys, name):
            s.dma("sp", lambda e: e.dma_start(
                out=dbg_d[:, 0:nbytes_elems], in_=t),
                reads=rkeys, tag=f"dump{name}")
            s.barrier()

        calls = {"f1": lambda: ffn_call("f1"), "at": lambda: attn_call("at"),
                 "f2": lambda: ffn_call("f2")}
        for ph in phases:
            calls[ph]()

        s.dma("sp", lambda e: e.dma_start(
            out=out_d.rearrange("(c p) t -> p c t", p=128),
            in_=x8[:, :].rearrange("p (c t) -> p c t", c=C)),
            reads=allx, tag="store")
        s.barrier()

        with contextlib.ExitStack() as ms:
            eng_sems = {n: ms.enter_context(nc.semaphore(f"sem_{n}"))
                        for n in ("sp", "pe", "act", "dve", "pool")}
            dma_sems = {}
            for q in ("sp", "act"):
                for sl in range(N_DMA_SLOTS):
                    dma_sems[(q, sl)] = ms.enter_context(nc.semaphore(f"dsem_{q}{sl}"))
            with nc.Block() as block:
                s.materialize(nc, block, eng_sems, dma_sems)

    return nc


_NC_CACHE = {}
_BUILD_PHASES = [("f1", "at", "f2")]
_BUILD_DUMP = [None]
_LAST_RESULTS = [None]


def kernel(L0_attn_x, L0_attn_w_qkv, L0_attn_w_out, L0_ffn_gate_w, L0_ffn_gate_bias,
           L0_ffn_w_gate, L0_ffn_w_up, L0_ffn_w_down, L0_ffn_sh_gate, L0_ffn_sh_up,
           L0_ffn_sh_down):
    x = np.asarray(L0_attn_x, dtype=np.float32)
    w_qkv = np.asarray(L0_attn_w_qkv, dtype=np.float32)
    w_out = np.asarray(L0_attn_w_out, dtype=np.float32)
    gate_w = np.asarray(L0_ffn_gate_w, dtype=np.float32)
    gate_bias = np.asarray(L0_ffn_gate_bias, dtype=np.float32)
    w_gate = np.asarray(L0_ffn_w_gate, dtype=np.float32)
    w_up = np.asarray(L0_ffn_w_up, dtype=np.float32)
    w_down = np.asarray(L0_ffn_w_down, dtype=np.float32)
    sh_gate = np.asarray(L0_ffn_sh_gate, dtype=np.float32)
    sh_up = np.asarray(L0_ffn_sh_up, dtype=np.float32)
    sh_down = np.asarray(L0_ffn_sh_down, dtype=np.float32)

    qkvw = np.ascontiguousarray(
        w_qkv.T.reshape(D, 24, 128).transpose(1, 0, 2))
    outw = np.ascontiguousarray(
        w_out.T.reshape(D, 8, 128).transpose(1, 0, 2))
    gT = np.ascontiguousarray(sh_gate.T)
    uT = np.ascontiguousarray(sh_up.T)
    gatew = np.ascontiguousarray(np.concatenate(
        [w_gate.transpose(0, 2, 1), gT[None, :, 0:512], gT[None, :, 512:1024]], axis=0))
    upw = np.ascontiguousarray(np.concatenate(
        [w_up.transpose(0, 2, 1), uT[None, :, 0:512], uT[None, :, 512:1024]], axis=0))
    dT = sh_down.T
    downw = np.ascontiguousarray(np.concatenate(
        [w_down.transpose(0, 2, 1), dT[None, 0:512, :], dT[None, 512:1024, :]], axis=0))
    gw = np.ascontiguousarray(gate_w.T)
    gb = np.ascontiguousarray(gate_bias.reshape(1, 32))

    ph = (tuple(_BUILD_PHASES[0]), _BUILD_DUMP[0])
    if ph not in _NC_CACHE:
        _NC_CACHE[ph] = build_program(phases=ph[0])
    nc = _NC_CACHE[ph]

    in_maps = []
    for b in range(8):
        in_maps.append({
            "x": np.ascontiguousarray(x[b].T),
            "qkvw": qkvw, "outw": outw, "gatew": gatew, "upw": upw,
            "downw": downw, "gw": gw, "gbias": gb,
        })
    res = run_bass_kernel_spmd(nc, in_maps, core_ids=list(range(8)))
    _LAST_RESULTS[0] = res.results
    out = np.stack([np.asarray(r["out"]).T for r in res.results], axis=0)
    return np.ascontiguousarray(out.astype(np.float32))
